# version 14
# baseline (speedup 1.0000x reference)
import numpy as np
import ml_dtypes
from contextlib import ExitStack

import concourse.bass as bass
import concourse.mybir as mybir
from concourse.bass_utils import run_bass_kernel_spmd

F32 = mybir.dt.float32
BF16 = mybir.dt.bfloat16
AF = mybir.ActivationFunctionType
OP = mybir.AluOpType

T = 16384
D = 1024
FW = 4096
NCORE = 8
TC = T // NCORE
SEG = 512
NSEG = TC // SEG
NB = D // 128
NWB = FW // 128
LOG_INV_SQRT_D = float(np.log(1.0 / np.sqrt(D)))

ENGINES = ("sync", "tensor", "vector", "scalar", "gpsimd")


class Plan:

    def __init__(self):
        self.seq = {e: [] for e in ENGINES}
        self.cnt = {}
        self.last_wait = {e: {} for e in ENGINES}

    def emit(self, eng, fn, waits=(), inc=None, amt=1):
        w = []
        lw = self.last_wait[eng]
        for item in waits:
            if item is None:
                continue
            sem, val = item
            if val is None:
                continue
            if val > lw.get(sem, 0):
                lw[sem] = val
                w.append((sem, val))
        t = None
        if inc is not None:
            self.cnt[inc] = self.cnt.get(inc, 0) + amt
            t = self.cnt[inc]
        self.seq[eng].append((fn, tuple(w), inc, amt))
        return t


class Rot:

    def __init__(self, slots):
        self.slots = list(slots)
        self.n = 0
        self.last = {}

    def next(self):
        i = self.n % len(self.slots)
        self.n += 1
        return i, self.slots[i], self.last.get(i)

    def done(self, i, sem, ticket):
        self.last[i] = (sem, ticket)


def build_program(debug=(), stage=3):
    nc = bass.Bass("TRN2", target_bir_lowering=False, debug=False)

    def ein(name, shape, dt=F32):
        return nc.dram_tensor(name, shape, dt, kind="ExternalInput").ap()

    x_in = ein("x", [TC, D])
    wq_in = ein("wq", [D, D])
    wk_in = ein("wk", [D, D])
    wv_in = ein("wv", [D, D])
    wo_in = ein("wo", [D, D])
    w1_in = ein("w1", [D, FW])
    b1_in = ein("b1col", [128, NWB])
    w2_in = ein("w2", [FW, D])
    b2_in = ein("b2row", [1, D])
    c_in = ein("c_cols", [128, NB])
    wmod_in = ein("wmod", [D, 6 * D])
    dlog_in = ein("decay_logit", [1, 1])
    osc_in = ein("out_scale", [1, 1])
    id32_in = ein("ident_f32", [128, 128])
    idbf_in = ein("ident_bf16", [128, 128], BF16)
    onesrow128_in = ein("ones_row128", [1, 128])
    onesrow512bf_in = ein("ones_row512_bf", [1, 512], BF16)
    onescol_bf_in = ein("ones_col_bf", [128, 1], BF16)
    iotacol_in = ein("iota_col", [128, 1])
    iotarow128_in = ein("iota_row128", [1, 128])
    iotarow512_in = ein("iota_row512", [1, 512])
    ecore_in = ein("ecore", [1, NCORE])
    emask_in = ein("emask", [1, NCORE])
    expo_in = ein("expo_t", [128, 4 * SEG])

    out_dram = nc.dram_tensor("out", [TC, D], F32, kind="ExternalOutput").ap()

    def scr(name, shape, dt):
        return nc.dram_tensor(name, shape, dt, kind="Internal").ap()

    q_sp = scr("q_sp", [NSEG, 128, NB * SEG], BF16)
    k_sp = scr("k_sp", [NSEG, 128, NB * SEG], BF16)
    v_sp = scr("v_sp", [NSEG, 128, NSEG * D], BF16)
    g_sp = scr("g_sp", [NSEG, 128, NB * D], BF16)
    x2_sp = scr("x2_sp", [NSEG, 128, NB * SEG], F32)
    retr_sp = scr("retr_sp", [NSEG, 128, NB * SEG], BF16)
    qt_sp = scr("qt_sp", [NSEG, 128, NB * SEG], BF16)
    xf_sp = scr("xf_sp", [NSEG, 128, NB * SEG], F32)
    w1bf = scr("w1bf", [D, FW], BF16)
    w2bf = scr("w2bf", [FW, D], BF16)
    h2_sp = scr("h2_sp", [NSEG, 128, NB * SEG], BF16)
    ag_in = scr("ag_in", [NB, 128, D], BF16)
    ag_out = nc.dram_tensor("ag_out", [NCORE * NB, 128, D], BF16,
                            kind="Internal", addr_space="Shared").ap()

    dbg = {}
    for name, shape, dt in debug:
        dbg[name] = nc.dram_tensor("dbg_" + name, shape, dt,
                                   kind="ExternalOutput").ap()

    P = Plan()
    st = {}

    with ExitStack() as ctx:
        SB_BASE = 18432
        SB_LIMIT = 16384 + 212863
        offmap = {}
        cur = [SB_BASE]

        def sb(name, shape, dt, at=None):
            size = shape[1] * mybir.dt.size(dt)
            if at is None:
                at = cur[0]
                cur[0] = (at + size + 63) & ~63
            offmap[name] = at
            return nc.alloc_sbuf_tensor_at(name, shape, dt, offset=at)

        ident32 = sb("ident32", [128, 128], F32)
        identbf = sb("identbf", [128, 128], BF16)
        onesrow128 = sb("onesrow128", [1, 128], F32)
        onesrow512bf = sb("onesrow512bf", [1, 512], BF16)
        onescolbf = sb("onescolbf", [128, 1], BF16)
        iotacol = sb("iotacol", [128, 1], F32)
        iotarow128 = sb("iotarow128", [1, 128], F32)
        iotarow512 = sb("iotarow512", [1, 512], F32)
        b2row = sb("b2row", [1, D], BF16)
        b1col = sb("b1col", [128, NWB], F32)
        c_cols = sb("c_cols", [128, NB], F32)
        mod_sb = sb("mod_sb", [128, 48], F32)
        derv = sb("derv", [128, 32], F32)
        dcol = sb("dcol", [128, 32], F32)
        rows2 = sb("rows2", [1, 64], F32)
        PERM_END = cur[0]
        rows = sb("rows", [1, 512 * 5], F32)
        dmat = sb("dmat", [128, 4 * SEG], BF16)
        bqtil = sb("bqtil", [128, SEG], F32)
        tstage = sb("tstage", [128, 512], F32)
        tstage2 = sb("tstage2", [128, 512], F32)
        expo_sb = sb("expo_sb", [128, 4 * SEG], F32)

        wqT = sb("wqT", [128, NB * D], BF16)
        wkT = sb("wkT", [128, NB * D], BF16)
        wvT = sb("wvT", [128, NB * D], BF16)
        xstage = sb("xstage", [128, 2 * D], F32)
        xf = sb("xf", [128, NB * SEG], F32)
        hseg = sb("hseg", [128, NB * SEG], BF16)
        sblk = sb("sblk", [128, 4 * 512], BF16)
        qseg = sb("qseg", [128, NB * SEG], BF16)
        kseg = sb("kseg", [128, NB * SEG], BF16)
        vseg = sb("vseg", [128, NSEG * D], BF16)
        ktil = sb("ktil", [128, 4 * D], BF16)
        khat = sb("khat", [128, 4 * D], BF16)
        ghat = sb("ghat", [128, NB * D], BF16)
        gstage = sb("gstage", [128, 2 * D], BF16)
        retrev = sb("retrev", [128, NB * SEG], BF16)
        xf2 = sb("xf2", [128, NB * SEG], F32)
        A_END = cur[0]
        assert A_END <= SB_LIMIT, A_END

        woT = sb("woT", [128, NB * D], BF16, at=offmap["wqT"])
        nbuf = sb("nbuf", [128, NB * D], BF16, at=offmap["wkT"])
        sc_sb = sb("sc_sb", [128, NB * D], BF16, at=offmap["wvT"])
        mbuf = sb("mbuf", [128, NB * D], BF16, at=offmap["ghat"])
        scT = sb("scT", [128, 4 * SEG], BF16, at=offmap["ktil"])
        qtil = sb("qtil", [128, NB * SEG], BF16, at=offmap["khat"])
        agstage = sb("agstage", [128, 2 * D], BF16, at=offmap["gstage"])
        rstr = sb("rstr", [128, NB * SEG], BF16, at=offmap["ktil"])
        gstg2 = sb("gstg2", [128, 4 * D], BF16, at=offmap["xstage"])
        rstr2 = sb("rstr2", [128, NB * SEG], BF16, at=offmap["kseg"])
        oebuf = sb("oebuf", [128, NB * SEG], BF16, at=offmap["xf"])
        qtil2 = sb("qtil2", [128, NB * SEG], BF16, at=offmap["vseg"])
        agstage2 = sb("agstage2", [128, NB * D], BF16, at=offmap["ghat"])

        fcur = [offmap["rows"]]

        def sbf(name, shape, dt):
            size = shape[1] * mybir.dt.size(dt)
            at = fcur[0]
            fcur[0] = (at + size + 63) & ~63
            offmap[name] = at
            return nc.alloc_sbuf_tensor_at(name, shape, dt, offset=at)

        w1sb = sbf("w1sb", [128, NB * FW], BF16)
        w2sb = sbf("w2sb", [128, NWB * D], BF16)
        midc = sbf("midc", [128, NWB * SEG], BF16)
        h2c = sbf("h2c", [128, NB * SEG], BF16)
        x2c = sbf("x2c", [128, NB * SEG], F32)
        ostage = sbf("ostage", [128, 2 * D], F32)
        assert fcur[0] <= SB_LIMIT, fcur[0]

        ps_a = ctx.enter_context(nc.psum_tensor("ps_a", [128, 512], F32))
        ps_b = ctx.enter_context(nc.psum_tensor("ps_b", [128, 512], F32))
        pg_a = ctx.enter_context(nc.psum_tensor("pg_a", [128, 1024], F32))
        pg_b = ctx.enter_context(nc.psum_tensor("pg_b", [128, 1024], F32))
        pk_a = ctx.enter_context(nc.psum_tensor("pk_a", [128, 128], BF16))
        pk_b = ctx.enter_context(nc.psum_tensor("pk_b", [128, 128], BF16))

        sem_names = ("s_c", "s_xa", "s_xb", "s_ql", "s_kl", "s_vl",
                     "s_nb", "s_ga", "s_gb", "s_aga", "s_agb",
                     "s_hl", "s_x2l", "s_oa", "s_ob",
                     "s_qsp", "s_ksp", "s_vsp", "s_gsp0", "s_gsp1",
                     "s_x2sp", "s_h2sp", "s_agin", "s_w1", "s_w2", "s_dbg",
                     "s_rsp", "s_rl", "s_x2b2", "s_qtsp", "s_w1b", "s_w1c", "s_w1d", "s_gc", "s_gd", "s_age", "s_agf", "s_ql2", "s_rl2", "s_wp1", "s_wp2", "s_w2b", "s_xfa", "s_xfb", "s_xla", "s_xlb",
                     "s_pe", "s_dve", "s_act", "s_cc")
        sems = {n: ctx.enter_context(nc.semaphore(n)) for n in sem_names}

        rot_ps = Rot([ps_a[:, 0:512], ps_b[:, 0:512],
                      pg_a[:, 0:512], pg_a[:, 512:1024],
                      pg_b[:, 0:512], pg_b[:, 512:1024]])
        rot_pt = rot_ps
        rot_pk = Rot([pk_a, pk_b])
        rot_xstage = Rot([0, 1])
        rot_nstage = Rot([0, 1, 2, 3])
        rot_gstage = Rot([0, 1])
        rot_ostage = Rot([0, 1])
        rot_sblk1 = Rot([0, 1])
        rot_sblk2 = Rot([2, 3])

        def dma(eng, out_ap, in_ap, waits=(), sem=None):
            assert sem is not None
            return P.emit(eng,
                          lambda e, S, o=out_ap, i=in_ap:
                              e.dma_start(out=o, in_=i),
                          waits, sem, 16)

        def full(sem):
            return (sem, P.cnt.get(sem, 0))

        def mm(out_ap, a_ap, b_ap, start, stop, waits=(), inc=None):
            return P.emit("tensor",
                          lambda e, S, o=out_ap, a=a_ap, b=b_ap, s1=start,
                          s2=stop: e.matmul(o, a, b, start=s1, stop=s2),
                          waits, inc)

        def tr(out_ap, in_ap, ident_ap, waits=(), inc="s_pe"):
            return P.emit("tensor",
                          lambda e, S, o=out_ap, i=in_ap, d=ident_ap:
                              e.transpose(o, i, d),
                          waits, inc)

        def dve(fn, waits=(), inc="s_dve"):
            return P.emit("vector", fn, waits, inc)

        def act(fn, waits=(), inc="s_act"):
            return P.emit("scalar", fn, waits, inc)

        def dump(name, src_ap, waits):
            if name in dbg:
                t = dma("sync", dbg[name], src_ap,
                        waits=list(waits) + [st.get("dump_prev")],
                        sem="s_dbg")
                st["dump_prev"] = ("s_dbg", t)
                return 1
            return 0

        n_dbg = 0

        t_cc0 = dma("sync", c_cols[:], c_in, sem="s_hl")
        for ap_sb, ap_in in ((ident32, id32_in), (identbf, idbf_in),
                             (onesrow128, onesrow128_in),
                             (onesrow512bf, onesrow512bf_in),
                             (onescolbf, onescol_bf_in),
                             (iotacol, iotacol_in),
                             (iotarow128, iotarow128_in),
                             (iotarow512, iotarow512_in),
                             (b1col, b1_in)):
            dma("sync", ap_sb[:], ap_in, sem="s_c")
        dma("sync", rows2[0:1, 0:NCORE], ecore_in, sem="s_c")
        dma("sync", rows2[0:1, 8:8 + NCORE], emask_in, sem="s_c")
        dma("sync", rows2[0:1, 16:17], dlog_in, sem="s_c")
        dma("sync", rows2[0:1, 17:18], osc_in, sem="s_c")
        dma("sync", expo_sb[:], expo_in, sem="s_c")
        t_b2 = dma("gpsimd", b2row[:], b2_in, sem="s_wp2")
        if stage >= 3:
            for ch in range(4):
                dma("gpsimd", w1bf[256 * ch:256 * (ch + 1), :],
                    w1_in[256 * ch:256 * (ch + 1), :], sem="s_wp1")
                dma("gpsimd", w2bf[1024 * ch:1024 * (ch + 1), :],
                    w2_in[1024 * ch:1024 * (ch + 1), :], sem="s_wp2")
        W_WPREP1 = full("s_wp1")
        W_WPREP2 = full("s_wp2")
        W_CONST = full("s_c")

        LND, TMP, APW, OSCC, BIA, ONE, KH0 = 0, 1, 2, 5, 6, 7, 8
        t_bia = dve(lambda e, S: e.memset(dcol[:, BIA:BIA + 1],
                                          LOG_INV_SQRT_D))
        t_one = dve(lambda e, S: e.memset(dcol[:, ONE:ONE + 1], 1.0))
        alpha_bf = sb("alpha_bf", [128, 32], BF16)
        gaefa = sb("gaefa", [128, 24], F32)

        i0, pt, g = rot_pt.next()
        t = mm(pt[:, 0:1], onesrow128[:], rows2[0:1, 16:17], True, False,
               waits=[W_CONST, g])
        t = mm(pt[:, 1:2], onesrow128[:], rows2[0:1, 17:18], False, True,
               waits=[], inc="s_pe")
        t_sg = act(lambda e, S, pt=pt: e.activation(
            dcol[:, TMP:TMP + 1], pt[:, 0:1], AF.Sigmoid),
            waits=[("s_pe", t)])
        t_oc = dve(lambda e, S, pt=pt: e.tensor_copy(
            dcol[:, OSCC:OSCC + 1], pt[:, 1:2]), waits=[("s_pe", t)])
        rot_pt.done(i0, "s_dve", t_oc)
        t_lnd = act(lambda e, S: e.activation(
            dcol[:, LND:LND + 1], dcol[:, TMP:TMP + 1], AF.Ln),
            waits=[("s_act", t_sg)])
        t_apw = None
        for p in (1, 2, 3):
            t_apw = act(lambda e, S, p=p: e.activation(
                dcol[:, APW + p - 1:APW + p], dcol[:, LND:LND + 1], AF.Exp,
                scale=float(SEG * p)), waits=[("s_act", t_lnd)])
        t_apwbf = dve(lambda e, S: e.tensor_copy(
            alpha_bf[:, APW:APW + 1], dcol[:, APW:APW + 1]),
            waits=[("s_act", t_apw)], inc="s_dve")
        W_APW = ("s_dve", t_apwbf)
        t_kh = None
        for s in range(NSEG):
            for b in range(4):
                tt = dve(lambda e, S, s=s, b=b: e.tensor_scalar(
                    dcol[:, TMP:TMP + 1], iotacol[:], 1.0,
                    float(SEG * s + 128 * b), OP.mult, OP.add),
                    waits=[W_CONST, ("s_act", t_kh if t_kh else t_lnd)])
                t_kh = act(lambda e, S, s=s, b=b: e.activation(
                    dcol[:, KH0 + 4 * s + b:KH0 + 4 * s + b + 1],
                    dcol[:, TMP:TMP + 1], AF.Exp,
                    scale=dcol[:, LND:LND + 1]), waits=[("s_dve", tt)])

        tt = dve(lambda e, S: e.tensor_scalar(
            rows[0:1, 0:512], iotarow512[:], -1.0, float(SEG - 1),
            OP.mult, OP.add), waits=[W_CONST])
        tt = act(lambda e, S: e.activation(
            rows[0:1, 512:1024], rows[0:1, 0:512], AF.Exp,
            scale=dcol[0:1, LND:LND + 1], bias=dcol[0:1, BIA:BIA + 1]),
            waits=[("s_dve", tt), ("s_act", t_lnd), ("s_dve", t_bia)])
        i0, ps, g = rot_ps.next()
        t = mm(ps[:], onesrow128[:], rows[0:1, 512:1024], True, True,
               waits=[("s_act", tt), g], inc="s_pe")
        t_bqtil = dve(lambda e, S, ps=ps: e.tensor_copy(bqtil[:], ps[:]),
                      waits=[("s_pe", t)])
        rot_ps.done(i0, "s_dve", t_bqtil)

        tt = dve(lambda e, S: e.tensor_scalar(
            rows2[0:1, 24:32], rows2[0:1, 0:8], dcol[0:1, LND:LND + 1],
            None, OP.mult), waits=[W_CONST, ("s_act", t_lnd)])
        tt = act(lambda e, S: e.activation(
            rows2[0:1, 32:40], rows2[0:1, 24:32], AF.Exp),
            waits=[("s_dve", tt)])
        tt = dve(lambda e, S: e.tensor_mul(
            rows2[0:1, 40:48], rows2[0:1, 32:40], rows2[0:1, 8:16]),
            waits=[("s_act", tt)])
        i0, ps, g = rot_ps.next()
        t = mm(ps[:, 0:8], onesrow128[:], rows2[0:1, 40:48], True, True,
               waits=[("s_dve", tt), g], inc="s_pe")
        t_wcomb = dve(lambda e, S, ps=ps: e.tensor_copy(
            derv[:, 24:32], ps[:, 0:8]), waits=[("s_pe", t)])
        rot_ps.done(i0, "s_dve", t_wcomb)

        t_dmat = None
        for b in range(4):
            eb = expo_sb[:, 512 * b:512 * (b + 1)]
            tm = dve(lambda e, S, eb=eb: e.tensor_scalar(
                tstage[:], eb, 0.0, None, OP.is_ge),
                waits=[W_CONST, ("s_dve", t_dmat)])
            tm2 = dve(lambda e, S, eb=eb: e.tensor_scalar(
                tstage2[:], eb, 0.0, None, OP.max),
                waits=[("s_dve", tm)])
            te = act(lambda e, S: e.activation(
                tstage2[:], tstage2[:], AF.Exp,
                scale=dcol[:, LND:LND + 1], bias=dcol[:, BIA:BIA + 1]),
                waits=[("s_dve", tm2), ("s_act", t_lnd), ("s_dve", t_bia)])
            t_dmat = dve(lambda e, S, b=b: e.tensor_mul(
                dmat[:, 512 * b:512 * (b + 1)], tstage2[:], tstage[:]),
                waits=[("s_act", te)])

        t_cs = act(lambda e, S: e.activation(c_cols[:], c_cols[:], AF.Silu),
                   waits=[("s_hl", t_cc0)])
        imod, pmod, gmod = rot_ps.next()
        first_mod = True
        t_mod_mm = None
        for k in range(NB):
            for ch in range(6):
                ih, half, gx = rot_xstage.next()
                tld = dma("sync", xstage[:, 1024 * half:1024 * (half + 1)],
                          wmod_in[128 * k:128 * (k + 1),
                                  1024 * ch:1024 * (ch + 1)],
                          waits=[gx] + ([gmod] if first_mod else []),
                          sem="s_xa" if half == 0 else "s_xb")
                for j2 in range(8):
                    col = 8 * ch + j2
                    last = (k == NB - 1 and col == 47)
                    t_mod_mm = mm(pmod[:, col:col + 1],
                                  xstage[:, 1024 * half + 128 * j2:
                                         1024 * half + 128 * (j2 + 1)],
                                  c_cols[:, k:k + 1],
                                  first_mod, last,
                                  waits=[("s_xa" if half == 0 else "s_xb",
                                          tld), ("s_act", t_cs)],
                                  inc="s_pe" if j2 == 7 else None)
                    first_mod = False
                rot_xstage.done(ih, "s_pe", t_mod_mm)
        t_modev = dve(lambda e, S, pmod=pmod: e.tensor_copy(
            mod_sb[:], pmod[:, 0:48]), waits=[("s_pe", t_mod_mm)])
        rot_ps.done(imod, "s_dve", t_modev)

        tv = dve(lambda e, S: e.tensor_copy(derv[:, 0:8], mod_sb[:, 0:8]),
                 waits=[("s_dve", t_modev)])
        tv = dve(lambda e, S: e.tensor_scalar(
            derv[:, 8:16], mod_sb[:, 8:16], 1.0, 1.0, OP.mult, OP.add),
            waits=[("s_dve", tv)])

        def plan_clamp(src_sl, dst_sl, tprev):
            tv = dve(lambda e, S: e.tensor_scalar(
                tstage[:, 0:8], src_sl, 0.0, -1.0, OP.is_ge, OP.add),
                waits=[("s_dve", tprev)])
            tv = dve(lambda e, S: e.tensor_scalar(
                tstage[:, 0:8], tstage[:, 0:8], 2.0, 1.0, OP.mult, OP.add),
                waits=[("s_dve", tv)])
            tv = dve(lambda e, S: e.tensor_mul(
                dst_sl, src_sl, tstage[:, 0:8]), waits=[("s_dve", tv)])
            tv = dve(lambda e, S: e.tensor_scalar(
                dst_sl, dst_sl, 0.25, None, OP.max), waits=[("s_dve", tv)])
            tv = dve(lambda e, S: e.tensor_mul(
                dst_sl, dst_sl, tstage[:, 0:8]), waits=[("s_dve", tv)])
            return tv

        tv = plan_clamp(mod_sb[:, 16:24], derv[:, 16:24], tv)
        tv = dve(lambda e, S: e.tensor_scalar(
            derv[:, 16:24], derv[:, 16:24], dcol[:, OSCC:OSCC + 1], None,
            OP.mult), waits=[("s_dve", tv), ("s_dve", t_oc)])
        tv = plan_clamp(mod_sb[:, 40:48], mod_sb[:, 40:48], tv)
        t_mod = dve(lambda e, S: e.tensor_scalar(
            mod_sb[:, 32:40], mod_sb[:, 32:40], 1.0, 1.0, OP.mult, OP.add),
            waits=[("s_dve", tv)])
        sa_c, ca1_c, gaeff_c = derv[:, 0:8], derv[:, 8:16], derv[:, 16:24]
        t_gaefa = None
        for k in range(3):
            t_gaefa = dve(lambda e, S, k=k: e.tensor_scalar(
                gaefa[:, 8 * k:8 * (k + 1)], derv[:, 16:24],
                dcol[:, APW + k:APW + k + 1], None, OP.mult),
                waits=[("s_dve", t_mod), ("s_act", t_apw)])
        wcomb_c = derv[:, 24:32]
        sb_c, cb1_c, gb_c = mod_sb[:, 24:32], mod_sb[:, 32:48 - 8], mod_sb[:, 40:48]
        n_dbg += dump("mod", mod_sb[:], [("s_dve", t_mod)])
        n_dbg += dump("derv", derv[:], [("s_dve", t_mod)])
        n_dbg += dump("dcol", dcol[:], [("s_act", t_kh), ("s_act", t_apw)])
        n_dbg += dump("dmat", dmat[:], [("s_dve", t_dmat)])

        def plan_wT(w_in_, wT_):
            tlast = None
            for ib in range(NB):
                ih, half, gx = rot_xstage.next()
                sx = "s_xa" if half == 0 else "s_xb"
                tld = dma("sync", xstage[:, 1024 * half:1024 * (half + 1)],
                          w_in_[128 * ib:128 * (ib + 1), :], waits=[gx],
                          sem=sx)
                tre = None
                for db in range(NB):
                    ipt, pt, gp = rot_pt.next()
                    t = tr(pt[:, 0:128],
                           xstage[:, 1024 * half + 128 * db:
                                  1024 * half + 128 * (db + 1)],
                           ident32[:], waits=[(sx, tld), gp])
                    te = act(lambda e, S, pt=pt, db=db, ib=ib, wT_=wT_:
                             e.activation(
                                 wT_[:, 1024 * db + 128 * ib:
                                     1024 * db + 128 * (ib + 1)],
                                 pt[:, 0:128], AF.Copy), waits=[("s_pe", t)])
                    rot_pt.done(ipt, "s_act", te)
                    tre = t
                    tlast = te
                rot_xstage.done(ih, "s_pe", tre)
            return tlast

        t_wq = plan_wT(wq_in, wqT)
        t_wk = plan_wT(wk_in, wkT)
        t_wv = plan_wT(wv_in, wvT)

        def plan_ln_stats(xf_t, tag, ready):
            isum, psum_s, gsum = rot_ps.next()
            isq, psum_q, gsq = rot_ps.next()
            t_summ = t_sqm = None
            for k in range(NB):
                i1, sl1, g1 = rot_sblk1.next()
                tc1 = dve(lambda e, S, k=k, sl1=sl1: e.tensor_copy(
                    sblk[:, 512 * sl1:512 * (sl1 + 1)],
                    xf_t[:, 512 * k:512 * (k + 1)]),
                    waits=[g1] + ready)
                t_summ = mm(psum_s[0:1, :], onescolbf[:],
                            sblk[:, 512 * sl1:512 * (sl1 + 1)],
                            k == 0, k == NB - 1,
                            waits=[("s_dve", tc1)] + ([gsum] if k == 0 else []),
                            inc="s_pe")
                rot_sblk1.done(i1, "s_pe", t_summ)
                i2, sl2, g2 = rot_sblk2.next()
                tc2 = act(lambda e, S, k=k, sl2=sl2: e.activation(
                    sblk[:, 512 * sl2:512 * (sl2 + 1)],
                    xf_t[:, 512 * k:512 * (k + 1)], AF.Square),
                    waits=[g2] + ready)
                t_sqm = mm(psum_q[0:1, :], onescolbf[:],
                           sblk[:, 512 * sl2:512 * (sl2 + 1)],
                           k == 0, k == NB - 1,
                           waits=[("s_act", tc2)] + ([gsq] if k == 0 else []),
                           inc="s_pe")
                rot_sblk2.done(i2, "s_pe", t_sqm)
                st[(tag, "c2", k)] = tc2

            R = lambda i: rows[0:1, 512 * i:512 * (i + 1)]
            t1 = dve(lambda e, S: e.tensor_scalar(
                R(0), psum_s[0:1, :], 1.0 / D, None, OP.mult),
                waits=[("s_pe", t_summ)])
            rot_ps.done(isum, "s_dve", t1)
            t2 = dve(lambda e, S: e.tensor_scalar(
                R(1), psum_q[0:1, :], 1.0 / D, None, OP.mult),
                waits=[("s_pe", t_sqm)])
            rot_ps.done(isq, "s_dve", t2)
            t3 = dve(lambda e, S: e.tensor_mul(R(2), R(0), R(0)),
                     waits=[("s_dve", t2)])
            t3 = dve(lambda e, S: e.tensor_sub(R(2), R(1), R(2)),
                     waits=[("s_dve", t3)])
            t3 = dve(lambda e, S: e.tensor_scalar(
                R(2), R(2), 1e-5, None, OP.add), waits=[("s_dve", t3)])
            t3 = dve(lambda e, S: e.reciprocal(R(3), R(2)),
                     waits=[("s_dve", t3)])
            t4 = act(lambda e, S: e.activation(R(3), R(3), AF.Sqrt),
                     waits=[("s_dve", t3)])
            t5 = dve(lambda e, S: e.tensor_mul(R(4), R(0), R(3)),
                     waits=[("s_act", t4)])
            return {"xf_t": xf_t, "tag": tag, "t4": t4, "t5": t5, "t1": t1,
                    "R": R, "xf_act": st[(tag, "c2", NB - 1)]}

        def plan_ln_apply(ctx, h_t, ca1col, sacol, h_guard):
            xf_t, tag, R = ctx["xf_t"], ctx["tag"], ctx["R"]
            ibr, ps_br, gbr = rot_ps.next()
            tba = mm(ps_br[:], onesrow128[:], R(3), True, True,
                     waits=[("s_act", ctx["t4"]), ("s_dve", ctx["t1"]), gbr],
                     inc="s_pe")
            ibm, ps_bm, gbm = rot_ps.next()
            tbb = mm(ps_bm[:], onesrow128[:], R(4), True, True,
                     waits=[("s_dve", ctx["t5"]), gbm], inc="s_pe")
            th = None
            tu_last = None
            for k in range(NB):
                tu = dve(lambda e, S, k=k, ps_br=ps_br: e.tensor_mul(
                    tstage[:], xf_t[:, 512 * k:512 * (k + 1)], ps_br[:]),
                    waits=[("s_pe", tba)])
                tu_last = tu
                tn = dve(lambda e, S, ps_bm=ps_bm: e.tensor_sub(
                    tstage2[:], tstage[:], ps_bm[:]),
                    waits=[("s_pe", tbb), ("s_dve", tu)])
                th = dve(lambda e, S, k=k: e.tensor_scalar(
                    h_t[:, 512 * k:512 * (k + 1)], tstage2[:],
                    ca1col[:, k:k + 1], sacol[:, k:k + 1], OP.mult, OP.add),
                    waits=[("s_dve", tn), ("s_dve", t_mod)] +
                          ([h_guard] if k == 0 else []))
                st[(tag, "h", k)] = th
            rot_ps.done(ibr, "s_dve", th)
            rot_ps.done(ibm, "s_dve", th)
            return {"h_last": th, "xf_dve": tu_last,
                    "xf_act": ctx["xf_act"]}

        def plan_ln(xf_t, h_t, ca1col, sacol, ready, tag, h_guard):
            ctx = plan_ln_stats(xf_t, tag, ready)
            return plan_ln_apply(ctx, h_t, ca1col, sacol, h_guard)

        def p1_x_ln(s):
            xf_s = xf if s % 2 == 0 else xf2
            prev2 = st.get(("p1ln", s - 2))
            t_xf = None
            for b in range(4):
                ih, half, gx = rot_xstage.next()
                xw = [gx]
                if s == 0 and b == 0:
                    xw.append(("s_act", t_wv))
                if prev2 is not None and b == 0:
                    xw += [("s_dve", prev2["xf_dve"]),
                           ("s_act", prev2["xf_act"]),
                           ("s_xfa" if s % 2 == 0 else "s_xfb",
                            st.get(("xfsp", s - 2)))]
                sx = "s_xa" if half == 0 else "s_xb"
                tld = dma("sync", xstage[:, 1024 * half:1024 * (half + 1)],
                          x_in[512 * s + 128 * b:512 * s + 128 * (b + 1), :],
                          waits=xw, sem=sx)
                tre = None
                for db in range(NB):
                    ipt, pt, gp = rot_pt.next()
                    t = tr(pt[:, 0:128],
                           xstage[:, 1024 * half + 128 * db:
                                  1024 * half + 128 * (db + 1)],
                           ident32[:], waits=[(sx, tld), gp])
                    te = act(lambda e, S, pt=pt, db=db, b=b, xf_s=xf_s:
                             e.activation(
                        xf_s[:, 512 * db + 128 * b:512 * db + 128 * (b + 1)],
                        pt[:, 0:128], AF.Copy), waits=[("s_pe", t)])
                    rot_pt.done(ipt, "s_act", te)
                    tre = t
                    t_xf = te
                rot_xstage.done(ih, "s_pe", tre)
            ctx = plan_ln_stats(xf_s, ("l1", s), [("s_act", t_xf)])
            st[("p1ctx", s)] = ctx
            st[("xfsp", s)] = dma("gpsimd", xf_sp[s], xf_s[:],
                                  waits=[("s_act", t_xf)],
                                  sem="s_xfa" if s % 2 == 0 else "s_xfb")

        def p1_apply(s):
            lnres = plan_ln_apply(st[("p1ctx", s)], hseg, ca1_c, sa_c,
                                  h_guard=("s_pe",
                                           st.get(("vmm_last", s - 1))))
            st[("p1ln", s)] = lnres

        if stage >= 1:
            p1_x_ln(0)
            p1_apply(0)
        for s in range(NSEG if stage >= 1 else 0):
            lnres = st[("p1ln", s)]

            for tag, wT_, dest, spill in (("q", wqT, qseg, q_sp),
                                          ("k", wkT, kseg, k_sp)):
                tev = None
                spsem = "s_qsp" if tag == "q" else "s_ksp"
                for ib in range(NB):
                    ips, ps, gp = rot_ps.next()
                    w0 = [gp, (spsem, st.get((tag + "sp", s - 1)))]
                    t = None
                    for db in range(NB):
                        t = mm(ps[:],
                               wT_[:, 1024 * db + 128 * ib:
                                   1024 * db + 128 * (ib + 1)],
                               hseg[:, 512 * db:512 * (db + 1)],
                               db == 0, db == NB - 1,
                               waits=[("s_dve", st[(("l1", s), "h", db)])] +
                                     (w0 if db == 0 else []),
                               inc="s_pe" if db == NB - 1 else None)
                    tev = dve(lambda e, S, ps=ps, ib=ib, dest=dest:
                              e.tensor_copy(
                                  dest[:, 512 * ib:512 * (ib + 1)], ps[:]),
                              waits=[("s_pe", t)])
                    rot_ps.done(ips, "s_dve", tev)
                    st[(tag + "ev", s, ib)] = tev
                st[(tag + "sp", s)] = dma("gpsimd", spill[s], dest[:],
                                          waits=[("s_dve", tev)], sem=spsem)
            tvl = None
            for b in range(4):
                for eh in range(2):
                    ips, ps, gp = rot_ps.next()
                    t = None
                    for db in range(NB):
                        t = mm(ps[:],
                               hseg[:, 512 * db + 128 * b:
                                    512 * db + 128 * (b + 1)],
                               wvT[:, 1024 * db + 512 * eh:
                                   1024 * db + 512 * (eh + 1)],
                               db == 0, db == NB - 1,
                               waits=[("s_dve", st[(("l1", s), "h", NB - 1)])]
                                     + ([gp, ("s_vsp", st.get(("vsp", s - 1)))]
                                        if db == 0 else []),
                               inc="s_pe" if db == NB - 1 else None)
                    tev = dve(lambda e, S, ps=ps, b=b, eh=eh: e.tensor_copy(
                        vseg[:, 1024 * b + 512 * eh:1024 * b + 512 * (eh + 1)],
                        ps[:]), waits=[("s_pe", t)])
                    rot_ps.done(ips, "s_dve", tev)
                    tvl = tev
            st[("vmm_last", s)] = t
            st[("vev_last", s)] = tvl
            st[("vsp", s)] = dma("gpsimd", v_sp[s], vseg[:],
                                 waits=[("s_dve", tvl)], sem="s_vsp")

            if s + 1 < NSEG:
                p1_x_ln(s + 1)

            tkh = None
            for b in range(4):
                for ib in range(NB):
                    ipk, pk, gp = rot_pk.next()
                    t = tr(pk[:],
                           kseg[:, 512 * ib + 128 * b:512 * ib + 128 * (b + 1)],
                           identbf[:],
                           waits=[("s_dve", st[("kev", s, ib)]), gp])
                    tt1 = dve(lambda e, S, pk=pk, b=b, ib=ib: e.tensor_scalar(
                        ktil[:, 1024 * b + 128 * ib:1024 * b + 128 * (ib + 1)],
                        pk[:], dcol[:, KH0 + b:KH0 + b + 1], None, OP.mult),
                        waits=[("s_pe", t), ("s_act", t_kh),
                               ("s_pe", st.get(("gmm_last", s - 1)))])
                    tt2 = dve(lambda e, S, pk=pk, b=b, ib=ib, s=s:
                              e.tensor_scalar(
                        khat[:, 1024 * b + 128 * ib:1024 * b + 128 * (ib + 1)],
                        pk[:], dcol[:, KH0 + 4 * s + b:KH0 + 4 * s + b + 1],
                        None, OP.mult),
                        waits=[("s_dve", tt1),
                               ("s_pe", st.get(("ghmm_last", s - 1)))])
                    rot_pk.done(ipk, "s_dve", tt2)
                    tkh = tt2
            st[("kh_done", s)] = tkh

            if s >= 1:
                tgm = None
                for ib in range(NB):
                    slots = [rot_ps.next(), rot_ps.next()]
                    tstop = []
                    for eh in range(2):
                        ipg, pg, gp = slots[eh]
                        t = None
                        for b in range(4):
                            t = mm(pg,
                                   ktil[:, 1024 * b + 128 * ib:
                                        1024 * b + 128 * (ib + 1)],
                                   vseg[:, 1024 * b + 512 * eh:
                                        1024 * b + 512 * (eh + 1)],
                                   b == 0, b == 3,
                                   waits=[("s_dve", tkh),
                                          ("s_dve", st[("vev_last", s)])] +
                                         ([gp] if b == 0 else []),
                                   inc="s_pe" if b == 3 else None)
                        tstop.append(t)
                    tgm = t
                    ig, gsl, gg = rot_gstage.next()
                    tev = None
                    for eh in range(2):
                        tev = dve(lambda e, S, pg=slots[eh][1], gsl=gsl, eh=eh:
                                  e.tensor_copy(
                            gstage[:, 1024 * gsl + 512 * eh:
                                   1024 * gsl + 512 * (eh + 1)], pg),
                            waits=[("s_pe", tstop[eh]), gg])
                        rot_ps.done(slots[eh][0], "s_dve", tev)
                    gsem = "s_gsp0" if gsl == 0 else "s_gsp1"
                    tsp = dma("gpsimd", g_sp[s][:, 1024 * ib:1024 * (ib + 1)],
                              gstage[:, 1024 * gsl:1024 * (gsl + 1)],
                              waits=[("s_dve", tev)], sem=gsem)
                    rot_gstage.done(ig, gsem, tsp)
                    st[("gsp", s, ib)] = tsp
                st[("gmm_last", s)] = tgm
            tghm = None
            for ib in range(NB):
                slots = [rot_ps.next(), rot_ps.next()]
                tstop = []
                for eh in range(2):
                    ipg, pg, gp = slots[eh]
                    t = None
                    for b in range(4):
                        t = mm(pg,
                               khat[:, 1024 * b + 128 * ib:
                                    1024 * b + 128 * (ib + 1)],
                               vseg[:, 1024 * b + 512 * eh:
                                    1024 * b + 512 * (eh + 1)],
                               b == 0, b == 3,
                               waits=[("s_dve", tkh),
                                      ("s_dve", st[("vev_last", s)])] +
                                     ([gp] if b == 0 else []),
                               inc="s_pe" if b == 3 else None)
                    tstop.append(t)
                tghm = t
                tg = None
                for eh in range(2):
                    sl = ghat[:, 1024 * ib + 512 * eh:
                              1024 * ib + 512 * (eh + 1)]
                    if s == 0:
                        tg = dve(lambda e, S, pg=slots[eh][1], sl=sl:
                                 e.tensor_copy(sl, pg),
                                 waits=[("s_pe", tstop[eh])])
                    else:
                        tg = dve(lambda e, S, pg=slots[eh][1], sl=sl:
                                 e.tensor_add(sl, sl, pg),
                                 waits=[("s_pe", tstop[eh]),
                                        ("s_dve", st.get(("ghacc", s - 1, ib)))])
                    rot_ps.done(slots[eh][0], "s_dve", tg)
                st[("ghacc", s, ib)] = tg
            st[("ghmm_last", s)] = tghm
            if s + 1 < NSEG:
                p1_apply(s + 1)

        t_p1_dve = (st[("ghacc", NSEG - 1, NB - 1)] if stage >= 1 else None)
        if stage >= 1:
            n_dbg += dump("ghat", ghat[:], [("s_dve", t_p1_dve)])

        t_agd = None
        for ib in range(NB if stage >= 2 else 0):
            t_agd = dma("gpsimd", ag_in[ib],
                        ghat[:, 1024 * ib:1024 * (ib + 1)],
                        waits=[("s_dve", st[("ghacc", NSEG - 1, ib)])],
                        sem="s_agin")
        t_cc = (P.emit("gpsimd",
                       lambda e, S: e.collective_compute(
                           "AllGather", OP.bypass,
                           replica_groups=[list(range(NCORE))],
                           ins=[ag_in], outs=[ag_out]),
                       waits=[full("s_agin")], inc="s_cc")
                if stage >= 2 else None)

        tlast = None
        for ib in range(NB if stage >= 2 else 0):
            ih, half, gx = rot_xstage.next()
            sx = "s_xa" if half == 0 else "s_xb"
            tld = dma("sync", xstage[:, 1024 * half:1024 * (half + 1)],
                      wo_in[128 * ib:128 * (ib + 1), :],
                      waits=[gx, ("s_dve", t_p1_dve)], sem=sx)
            tre = None
            for db in range(NB):
                ipt, pt, gp = rot_pt.next()
                t = tr(pt[:, 0:128],
                       xstage[:, 1024 * half + 128 * db:
                              1024 * half + 128 * (db + 1)],
                       ident32[:], waits=[(sx, tld), gp])
                te = act(lambda e, S, pt=pt, db=db, ib=ib: e.activation(
                    woT[:, 1024 * db + 128 * ib:1024 * db + 128 * (ib + 1)],
                    pt[:, 0:128], AF.Copy), waits=[("s_pe", t)])
                rot_pt.done(ipt, "s_act", te)
                tre = t
                tlast = te
            rot_xstage.done(ih, "s_pe", tre)
        t_wo = tlast

        for s in (range(NSEG - 1, -1, -1) if stage >= 2 else []):
            nxt = s + 1
            if s == NSEG - 1:
                t_qld = t_kld = None
                W_Q = ("s_dve", st[("qev", NSEG - 1, NB - 1)])
                W_K = ("s_dve", st[("kev", NSEG - 1, NB - 1)])
            else:
                t_qld = dma("sync", qseg[:], q_sp[s],
                            waits=[full("s_qsp"),
                                   ("s_pe", st.get(("p2score", nxt))),
                                   ("s_dve", st.get(("p2qtt", nxt)))],
                            sem="s_ql")
                t_kld = dma("sync", kseg[:], k_sp[s],
                            waits=[full("s_ksp"),
                                   ("s_pe", st.get(("p2score", nxt))),
                                   ("s_dve", t_p1_dve)], sem="s_kl")
                W_Q = ("s_ql", t_qld)
                W_K = ("s_kl", t_kld)
            if s == NSEG - 2:
                for ib in range(NB):
                    dma("sync", nbuf[:, 1024 * ib:1024 * (ib + 1)],
                        g_sp[nxt][:, 1024 * ib:1024 * (ib + 1)],
                        waits=[full("s_gsp0"), full("s_gsp1"),
                               ("s_dve", t_p1_dve)], sem="s_nb")
                nw = full("s_nb")
            elif s < NSEG - 2:
                tnu = None
                for ib in range(NB):
                    ig, gsl, gg = rot_nstage.next()
                    sg = ("s_ga", "s_gb", "s_gc", "s_gd")[gsl]
                    tgl = dma("sync",
                              gstg2[:, 1024 * gsl:1024 * (gsl + 1)],
                              g_sp[nxt][:, 1024 * ib:1024 * (ib + 1)],
                              waits=[full("s_gsp0"), full("s_gsp1"), gg,
                                     ("s_act", t_wo)],
                              sem=sg)
                    tnu = dve(lambda e, S, ib=ib, gsl=gsl:
                              e.scalar_tensor_tensor(
                                  nbuf[:, 1024 * ib:1024 * (ib + 1)],
                                  nbuf[:, 1024 * ib:1024 * (ib + 1)],
                                  alpha_bf[:, APW:APW + 1],
                                  gstg2[:, 1024 * gsl:1024 * (gsl + 1)],
                                  OP.mult, OP.add),
                              waits=[(sg, tgl), W_APW,
                                     ("s_pe", st.get(("p2ret", nxt)))])
                    rot_nstage.done(ig, "s_dve", tnu)
                st[("nscan", s)] = tnu
                nw = ("s_dve", tnu)
            else:
                nw = None

            if s == NSEG - 1:
                W_V = ("s_dve", st[("vev_last", NSEG - 1)])
            else:
                t_vld = dma("sync", vseg[:], v_sp[s],
                            waits=[full("s_vsp"),
                                   ("s_pe", st.get(("p2ret", nxt)))],
                            sem="s_vl")
                W_V = ("s_vl", t_vld)

            t_scev = {}
            tsc_mm = None
            for b in range(4):
                ips, ps, gp = rot_ps.next()
                t = None
                for ib in range(NB):
                    t = mm(ps[:],
                           kseg[:, 512 * ib + 128 * b:512 * ib + 128 * (b + 1)],
                           qseg[:, 512 * ib:512 * (ib + 1)],
                           ib == 0, ib == NB - 1,
                           waits=[W_K, W_Q] +
                                 ([gp] if ib == 0 else []),
                           inc="s_pe" if ib == NB - 1 else None)
                tsc_mm = t
                tev = dve(lambda e, S, ps=ps, b=b: e.tensor_mul(
                    scT[:, 512 * b:512 * (b + 1)], ps[:],
                    dmat[:, 512 * b:512 * (b + 1)]),
                    waits=[("s_pe", t), ("s_dve", t_dmat),
                           ("s_pe", st.get(("p2ret", nxt))),
                           ("s_dve", t_p1_dve)])
                rot_ps.done(ips, "s_dve", tev)
                t_scev[b] = tev
            st[("p2score", s)] = tsc_mm

            t_qt = None
            for ib in range(NB):
                t_qt = dve(lambda e, S, ib=ib: e.tensor_mul(
                    qtil[:, 512 * ib:512 * (ib + 1)],
                    qseg[:, 512 * ib:512 * (ib + 1)], bqtil[:]),
                    waits=[W_Q, ("s_dve", t_bqtil),
                           ("s_dve", t_p1_dve),
                           ("s_pe", st.get(("p2ret", nxt))),
                           ("s_qtsp", st.get(("qtsp", nxt)))])
            st[("p2qtt", s)] = t_qt
            st[("qtsp", s)] = dma("gpsimd", qt_sp[s], qtil[:],
                                  waits=[("s_dve", t_qt)], sem="s_qtsp")

            t_ret = None
            tevl = None
            for j in range(NB):
                ips, ps, gp = rot_ps.next()
                t = None
                for b in range(4):
                    t = mm(ps[:],
                           vseg[:, 1024 * b + 128 * j:1024 * b + 128 * (j + 1)],
                           scT[:, 512 * b:512 * (b + 1)],
                           b == 0, (b == 3 and s == NSEG - 1),
                           waits=[W_V, ("s_dve", t_scev[b])] +
                                 ([gp] if b == 0 else []),
                           inc="s_pe" if (b == 3 and s == NSEG - 1) else None)
                if s < NSEG - 1:
                    for ib in range(NB):
                        t = mm(ps[:],
                               nbuf[:, 1024 * ib + 128 * j:
                                    1024 * ib + 128 * (j + 1)],
                               qtil[:, 512 * ib:512 * (ib + 1)],
                               False, ib == NB - 1,
                               waits=[nw, ("s_dve", t_qt)],
                               inc="s_pe" if ib == NB - 1 else None)
                t_ret = t
                tev = dve(lambda e, S, ps=ps, j=j: e.tensor_copy(
                    retrev[:, 512 * j:512 * (j + 1)], ps[:]),
                    waits=[("s_pe", t),
                           ("s_pe", st.get(("p2woE", nxt)))])
                rot_ps.done(ips, "s_dve", tev)
                st[("rev2a", s, j)] = tev
            st[("p2ret", s)] = t_ret
            tevl = None
            t_woE = None
            for o in range(NB):
                ips, ps, gp = rot_ps.next()
                t = None
                for eb in range(NB):
                    t = mm(ps[:],
                           woT[:, 1024 * eb + 128 * o:1024 * eb + 128 * (o + 1)],
                           retrev[:, 512 * eb:512 * (eb + 1)],
                           eb == 0, eb == NB - 1,
                           waits=[("s_dve", st[("rev2a", s, eb)]),
                                  ("s_act", t_wo)] + ([gp] if eb == 0 else []),
                           inc="s_pe" if eb == NB - 1 else None)
                t_woE = t
                tev = dve(lambda e, S, ps=ps, o=o: e.tensor_copy(
                    oebuf[:, 512 * o:512 * (o + 1)], ps[:]),
                    waits=[("s_pe", t), ("s_rsp", st.get(("rsp", nxt))),
                           ("s_xfa", st.get(("xfsp", 2)))])
                rot_ps.done(ips, "s_dve", tev)
                tevl = tev
            st[("p2woE", s)] = t_woE
            st[("rsp", s)] = dma("gpsimd", retr_sp[s], oebuf[:],
                                 waits=[("s_dve", tevl)], sem="s_rsp")

        t_sc = None
        pair_guard = {}
        for p in range(4 if stage >= 2 else 0):
            tacc0 = tacc1 = None
            for j in range(NCORE):
                sl = j % 4
                sag = ("s_aga", "s_agb", "s_age", "s_agf")[sl]
                tld = dma("gpsimd", agstage2[:, 2048 * sl:2048 * sl + 1024],
                          ag_out[NB * j + 2 * p],
                          waits=[("s_cc", t_cc), full("s_dbg"),
                                 ("s_dve", pair_guard.get(sl))],
                          sem=sag)
                tld = dma("gpsimd", agstage2[:, 2048 * sl + 1024:
                                             2048 * (sl + 1)],
                          ag_out[NB * j + 2 * p + 1],
                          waits=[], sem=sag)
                for h in range(2):
                    stg = agstage2[:, 2048 * sl + 1024 * h:
                                   2048 * sl + 1024 * (h + 1)]
                    ib = 2 * p + h
                    dst = sc_sb[:, 1024 * ib:1024 * (ib + 1)]
                    if j == 0:
                        tacc = dve(lambda e, S, stg=stg, dst=dst:
                                   e.tensor_scalar(dst, stg,
                                                   wcomb_c[:, 0:1], None,
                                                   OP.mult),
                                   waits=[(sag, tld), ("s_dve", t_wcomb),
                                          ("s_dve", t_p1_dve)])
                    else:
                        tacc = dve(lambda e, S, stg=stg, dst=dst, j=j:
                                   e.scalar_tensor_tensor(
                                       dst, stg, wcomb_c[:, j:j + 1], dst,
                                       OP.mult, OP.add),
                                   waits=[(sag, tld),
                                          ("s_dve", tacc0 if h == 0
                                           else tacc1)])
                    if h == 0:
                        tacc0 = tacc
                    else:
                        tacc1 = tacc
                pair_guard[sl] = tacc1
            st[("scdone", 2 * p)] = tacc0
            st[("scdone", 2 * p + 1)] = tacc1
            t_sc = tacc1
        if stage >= 2:
            n_dbg += dump("sc", sc_sb[:], [("s_dve", t_sc)])

        def p2b_x(s):
            xf_s = xf if s % 2 == 0 else xf2
            prev2 = st.get(("p2ln", s + 2))
            xw = [("s_xfa" if s % 2 == 0 else "s_xfb",
                   st[("xfsp", s)])]
            if s == NSEG - 1:
                xw.append(("s_dve", st.get(("nscan", 0))))
            if prev2 is not None:
                xw += [("s_dve", prev2["xf_dve"]),
                       ("s_act", prev2["xf_act"]),
                       ("s_x2sp" if s % 2 == 0 else "s_x2b2",
                        st.get(("x2sp", s + 2)))]
            if s >= NSEG - 2:
                p1l = st.get(("p1ln", 2 + (s % 2)))
                if p1l is not None:
                    xw += [("s_dve", p1l["xf_dve"]),
                           ("s_act", p1l["xf_act"])]
            if s % 2 == 0:
                xw.append(full("s_rsp"))
            t_xl = dma("sync", xf_s[:], xf_sp[s], waits=xw,
                       sem="s_xla" if s % 2 == 0 else "s_xlb")
            st[("p2bx", s)] = (xf_s, None, t_xl)

        for s in (range(NSEG - 1, -1, -1) if stage >= 2 else []):
            first = (s == NSEG - 1)
            nxt = s + 1
            qtil_s = qtil if s % 2 == 0 else qtil2
            rstr_s = rstr if s % 2 == 0 else rstr2
            sql = "s_ql" if s % 2 == 0 else "s_ql2"
            srl = "s_rl" if s % 2 == 0 else "s_rl2"
            t_qtl = dma("sync", qtil_s[:], qt_sp[s],
                        waits=[full("s_qtsp"),
                               ("s_pe", st.get(("p2ret", 0))),
                               ("s_pe", st.get(("p2score", 0))),
                               ("s_pe", st.get(("p2bS", s + 2)))],
                        sem=sql)
            t_rld = dma("sync", rstr_s[:], retr_sp[s],
                        waits=[full("s_rsp"),
                               ("s_dve", st.get(("nscan", 0))),
                               ("s_dve", st.get(("bev", s + 2)))],
                        sem=srl)
            if s == NSEG - 1:
                p2b_x(s)
            xf_s, t_xf, t_xl = st[("p2bx", s)]
            W_XF = ("s_xla" if s % 2 == 0 else "s_xlb", t_xl)

            tbs = None
            tbev = None
            for j in range(NB):
                ips, ps, gp = rot_ps.next()
                t = None
                for ib in range(NB):
                    t = mm(ps[:],
                           sc_sb[:, 1024 * ib + 128 * j:
                                 1024 * ib + 128 * (j + 1)],
                           qtil_s[:, 512 * ib:512 * (ib + 1)],
                           ib == 0, ib == NB - 1,
                           waits=[("s_dve", st[("scdone", ib)]),
                                  (sql, t_qtl)] +
                                 ([gp] if ib == 0 else []),
                           inc="s_pe" if ib == NB - 1 else None)
                tbs = t
                tev = dve(lambda e, S, ps=ps, j=j: e.tensor_copy(
                    retrev[:, 512 * j:512 * (j + 1)], ps[:]),
                    waits=[("s_pe", t),
                           ("s_pe", st.get(("p2wo", nxt))),
                           ("s_pe", st.get(("p2woE", 0)))])
                rot_ps.done(ips, "s_dve", tev)
                st[("rev", s, j)] = tev
                tbev = tev
            st[("p2bS", s)] = tbs
            st[("bev", s)] = tbev

            t1w = {}
            for o in range(NB):
                t1w[o] = dve(lambda e, S, o=o, xf_s=xf_s, rstr_s=rstr_s:
                             e.scalar_tensor_tensor(
                    xf_s[:, 512 * o:512 * (o + 1)],
                    rstr_s[:, 512 * o:512 * (o + 1)],
                    gaeff_c[:, o:o + 1], xf_s[:, 512 * o:512 * (o + 1)],
                    OP.mult, OP.add),
                    waits=[(srl, t_rld), ("s_dve", t_mod), W_XF,
                           ("s_dve", t_gaefa)])

            if s > 0:
                p2b_x(s - 1)

            gcol = (gaeff_c if first else
                    gaefa[:, 8 * (NSEG - 2 - s):8 * (NSEG - 1 - s)])
            t_wo_mm = None
            tx2 = None
            for o in range(NB):
                ips, ps, gp = rot_ps.next()
                t = None
                for eb in range(NB):
                    t = mm(ps[:],
                           woT[:, 1024 * eb + 128 * o:1024 * eb + 128 * (o + 1)],
                           retrev[:, 512 * eb:512 * (eb + 1)],
                           eb == 0, eb == NB - 1,
                           waits=[("s_dve", st[("rev", s, eb)]),
                                  ("s_act", t_wo)] + ([gp] if eb == 0 else []),
                           inc="s_pe" if eb == NB - 1 else None)
                t_wo_mm = t
                tx2 = dve(lambda e, S, ps=ps, o=o, xf_s=xf_s, gcol=gcol:
                          e.scalar_tensor_tensor(
                    xf_s[:, 512 * o:512 * (o + 1)], ps[:],
                    gcol[:, o:o + 1], xf_s[:, 512 * o:512 * (o + 1)],
                    OP.mult, OP.add),
                    waits=[("s_pe", t), ("s_dve", t1w[o])])
                rot_ps.done(ips, "s_dve", tx2)
            st[("p2wo", s)] = t_wo_mm
            st[("x2w", s)] = tx2
            st[("x2sp", s)] = dma("gpsimd", x2_sp[s], xf_s[:],
                                  waits=[("s_dve", tx2)],
                                  sem="s_x2sp" if s % 2 == 0 else "s_x2b2")

            lnres2 = plan_ln(xf_s, hseg, cb1_c, sb_c,
                             [("s_dve", tx2)], ("l2", s),
                             h_guard=("s_h2sp", st.get(("h2sp", nxt)))
                             if not first else
                             ("s_pe", st.get(("vmm_last", NSEG - 1))))
            st[("p2ln", s)] = lnres2
            st[("h2sp", s)] = dma("gpsimd", h2_sp[s], hseg[:],
                                  waits=[("s_dve", lnres2["h_last"])],
                                  sem="s_h2sp")

        W_P2 = ([full("s_x2sp"), full("s_x2b2"), full("s_h2sp"),
                 full("s_dbg")] if stage >= 3 else [])
        W_W1G = []
        if stage >= 3:
            for wq in range(4):
                sw = ("s_w1", "s_w1b", "s_w1c", "s_w1d")[wq]
                for db in range(NB):
                    dma("sync",
                        w1sb[:, FW * db + 1024 * wq:FW * db + 1024 * (wq + 1)],
                        w1bf[128 * db:128 * (db + 1),
                             1024 * wq:1024 * (wq + 1)],
                        waits=W_P2 + [W_WPREP1], sem=sw)
                W_W1G.append(full(sw))
        for wb in range(NWB if stage >= 3 else 0):
            dma("sync" if wb % 2 == 0 else "scalar",
                w2sb[:, 1024 * wb:1024 * (wb + 1)],
                w2bf[128 * wb:128 * (wb + 1), :],
                waits=W_P2 + [W_WPREP2],
                sem="s_w2" if wb % 2 == 0 else "s_w2b")
        W_W2 = full("s_w2")
        W_W2B = full("s_w2b")

        n_out = 0
        for s in range(NSEG if stage >= 3 else 0):
            t_h2l = dma("sync", h2c[:], h2_sp[s],
                        waits=W_P2 + [("s_pe", st.get(("ffnmid", s - 1)))],
                        sem="s_hl")
            t_x2l = dma("sync", x2c[:], x2_sp[s],
                        waits=W_P2 + [("s_dve", st.get(("ffnx3", s - 1))),
                                      ("s_pe", st.get(("p3tr", s - 1)))],
                        sem="s_x2l")
            tmid = None
            for wb in range(NWB):
                ips, ps, gp = rot_ps.next()
                t = None
                for db in range(NB):
                    t = mm(ps[:],
                           w1sb[:, FW * db + 128 * wb:FW * db + 128 * (wb + 1)],
                           h2c[:, 512 * db:512 * (db + 1)],
                           db == 0, db == NB - 1,
                           waits=[("s_hl", t_h2l), W_W1G[wb // 8]] +
                                 ([gp] if db == 0 else []),
                           inc="s_pe" if db == NB - 1 else None)
                tmid = t
                te = act(lambda e, S, ps=ps, wb=wb: e.activation(
                    midc[:, 512 * wb:512 * (wb + 1)], ps[:], AF.Relu,
                    bias=b1col[:, wb:wb + 1]),
                    waits=[("s_pe", t), ("s_pe", st.get(("ffnout", s - 1)))])
                rot_ps.done(ips, "s_act", te)
                st[("ffnme", s, wb)] = te
            st[("ffnmid", s)] = tmid
            tout = None
            tx3 = None
            for o in range(NB):
                ipg, po, gp = rot_ps.next()
                t = mm(po, b2row[0:1, 128 * o:128 * (o + 1)],
                       onesrow512bf[:], True, False,
                       waits=[W_W2, W_W2B, W_WPREP2, gp, W_CONST])
                for wb in range(NWB):
                    t = mm(po,
                           w2sb[:, 1024 * wb + 128 * o:1024 * wb + 128 * (o + 1)],
                           midc[:, 512 * wb:512 * (wb + 1)],
                           False, wb == NWB - 1,
                           waits=[("s_act", st[("ffnme", s, wb)]), W_W2],
                           inc="s_pe" if wb == NWB - 1 else None)
                tout = t
                tx3 = dve(lambda e, S, po=po, o=o: e.scalar_tensor_tensor(
                    x2c[:, 512 * o:512 * (o + 1)], po,
                    gb_c[:, o:o + 1], x2c[:, 512 * o:512 * (o + 1)],
                    OP.mult, OP.add),
                    waits=[("s_pe", t), ("s_x2l", t_x2l), ("s_dve", t_mod)])
                rot_ps.done(ipg, "s_dve", tx3)
                st[("x3", s, o)] = tx3
            st[("ffnout", s)] = tout
            st[("ffnx3", s)] = tx3
            tp3 = None
            for b in range(4):
                io, osl, go = rot_ostage.next()
                tev = None
                for o in range(NB):
                    ipt, pt, gp = rot_pt.next()
                    t = tr(pt[:, 0:128],
                           x2c[:, 512 * o + 128 * b:512 * o + 128 * (b + 1)],
                           ident32[:],
                           waits=[("s_dve", st[("x3", s, o)]), gp] +
                                 ([go] if o == 0 else []))
                    tp3 = t
                    tev = act(lambda e, S, pt=pt, o=o, osl=osl: e.activation(
                        ostage[:, 1024 * osl + 128 * o:
                               1024 * osl + 128 * (o + 1)],
                        pt[:, 0:128], AF.Copy), waits=[("s_pe", t)])
                    rot_pt.done(ipt, "s_act", tev)
                so = "s_oa" if osl == 0 else "s_ob"
                tod = dma("sync",
                          out_dram[512 * s + 128 * b:512 * s + 128 * (b + 1), :],
                          ostage[:, 1024 * osl:1024 * (osl + 1)],
                          waits=[("s_act", tev)], sem=so)
                rot_ostage.done(io, so, tod)
                n_out += 1
            st[("p3tr", s)] = tp3

        P.emit("sync", lambda e, S: e.nop(),
               waits=[full("s_oa"), full("s_ob"), full("s_dbg")])

        with nc.Block() as block:
            def mk(ename):
                def body(eng):
                    for fn, waits, inc, amt in P.seq[ename]:
                        for semn, val in waits:
                            eng.wait_ge(sems[semn], val)
                        inst = fn(eng, sems)
                        if inc is not None:
                            inst.then_inc(sems[inc], amt)
                return body
            for ename in ENGINES:
                if P.seq[ename]:
                    getattr(block, ename)(mk(ename))

    return nc


_CACHE = {}


def _get_nc(debug=(), stage=3):
    key = (tuple((n, tuple(s), d) for n, s, d in debug), stage)
    if key not in _CACHE:
        _CACHE[key] = build_program(debug, stage)
    return _CACHE[key]


def _consts():
    bf = ml_dtypes.bfloat16
    return {
        "ident_f32": np.eye(128, dtype=np.float32),
        "ident_bf16": np.eye(128, dtype=np.float32).astype(bf),
        "ones_row128": np.ones((1, 128), np.float32),
        "ones_row512_bf": np.ones((1, 512), np.float32).astype(bf),
        "ones_col_bf": np.ones((128, 1), np.float32).astype(bf),
        "iota_col": np.arange(128, dtype=np.float32).reshape(128, 1),
        "iota_row128": np.arange(128, dtype=np.float32).reshape(1, 128),
        "iota_row512": np.arange(512, dtype=np.float32).reshape(1, 512),
        "expo_t": np.concatenate(
            [(128 * b + np.arange(128)[:, None]
              - np.arange(512)[None, :] - 1.0).astype(np.float32)
             for b in range(4)], axis=1),
    }


def kernel(L0_attn_x, L0_attn_Wq, L0_attn_Wk, L0_attn_Wv, L0_attn_Wo,
           L0_attn_decay_logit, L0_attn_out_scale,
           L0_ffn_w1, L0_ffn_b1, L0_ffn_w2, L0_ffn_b2, L0_c, L0_wmod,
           _debug=(), _stage=3):
    nc = _get_nc(_debug, _stage)
    x2d = np.ascontiguousarray(np.asarray(L0_attn_x, np.float32)
                               .reshape(T, D))
    common = dict(
        wq=np.ascontiguousarray(np.asarray(L0_attn_Wq, np.float32)),
        wk=np.ascontiguousarray(np.asarray(L0_attn_Wk, np.float32)),
        wv=np.ascontiguousarray(np.asarray(L0_attn_Wv, np.float32)),
        wo=np.ascontiguousarray(np.asarray(L0_attn_Wo, np.float32)),
        w1=np.ascontiguousarray(np.asarray(L0_ffn_w1, np.float32)),
        w2=np.ascontiguousarray(np.asarray(L0_ffn_w2, np.float32)),
        b1col=np.ascontiguousarray(
            np.asarray(L0_ffn_b1, np.float32).reshape(NWB, 128).T),
        b2row=np.ascontiguousarray(
            np.asarray(L0_ffn_b2, np.float32).reshape(1, D)),
        c_cols=np.ascontiguousarray(
            np.asarray(L0_c, np.float32).reshape(NB, 128).T),
        wmod=np.ascontiguousarray(np.asarray(L0_wmod, np.float32)),
        decay_logit=np.asarray(L0_attn_decay_logit,
                               np.float32).reshape(1, 1),
        out_scale=np.asarray(L0_attn_out_scale, np.float32).reshape(1, 1),
        **_consts(),
    )
    in_maps = []
    for c in range(NCORE):
        ecore = np.zeros((1, NCORE), np.float32)
        emask = np.zeros((1, NCORE), np.float32)
        for j in range(NCORE):
            if j > c:
                ecore[0, j] = float(TC * (j - c - 1))
                emask[0, j] = 1.0
        m = dict(common)
        m["x"] = np.ascontiguousarray(x2d[TC * c:TC * (c + 1)])
        m["ecore"] = ecore
        m["emask"] = emask
        in_maps.append(m)
    res = run_bass_kernel_spmd(nc, in_maps, core_ids=list(range(NCORE)))
    out = np.concatenate([res.results[c]["out"] for c in range(NCORE)],
                         axis=0)
    if _debug:
        return out.reshape(1, T, D), list(res.results)
    return out.reshape(1, T, D)


# version 15
# speedup vs baseline: 1.0203x; 1.0203x over previous
import numpy as np
import ml_dtypes
from contextlib import ExitStack

import concourse.bass as bass
import concourse.mybir as mybir
from concourse.bass_utils import run_bass_kernel_spmd

F32 = mybir.dt.float32
BF16 = mybir.dt.bfloat16
AF = mybir.ActivationFunctionType
OP = mybir.AluOpType

T = 16384
D = 1024
FW = 4096
NCORE = 8
TC = T // NCORE
SEG = 512
NSEG = TC // SEG
NB = D // 128
NWB = FW // 128
LOG_INV_SQRT_D = float(np.log(1.0 / np.sqrt(D)))

ENGINES = ("sync", "tensor", "vector", "scalar", "gpsimd")


class Plan:

    def __init__(self):
        self.seq = {e: [] for e in ENGINES}
        self.cnt = {}
        self.last_wait = {e: {} for e in ENGINES}

    def emit(self, eng, fn, waits=(), inc=None, amt=1):
        w = []
        lw = self.last_wait[eng]
        for item in waits:
            if item is None:
                continue
            sem, val = item
            if val is None:
                continue
            if val > lw.get(sem, 0):
                lw[sem] = val
                w.append((sem, val))
        t = None
        if inc is not None:
            self.cnt[inc] = self.cnt.get(inc, 0) + amt
            t = self.cnt[inc]
        self.seq[eng].append((fn, tuple(w), inc, amt))
        return t


class Rot:

    def __init__(self, slots):
        self.slots = list(slots)
        self.n = 0
        self.last = {}

    def next(self):
        i = self.n % len(self.slots)
        self.n += 1
        return i, self.slots[i], self.last.get(i)

    def done(self, i, sem, ticket):
        self.last[i] = (sem, ticket)


def build_program(debug=(), stage=3):
    nc = bass.Bass("TRN2", target_bir_lowering=False, debug=False)

    def ein(name, shape, dt=F32):
        return nc.dram_tensor(name, shape, dt, kind="ExternalInput").ap()

    x_in = ein("x", [TC, D])
    wq_in = ein("wq", [D, D])
    wk_in = ein("wk", [D, D])
    wv_in = ein("wv", [D, D])
    wo_in = ein("wo", [D, D])
    w1_in = ein("w1", [D, FW])
    b1_in = ein("b1col", [128, NWB])
    w2_in = ein("w2", [FW, D])
    b2_in = ein("b2row", [1, D])
    c_in = ein("c_cols", [128, NB])
    wmod_in = ein("wmod", [D, 6 * D])
    dlog_in = ein("decay_logit", [1, 1])
    osc_in = ein("out_scale", [1, 1])
    id32_in = ein("ident_f32", [128, 128])
    idbf_in = ein("ident_bf16", [128, 128], BF16)
    onesrow128_in = ein("ones_row128", [1, 128])
    onesrow512bf_in = ein("ones_row512_bf", [1, 512], BF16)
    onescol_bf_in = ein("ones_col_bf", [128, 1], BF16)
    iotacol_in = ein("iota_col", [128, 1])
    iotarow128_in = ein("iota_row128", [1, 128])
    iotarow512_in = ein("iota_row512", [1, 512])
    ecore_in = ein("ecore", [1, NCORE])
    emask_in = ein("emask", [1, NCORE])
    expo_in = ein("expo_t", [128, 4 * SEG])

    out_dram = nc.dram_tensor("out", [TC, D], F32, kind="ExternalOutput").ap()

    def scr(name, shape, dt):
        return nc.dram_tensor(name, shape, dt, kind="Internal").ap()

    q_sp = scr("q_sp", [NSEG, 128, NB * SEG], BF16)
    k_sp = scr("k_sp", [NSEG, 128, NB * SEG], BF16)
    v_sp = scr("v_sp", [NSEG, 128, NSEG * D], BF16)
    g_sp = scr("g_sp", [NSEG, 128, NB * D], BF16)
    x2_sp = scr("x2_sp", [NSEG, 128, NB * SEG], F32)
    retr_sp = scr("retr_sp", [NSEG, 128, NB * SEG], BF16)
    qt_sp = scr("qt_sp", [NSEG, 128, NB * SEG], BF16)
    xf_sp = scr("xf_sp", [NSEG, 128, NB * SEG], F32)
    w1bf = scr("w1bf", [D, FW], BF16)
    w2bf = scr("w2bf", [FW, D], BF16)
    h2_sp = scr("h2_sp", [NSEG, 128, NB * SEG], BF16)
    ag_in = scr("ag_in", [NB, 128, D], BF16)
    ag_out = nc.dram_tensor("ag_out", [NCORE * NB, 128, D], BF16,
                            kind="Internal", addr_space="Shared").ap()

    dbg = {}
    for name, shape, dt in debug:
        dbg[name] = nc.dram_tensor("dbg_" + name, shape, dt,
                                   kind="ExternalOutput").ap()

    P = Plan()
    st = {}

    with ExitStack() as ctx:
        SB_BASE = 18432
        SB_LIMIT = 16384 + 212863
        offmap = {}
        cur = [SB_BASE]

        def sb(name, shape, dt, at=None):
            size = shape[1] * mybir.dt.size(dt)
            if at is None:
                at = cur[0]
                cur[0] = (at + size + 63) & ~63
            offmap[name] = at
            return nc.alloc_sbuf_tensor_at(name, shape, dt, offset=at)

        ident32 = sb("ident32", [128, 128], F32)
        identbf = sb("identbf", [128, 128], BF16)
        onesrow128 = sb("onesrow128", [1, 128], F32)
        onesrow512bf = sb("onesrow512bf", [1, 512], BF16)
        onescolbf = sb("onescolbf", [128, 1], BF16)
        iotacol = sb("iotacol", [128, 1], F32)
        iotarow128 = sb("iotarow128", [1, 128], F32)
        iotarow512 = sb("iotarow512", [1, 512], F32)
        b2row = sb("b2row", [1, D], BF16)
        b1col = sb("b1col", [128, NWB], F32)
        c_cols = sb("c_cols", [128, NB], F32)
        mod_sb = sb("mod_sb", [128, 48], F32)
        derv = sb("derv", [128, 32], F32)
        dcol = sb("dcol", [128, 32], F32)
        rows2 = sb("rows2", [1, 64], F32)
        PERM_END = cur[0]
        rows = sb("rows", [1, 512 * 5], F32)
        dmat = sb("dmat", [128, 4 * SEG], BF16)
        bqtil = sb("bqtil", [128, SEG], F32)
        tstage = sb("tstage", [128, 512], F32)
        tstage2 = sb("tstage2", [128, 512], F32)
        expo_sb = sb("expo_sb", [128, 4 * SEG], F32)

        wqT = sb("wqT", [128, NB * D], BF16)
        wkT = sb("wkT", [128, NB * D], BF16)
        wvT = sb("wvT", [128, NB * D], BF16)
        xstage = sb("xstage", [128, 2 * D], F32)
        xf = sb("xf", [128, NB * SEG], F32)
        hseg = sb("hseg", [128, NB * SEG], BF16)
        sblk = sb("sblk", [128, 4 * 512], BF16)
        qseg = sb("qseg", [128, NB * SEG], BF16)
        kseg = sb("kseg", [128, NB * SEG], BF16)
        vseg = sb("vseg", [128, NSEG * D], BF16)
        ktil = sb("ktil", [128, 4 * D], BF16)
        khat = sb("khat", [128, 4 * D], BF16)
        ghat = sb("ghat", [128, NB * D], BF16)
        gstage = sb("gstage", [128, 2 * D], BF16)
        retrev = sb("retrev", [128, NB * SEG], BF16)
        xf2 = sb("xf2", [128, NB * SEG], F32)
        A_END = cur[0]
        assert A_END <= SB_LIMIT, A_END

        woT = sb("woT", [128, NB * D], BF16, at=offmap["wqT"])
        nbuf = sb("nbuf", [128, NB * D], BF16, at=offmap["wkT"])
        sc_sb = sb("sc_sb", [128, NB * D], BF16, at=offmap["wvT"])
        mbuf = sb("mbuf", [128, NB * D], BF16, at=offmap["ghat"])
        scT = sb("scT", [128, 4 * SEG], BF16, at=offmap["ktil"])
        qtil = sb("qtil", [128, NB * SEG], BF16, at=offmap["khat"])
        agstage = sb("agstage", [128, 2 * D], BF16, at=offmap["gstage"])
        rstr = sb("rstr", [128, NB * SEG], BF16, at=offmap["ktil"])
        gstg2 = sb("gstg2", [128, 4 * D], BF16, at=offmap["xstage"])
        rstr2 = sb("rstr2", [128, NB * SEG], BF16, at=offmap["kseg"])
        oebuf = sb("oebuf", [128, NB * SEG], BF16, at=offmap["xf"])
        qtil2 = sb("qtil2", [128, NB * SEG], BF16, at=offmap["vseg"])
        agstage2 = sb("agstage2", [128, NB * D], BF16, at=offmap["ghat"])

        fcur = [offmap["rows"]]

        def sbf(name, shape, dt):
            size = shape[1] * mybir.dt.size(dt)
            at = fcur[0]
            fcur[0] = (at + size + 63) & ~63
            offmap[name] = at
            return nc.alloc_sbuf_tensor_at(name, shape, dt, offset=at)

        w1sb = sbf("w1sb", [128, NB * FW], BF16)
        w2sb = sbf("w2sb", [128, NWB * D], BF16)
        midc = sbf("midc", [128, NWB * SEG], BF16)
        h2c = sbf("h2c", [128, NB * SEG], BF16)
        x2c = sbf("x2c", [128, NB * SEG], F32)
        ostage = sbf("ostage", [128, 2 * D], F32)
        assert fcur[0] <= SB_LIMIT, fcur[0]

        ps_a = ctx.enter_context(nc.psum_tensor("ps_a", [128, 512], F32))
        ps_b = ctx.enter_context(nc.psum_tensor("ps_b", [128, 512], F32))
        pg_a = ctx.enter_context(nc.psum_tensor("pg_a", [128, 1024], F32))
        pg_b = ctx.enter_context(nc.psum_tensor("pg_b", [128, 1024], F32))
        pk_a = ctx.enter_context(nc.psum_tensor("pk_a", [128, 128], BF16))
        pk_b = ctx.enter_context(nc.psum_tensor("pk_b", [128, 128], BF16))

        sem_names = ("s_c", "s_xa", "s_xb", "s_ql", "s_kl", "s_vl",
                     "s_nb", "s_ga", "s_gb", "s_aga", "s_agb",
                     "s_hl", "s_x2l", "s_oa", "s_ob",
                     "s_qsp", "s_ksp", "s_vsp", "s_gsp0", "s_gsp1",
                     "s_x2sp", "s_h2sp", "s_agin", "s_w1", "s_w2", "s_dbg",
                     "s_rsp", "s_rl", "s_x2b2", "s_qtsp", "s_w1b", "s_w1c", "s_w1d", "s_gc", "s_gd", "s_age", "s_agf", "s_ql2", "s_rl2", "s_wp1", "s_wp2", "s_w2b", "s_xfa", "s_xfb", "s_xla", "s_xlb",
                     "s_pe", "s_dve", "s_act", "s_cc")
        sems = {n: ctx.enter_context(nc.semaphore(n)) for n in sem_names}

        rot_ps = Rot([ps_a[:, 0:512], ps_b[:, 0:512],
                      pg_a[:, 0:512], pg_a[:, 512:1024],
                      pg_b[:, 0:512], pg_b[:, 512:1024]])
        rot_pt = rot_ps
        rot_pk = Rot([pk_a, pk_b])
        rot_xstage = Rot([0, 1])
        rot_nstage = Rot([0, 1, 2, 3])
        rot_gstage = Rot([0, 1])
        rot_ostage = Rot([0, 1])
        rot_sblk1 = Rot([0, 1])
        rot_sblk2 = Rot([2, 3])

        def dma(eng, out_ap, in_ap, waits=(), sem=None):
            assert sem is not None
            return P.emit(eng,
                          lambda e, S, o=out_ap, i=in_ap:
                              e.dma_start(out=o, in_=i),
                          waits, sem, 16)

        def full(sem):
            return (sem, P.cnt.get(sem, 0))

        def mm(out_ap, a_ap, b_ap, start, stop, waits=(), inc=None):
            return P.emit("tensor",
                          lambda e, S, o=out_ap, a=a_ap, b=b_ap, s1=start,
                          s2=stop: e.matmul(o, a, b, start=s1, stop=s2),
                          waits, inc)

        def tr(out_ap, in_ap, ident_ap, waits=(), inc="s_pe"):
            return P.emit("tensor",
                          lambda e, S, o=out_ap, i=in_ap, d=ident_ap:
                              e.transpose(o, i, d),
                          waits, inc)

        def dve(fn, waits=(), inc="s_dve"):
            return P.emit("vector", fn, waits, inc)

        def act(fn, waits=(), inc="s_act"):
            return P.emit("scalar", fn, waits, inc)

        def dump(name, src_ap, waits):
            if name in dbg:
                t = dma("sync", dbg[name], src_ap,
                        waits=list(waits) + [st.get("dump_prev")],
                        sem="s_dbg")
                st["dump_prev"] = ("s_dbg", t)
                return 1
            return 0

        n_dbg = 0

        t_cc0 = dma("sync", c_cols[:], c_in, sem="s_hl")
        for ap_sb, ap_in in ((ident32, id32_in), (identbf, idbf_in),
                             (onesrow128, onesrow128_in),
                             (onesrow512bf, onesrow512bf_in),
                             (onescolbf, onescol_bf_in),
                             (iotacol, iotacol_in),
                             (iotarow128, iotarow128_in),
                             (iotarow512, iotarow512_in),
                             (b1col, b1_in)):
            dma("sync", ap_sb[:], ap_in, sem="s_c")
        dma("sync", rows2[0:1, 0:NCORE], ecore_in, sem="s_c")
        dma("sync", rows2[0:1, 8:8 + NCORE], emask_in, sem="s_c")
        dma("sync", rows2[0:1, 16:17], dlog_in, sem="s_c")
        dma("sync", rows2[0:1, 17:18], osc_in, sem="s_c")
        dma("sync", expo_sb[:], expo_in, sem="s_c")
        t_b2 = dma("gpsimd", b2row[:], b2_in, sem="s_wp2")
        if stage >= 3:
            for ch in range(4):
                dma("gpsimd", w1bf[256 * ch:256 * (ch + 1), :],
                    w1_in[256 * ch:256 * (ch + 1), :], sem="s_wp1")
                dma("gpsimd", w2bf[1024 * ch:1024 * (ch + 1), :],
                    w2_in[1024 * ch:1024 * (ch + 1), :], sem="s_wp2")
        W_WPREP1 = full("s_wp1")
        W_WPREP2 = full("s_wp2")
        W_CONST = full("s_c")

        LND, TMP, APW, OSCC, BIA, ONE, KH0 = 0, 1, 2, 5, 6, 7, 8
        t_bia = dve(lambda e, S: e.memset(dcol[:, BIA:BIA + 1],
                                          LOG_INV_SQRT_D))
        t_one = dve(lambda e, S: e.memset(dcol[:, ONE:ONE + 1], 1.0))
        alpha_bf = sb("alpha_bf", [128, 32], BF16)
        gaefa = sb("gaefa", [128, 24], F32)

        i0, pt, g = rot_pt.next()
        t = mm(pt[:, 0:1], onesrow128[:], rows2[0:1, 16:17], True, False,
               waits=[W_CONST, g])
        t = mm(pt[:, 1:2], onesrow128[:], rows2[0:1, 17:18], False, True,
               waits=[], inc="s_pe")
        t_sg = act(lambda e, S, pt=pt: e.activation(
            dcol[:, TMP:TMP + 1], pt[:, 0:1], AF.Sigmoid),
            waits=[("s_pe", t)])
        t_oc = dve(lambda e, S, pt=pt: e.tensor_copy(
            dcol[:, OSCC:OSCC + 1], pt[:, 1:2]), waits=[("s_pe", t)])
        rot_pt.done(i0, "s_dve", t_oc)
        t_lnd = act(lambda e, S: e.activation(
            dcol[:, LND:LND + 1], dcol[:, TMP:TMP + 1], AF.Ln),
            waits=[("s_act", t_sg)])
        t_apw = None
        for p in (1, 2, 3):
            t_apw = act(lambda e, S, p=p: e.activation(
                dcol[:, APW + p - 1:APW + p], dcol[:, LND:LND + 1], AF.Exp,
                scale=float(SEG * p)), waits=[("s_act", t_lnd)])
        t_apwbf = dve(lambda e, S: e.tensor_copy(
            alpha_bf[:, APW:APW + 1], dcol[:, APW:APW + 1]),
            waits=[("s_act", t_apw)], inc="s_dve")
        W_APW = ("s_dve", t_apwbf)
        t_kh = None
        for s in range(NSEG):
            for b in range(4):
                tt = dve(lambda e, S, s=s, b=b: e.tensor_scalar(
                    dcol[:, TMP:TMP + 1], iotacol[:], 1.0,
                    float(SEG * s + 128 * b), OP.mult, OP.add),
                    waits=[W_CONST, ("s_act", t_kh if t_kh else t_lnd)])
                t_kh = act(lambda e, S, s=s, b=b: e.activation(
                    dcol[:, KH0 + 4 * s + b:KH0 + 4 * s + b + 1],
                    dcol[:, TMP:TMP + 1], AF.Exp,
                    scale=dcol[:, LND:LND + 1]), waits=[("s_dve", tt)])

        tt = dve(lambda e, S: e.tensor_scalar(
            rows[0:1, 0:512], iotarow512[:], -1.0, float(SEG - 1),
            OP.mult, OP.add), waits=[W_CONST])
        tt = act(lambda e, S: e.activation(
            rows[0:1, 512:1024], rows[0:1, 0:512], AF.Exp,
            scale=dcol[0:1, LND:LND + 1], bias=dcol[0:1, BIA:BIA + 1]),
            waits=[("s_dve", tt), ("s_act", t_lnd), ("s_dve", t_bia)])
        i0, ps, g = rot_ps.next()
        t = mm(ps[:], onesrow128[:], rows[0:1, 512:1024], True, True,
               waits=[("s_act", tt), g], inc="s_pe")
        t_bqtil = dve(lambda e, S, ps=ps: e.tensor_copy(bqtil[:], ps[:]),
                      waits=[("s_pe", t)])
        rot_ps.done(i0, "s_dve", t_bqtil)

        tt = dve(lambda e, S: e.tensor_scalar(
            rows2[0:1, 24:32], rows2[0:1, 0:8], dcol[0:1, LND:LND + 1],
            None, OP.mult), waits=[W_CONST, ("s_act", t_lnd)])
        tt = act(lambda e, S: e.activation(
            rows2[0:1, 32:40], rows2[0:1, 24:32], AF.Exp),
            waits=[("s_dve", tt)])
        tt = dve(lambda e, S: e.tensor_mul(
            rows2[0:1, 40:48], rows2[0:1, 32:40], rows2[0:1, 8:16]),
            waits=[("s_act", tt)])
        i0, ps, g = rot_ps.next()
        t = mm(ps[:, 0:8], onesrow128[:], rows2[0:1, 40:48], True, True,
               waits=[("s_dve", tt), g], inc="s_pe")
        t_wcomb = dve(lambda e, S, ps=ps: e.tensor_copy(
            derv[:, 24:32], ps[:, 0:8]), waits=[("s_pe", t)])
        rot_ps.done(i0, "s_dve", t_wcomb)

        t_dmat = None
        for b in range(4):
            eb = expo_sb[:, 512 * b:512 * (b + 1)]
            tm = dve(lambda e, S, eb=eb: e.tensor_scalar(
                tstage[:], eb, 0.0, None, OP.is_ge),
                waits=[W_CONST, ("s_dve", t_dmat)])
            tm2 = dve(lambda e, S, eb=eb: e.tensor_scalar(
                tstage2[:], eb, 0.0, None, OP.max),
                waits=[("s_dve", tm)])
            te = act(lambda e, S: e.activation(
                tstage2[:], tstage2[:], AF.Exp,
                scale=dcol[:, LND:LND + 1], bias=dcol[:, BIA:BIA + 1]),
                waits=[("s_dve", tm2), ("s_act", t_lnd), ("s_dve", t_bia)])
            t_dmat = dve(lambda e, S, b=b: e.tensor_mul(
                dmat[:, 512 * b:512 * (b + 1)], tstage2[:], tstage[:]),
                waits=[("s_act", te)])

        t_cs = act(lambda e, S: e.activation(c_cols[:], c_cols[:], AF.Silu),
                   waits=[("s_hl", t_cc0)])
        imod, pmod, gmod = rot_ps.next()
        first_mod = True
        t_mod_mm = None
        for k in range(NB):
            for ch in range(6):
                ih, half, gx = rot_xstage.next()
                tld = dma("sync", xstage[:, 1024 * half:1024 * (half + 1)],
                          wmod_in[128 * k:128 * (k + 1),
                                  1024 * ch:1024 * (ch + 1)],
                          waits=[gx] + ([gmod] if first_mod else []),
                          sem="s_xa" if half == 0 else "s_xb")
                for j2 in range(8):
                    col = 8 * ch + j2
                    last = (k == NB - 1 and col == 47)
                    t_mod_mm = mm(pmod[:, col:col + 1],
                                  xstage[:, 1024 * half + 128 * j2:
                                         1024 * half + 128 * (j2 + 1)],
                                  c_cols[:, k:k + 1],
                                  first_mod, last,
                                  waits=[("s_xa" if half == 0 else "s_xb",
                                          tld), ("s_act", t_cs)],
                                  inc="s_pe" if j2 == 7 else None)
                    first_mod = False
                rot_xstage.done(ih, "s_pe", t_mod_mm)
        t_modev = dve(lambda e, S, pmod=pmod: e.tensor_copy(
            mod_sb[:], pmod[:, 0:48]), waits=[("s_pe", t_mod_mm)])
        rot_ps.done(imod, "s_dve", t_modev)

        tv = dve(lambda e, S: e.tensor_copy(derv[:, 0:8], mod_sb[:, 0:8]),
                 waits=[("s_dve", t_modev)])
        tv = dve(lambda e, S: e.tensor_scalar(
            derv[:, 8:16], mod_sb[:, 8:16], 1.0, 1.0, OP.mult, OP.add),
            waits=[("s_dve", tv)])

        def plan_clamp(src_sl, dst_sl, tprev):
            tv = dve(lambda e, S: e.tensor_scalar(
                tstage[:, 0:8], src_sl, 0.0, -1.0, OP.is_ge, OP.add),
                waits=[("s_dve", tprev)])
            tv = dve(lambda e, S: e.tensor_scalar(
                tstage[:, 0:8], tstage[:, 0:8], 2.0, 1.0, OP.mult, OP.add),
                waits=[("s_dve", tv)])
            tv = dve(lambda e, S: e.tensor_mul(
                dst_sl, src_sl, tstage[:, 0:8]), waits=[("s_dve", tv)])
            tv = dve(lambda e, S: e.tensor_scalar(
                dst_sl, dst_sl, 0.25, None, OP.max), waits=[("s_dve", tv)])
            tv = dve(lambda e, S: e.tensor_mul(
                dst_sl, dst_sl, tstage[:, 0:8]), waits=[("s_dve", tv)])
            return tv

        tv = plan_clamp(mod_sb[:, 16:24], derv[:, 16:24], tv)
        tv = dve(lambda e, S: e.tensor_scalar(
            derv[:, 16:24], derv[:, 16:24], dcol[:, OSCC:OSCC + 1], None,
            OP.mult), waits=[("s_dve", tv), ("s_dve", t_oc)])
        tv = plan_clamp(mod_sb[:, 40:48], mod_sb[:, 40:48], tv)
        t_mod = dve(lambda e, S: e.tensor_scalar(
            mod_sb[:, 32:40], mod_sb[:, 32:40], 1.0, 1.0, OP.mult, OP.add),
            waits=[("s_dve", tv)])
        sa_c, ca1_c, gaeff_c = derv[:, 0:8], derv[:, 8:16], derv[:, 16:24]
        t_gaefa = None
        for k in range(3):
            t_gaefa = dve(lambda e, S, k=k: e.tensor_scalar(
                gaefa[:, 8 * k:8 * (k + 1)], derv[:, 16:24],
                dcol[:, APW + k:APW + k + 1], None, OP.mult),
                waits=[("s_dve", t_mod), ("s_act", t_apw)])
        wcomb_c = derv[:, 24:32]
        sb_c, cb1_c, gb_c = mod_sb[:, 24:32], mod_sb[:, 32:48 - 8], mod_sb[:, 40:48]
        n_dbg += dump("mod", mod_sb[:], [("s_dve", t_mod)])
        n_dbg += dump("derv", derv[:], [("s_dve", t_mod)])
        n_dbg += dump("dcol", dcol[:], [("s_act", t_kh), ("s_act", t_apw)])
        n_dbg += dump("dmat", dmat[:], [("s_dve", t_dmat)])

        def plan_wT(w_in_, wT_):
            tlast = None
            for ib in range(NB):
                ih, half, gx = rot_xstage.next()
                sx = "s_xa" if half == 0 else "s_xb"
                tld = dma("sync", xstage[:, 1024 * half:1024 * (half + 1)],
                          w_in_[128 * ib:128 * (ib + 1), :], waits=[gx],
                          sem=sx)
                tre = None
                for db in range(NB):
                    ipt, pt, gp = rot_pt.next()
                    t = tr(pt[:, 0:128],
                           xstage[:, 1024 * half + 128 * db:
                                  1024 * half + 128 * (db + 1)],
                           ident32[:], waits=[(sx, tld), gp])
                    te = act(lambda e, S, pt=pt, db=db, ib=ib, wT_=wT_:
                             e.activation(
                                 wT_[:, 1024 * db + 128 * ib:
                                     1024 * db + 128 * (ib + 1)],
                                 pt[:, 0:128], AF.Copy), waits=[("s_pe", t)])
                    rot_pt.done(ipt, "s_act", te)
                    tre = t
                    tlast = te
                rot_xstage.done(ih, "s_pe", tre)
            return tlast

        t_wq = plan_wT(wq_in, wqT)
        t_wk = plan_wT(wk_in, wkT)
        t_wv = plan_wT(wv_in, wvT)

        def plan_ln_stats(xf_t, tag, ready):
            isum, psum_s, gsum = rot_ps.next()
            isq, psum_q, gsq = rot_ps.next()
            t_summ = t_sqm = None
            for k in range(NB):
                i1, sl1, g1 = rot_sblk1.next()
                tc1 = dve(lambda e, S, k=k, sl1=sl1: e.tensor_copy(
                    sblk[:, 512 * sl1:512 * (sl1 + 1)],
                    xf_t[:, 512 * k:512 * (k + 1)]),
                    waits=[g1] + ready)
                t_summ = mm(psum_s[0:1, :], onescolbf[:],
                            sblk[:, 512 * sl1:512 * (sl1 + 1)],
                            k == 0, k == NB - 1,
                            waits=[("s_dve", tc1)] + ([gsum] if k == 0 else []),
                            inc="s_pe")
                rot_sblk1.done(i1, "s_pe", t_summ)
                i2, sl2, g2 = rot_sblk2.next()
                tc2 = act(lambda e, S, k=k, sl2=sl2: e.activation(
                    sblk[:, 512 * sl2:512 * (sl2 + 1)],
                    xf_t[:, 512 * k:512 * (k + 1)], AF.Square),
                    waits=[g2] + ready)
                t_sqm = mm(psum_q[0:1, :], onescolbf[:],
                           sblk[:, 512 * sl2:512 * (sl2 + 1)],
                           k == 0, k == NB - 1,
                           waits=[("s_act", tc2)] + ([gsq] if k == 0 else []),
                           inc="s_pe")
                rot_sblk2.done(i2, "s_pe", t_sqm)
                st[(tag, "c2", k)] = tc2

            R = lambda i: rows[0:1, 512 * i:512 * (i + 1)]
            t1 = dve(lambda e, S: e.tensor_scalar(
                R(0), psum_s[0:1, :], 1.0 / D, None, OP.mult),
                waits=[("s_pe", t_summ)])
            rot_ps.done(isum, "s_dve", t1)
            t2 = dve(lambda e, S: e.tensor_scalar(
                R(1), psum_q[0:1, :], 1.0 / D, None, OP.mult),
                waits=[("s_pe", t_sqm)])
            rot_ps.done(isq, "s_dve", t2)
            t3 = dve(lambda e, S: e.tensor_mul(R(2), R(0), R(0)),
                     waits=[("s_dve", t2)])
            t3 = dve(lambda e, S: e.tensor_sub(R(2), R(1), R(2)),
                     waits=[("s_dve", t3)])
            t3 = dve(lambda e, S: e.tensor_scalar(
                R(2), R(2), 1e-5, None, OP.add), waits=[("s_dve", t3)])
            t3 = dve(lambda e, S: e.reciprocal(R(3), R(2)),
                     waits=[("s_dve", t3)])
            t4 = act(lambda e, S: e.activation(R(3), R(3), AF.Sqrt),
                     waits=[("s_dve", t3)])
            t5 = dve(lambda e, S: e.tensor_mul(R(4), R(0), R(3)),
                     waits=[("s_act", t4)])
            return {"xf_t": xf_t, "tag": tag, "t4": t4, "t5": t5, "t1": t1,
                    "R": R, "xf_act": st[(tag, "c2", NB - 1)]}

        def plan_ln_apply(ctx, h_t, ca1col, sacol, h_guard):
            xf_t, tag, R = ctx["xf_t"], ctx["tag"], ctx["R"]
            ibr, ps_br, gbr = rot_ps.next()
            tba = mm(ps_br[:], onesrow128[:], R(3), True, True,
                     waits=[("s_act", ctx["t4"]), ("s_dve", ctx["t1"]), gbr],
                     inc="s_pe")
            ibm, ps_bm, gbm = rot_ps.next()
            tbb = mm(ps_bm[:], onesrow128[:], R(4), True, True,
                     waits=[("s_dve", ctx["t5"]), gbm], inc="s_pe")
            th = None
            tu_last = None
            for k in range(NB):
                tu = dve(lambda e, S, k=k, ps_br=ps_br: e.tensor_mul(
                    tstage[:], xf_t[:, 512 * k:512 * (k + 1)], ps_br[:]),
                    waits=[("s_pe", tba)])
                tu_last = tu
                tn = dve(lambda e, S, ps_bm=ps_bm: e.tensor_sub(
                    tstage2[:], tstage[:], ps_bm[:]),
                    waits=[("s_pe", tbb), ("s_dve", tu)])
                th = dve(lambda e, S, k=k: e.tensor_scalar(
                    h_t[:, 512 * k:512 * (k + 1)], tstage2[:],
                    ca1col[:, k:k + 1], sacol[:, k:k + 1], OP.mult, OP.add),
                    waits=[("s_dve", tn), ("s_dve", t_mod)] +
                          ([h_guard] if k == 0 else []))
                st[(tag, "h", k)] = th
            rot_ps.done(ibr, "s_dve", th)
            rot_ps.done(ibm, "s_dve", th)
            return {"h_last": th, "xf_dve": tu_last,
                    "xf_act": ctx["xf_act"]}

        def plan_ln(xf_t, h_t, ca1col, sacol, ready, tag, h_guard):
            ctx = plan_ln_stats(xf_t, tag, ready)
            return plan_ln_apply(ctx, h_t, ca1col, sacol, h_guard)

        def p1_x_ln(s):
            xf_s = xf if s % 2 == 0 else xf2
            prev2 = st.get(("p1ln", s - 2))
            t_xf = None
            for b in range(4):
                ih, half, gx = rot_xstage.next()
                xw = [gx]
                if s == 0 and b == 0:
                    xw.append(("s_act", t_wv))
                if prev2 is not None and b == 0:
                    xw += [("s_dve", prev2["xf_dve"]),
                           ("s_act", prev2["xf_act"]),
                           ("s_xfa" if s % 2 == 0 else "s_xfb",
                            st.get(("xfsp", s - 2)))]
                sx = "s_xa" if half == 0 else "s_xb"
                tld = dma("sync", xstage[:, 1024 * half:1024 * (half + 1)],
                          x_in[512 * s + 128 * b:512 * s + 128 * (b + 1), :],
                          waits=xw, sem=sx)
                tre = None
                for db in range(NB):
                    ipt, pt, gp = rot_pt.next()
                    t = tr(pt[:, 0:128],
                           xstage[:, 1024 * half + 128 * db:
                                  1024 * half + 128 * (db + 1)],
                           ident32[:], waits=[(sx, tld), gp])
                    te = act(lambda e, S, pt=pt, db=db, b=b, xf_s=xf_s:
                             e.activation(
                        xf_s[:, 512 * db + 128 * b:512 * db + 128 * (b + 1)],
                        pt[:, 0:128], AF.Copy), waits=[("s_pe", t)])
                    rot_pt.done(ipt, "s_act", te)
                    tre = t
                    t_xf = te
                rot_xstage.done(ih, "s_pe", tre)
            ctx = plan_ln_stats(xf_s, ("l1", s), [("s_act", t_xf)])
            st[("p1ctx", s)] = ctx
            st[("xfsp", s)] = dma("gpsimd", xf_sp[s], xf_s[:],
                                  waits=[("s_act", t_xf)],
                                  sem="s_xfa" if s % 2 == 0 else "s_xfb")

        def p1_apply(s):
            lnres = plan_ln_apply(st[("p1ctx", s)], hseg, ca1_c, sa_c,
                                  h_guard=("s_pe",
                                           st.get(("vmm_last", s - 1))))
            st[("p1ln", s)] = lnres

        if stage >= 1:
            p1_x_ln(0)
            p1_apply(0)
        for s in range(NSEG if stage >= 1 else 0):
            lnres = st[("p1ln", s)]

            for tag, wT_, dest, spill in (("q", wqT, qseg, q_sp),
                                          ("k", wkT, kseg, k_sp)):
                tev = None
                spsem = "s_qsp" if tag == "q" else "s_ksp"
                for ib in range(NB):
                    ips, ps, gp = rot_ps.next()
                    w0 = [gp, (spsem, st.get((tag + "sp", s - 1)))]
                    t = None
                    for db in range(NB):
                        t = mm(ps[:],
                               wT_[:, 1024 * db + 128 * ib:
                                   1024 * db + 128 * (ib + 1)],
                               hseg[:, 512 * db:512 * (db + 1)],
                               db == 0, db == NB - 1,
                               waits=[("s_dve", st[(("l1", s), "h", db)])] +
                                     (w0 if db == 0 else []),
                               inc="s_pe" if db == NB - 1 else None)
                    tev = dve(lambda e, S, ps=ps, ib=ib, dest=dest:
                              e.tensor_copy(
                                  dest[:, 512 * ib:512 * (ib + 1)], ps[:]),
                              waits=[("s_pe", t)])
                    rot_ps.done(ips, "s_dve", tev)
                    st[(tag + "ev", s, ib)] = tev
                st[(tag + "sp", s)] = dma("gpsimd", spill[s], dest[:],
                                          waits=[("s_dve", tev)], sem=spsem)
            tvl = None
            for b in range(4):
                for eh in range(2):
                    ips, ps, gp = rot_ps.next()
                    t = None
                    for db in range(NB):
                        t = mm(ps[:],
                               hseg[:, 512 * db + 128 * b:
                                    512 * db + 128 * (b + 1)],
                               wvT[:, 1024 * db + 512 * eh:
                                   1024 * db + 512 * (eh + 1)],
                               db == 0, db == NB - 1,
                               waits=[("s_dve", st[(("l1", s), "h", NB - 1)])]
                                     + ([gp, ("s_vsp", st.get(("vsp", s - 1)))]
                                        if db == 0 else []),
                               inc="s_pe" if db == NB - 1 else None)
                    tev = dve(lambda e, S, ps=ps, b=b, eh=eh: e.tensor_copy(
                        vseg[:, 1024 * b + 512 * eh:1024 * b + 512 * (eh + 1)],
                        ps[:]), waits=[("s_pe", t)])
                    rot_ps.done(ips, "s_dve", tev)
                    tvl = tev
            st[("vmm_last", s)] = t
            st[("vev_last", s)] = tvl
            st[("vsp", s)] = dma("gpsimd", v_sp[s], vseg[:],
                                 waits=[("s_dve", tvl)], sem="s_vsp")

            if s + 1 < NSEG:
                p1_x_ln(s + 1)

            tkh = None
            for b in range(4):
                for ib in range(NB):
                    ipk, pk, gp = rot_pk.next()
                    t = tr(pk[:],
                           kseg[:, 512 * ib + 128 * b:512 * ib + 128 * (b + 1)],
                           identbf[:],
                           waits=[("s_dve", st[("kev", s, ib)]), gp])
                    tt1 = dve(lambda e, S, pk=pk, b=b, ib=ib: e.tensor_scalar(
                        ktil[:, 1024 * b + 128 * ib:1024 * b + 128 * (ib + 1)],
                        pk[:], dcol[:, KH0 + b:KH0 + b + 1], None, OP.mult),
                        waits=[("s_pe", t), ("s_act", t_kh),
                               ("s_pe", st.get(("gmm_last", s - 1)))])
                    tt2 = dve(lambda e, S, pk=pk, b=b, ib=ib, s=s:
                              e.tensor_scalar(
                        khat[:, 1024 * b + 128 * ib:1024 * b + 128 * (ib + 1)],
                        pk[:], dcol[:, KH0 + 4 * s + b:KH0 + 4 * s + b + 1],
                        None, OP.mult),
                        waits=[("s_dve", tt1),
                               ("s_pe", st.get(("ghmm_last", s - 1)))])
                    rot_pk.done(ipk, "s_dve", tt2)
                    tkh = tt2
            st[("kh_done", s)] = tkh

            if s >= 1:
                tgm = None
                for ib in range(NB):
                    slots = [rot_ps.next(), rot_ps.next()]
                    tstop = []
                    for eh in range(2):
                        ipg, pg, gp = slots[eh]
                        t = None
                        for b in range(4):
                            t = mm(pg,
                                   ktil[:, 1024 * b + 128 * ib:
                                        1024 * b + 128 * (ib + 1)],
                                   vseg[:, 1024 * b + 512 * eh:
                                        1024 * b + 512 * (eh + 1)],
                                   b == 0, b == 3,
                                   waits=[("s_dve", tkh),
                                          ("s_dve", st[("vev_last", s)])] +
                                         ([gp] if b == 0 else []),
                                   inc="s_pe" if b == 3 else None)
                        tstop.append(t)
                    tgm = t
                    ig, gsl, gg = rot_gstage.next()
                    tev = None
                    for eh in range(2):
                        tev = dve(lambda e, S, pg=slots[eh][1], gsl=gsl, eh=eh:
                                  e.tensor_copy(
                            gstage[:, 1024 * gsl + 512 * eh:
                                   1024 * gsl + 512 * (eh + 1)], pg),
                            waits=[("s_pe", tstop[eh]), gg])
                        rot_ps.done(slots[eh][0], "s_dve", tev)
                    gsem = "s_gsp0" if gsl == 0 else "s_gsp1"
                    tsp = dma("gpsimd", g_sp[s][:, 1024 * ib:1024 * (ib + 1)],
                              gstage[:, 1024 * gsl:1024 * (gsl + 1)],
                              waits=[("s_dve", tev)], sem=gsem)
                    rot_gstage.done(ig, gsem, tsp)
                    st[("gsp", s, ib)] = tsp
                st[("gmm_last", s)] = tgm
            tghm = None
            for ib in range(NB):
                slots = [rot_ps.next(), rot_ps.next()]
                tstop = []
                for eh in range(2):
                    ipg, pg, gp = slots[eh]
                    t = None
                    for b in range(4):
                        t = mm(pg,
                               khat[:, 1024 * b + 128 * ib:
                                    1024 * b + 128 * (ib + 1)],
                               vseg[:, 1024 * b + 512 * eh:
                                    1024 * b + 512 * (eh + 1)],
                               b == 0, b == 3,
                               waits=[("s_dve", tkh),
                                      ("s_dve", st[("vev_last", s)])] +
                                     ([gp] if b == 0 else []),
                               inc="s_pe" if b == 3 else None)
                    tstop.append(t)
                tghm = t
                tg = None
                for eh in range(2):
                    sl = ghat[:, 1024 * ib + 512 * eh:
                              1024 * ib + 512 * (eh + 1)]
                    if s == 0:
                        tg = dve(lambda e, S, pg=slots[eh][1], sl=sl:
                                 e.tensor_copy(sl, pg),
                                 waits=[("s_pe", tstop[eh])])
                    else:
                        tg = dve(lambda e, S, pg=slots[eh][1], sl=sl:
                                 e.tensor_add(sl, sl, pg),
                                 waits=[("s_pe", tstop[eh]),
                                        ("s_dve", st.get(("ghacc", s - 1, ib)))])
                    rot_ps.done(slots[eh][0], "s_dve", tg)
                st[("ghacc", s, ib)] = tg
            st[("ghmm_last", s)] = tghm
            if s + 1 < NSEG:
                p1_apply(s + 1)

        t_p1_dve = (st[("ghacc", NSEG - 1, NB - 1)] if stage >= 1 else None)
        if stage >= 1:
            n_dbg += dump("ghat", ghat[:], [("s_dve", t_p1_dve)])

        t_agd = None
        for ib in range(NB if stage >= 2 else 0):
            t_agd = dma("gpsimd", ag_in[ib],
                        ghat[:, 1024 * ib:1024 * (ib + 1)],
                        waits=[("s_dve", st[("ghacc", NSEG - 1, ib)])],
                        sem="s_agin")
        t_cc = (P.emit("gpsimd",
                       lambda e, S: e.collective_compute(
                           "AllGather", OP.bypass,
                           replica_groups=[list(range(NCORE))],
                           ins=[ag_in], outs=[ag_out]),
                       waits=[full("s_agin")], inc="s_cc")
                if stage >= 2 else None)

        tlast = None
        for ib in range(NB if stage >= 2 else 0):
            ih, half, gx = rot_xstage.next()
            sx = "s_xa" if half == 0 else "s_xb"
            tld = dma("sync", xstage[:, 1024 * half:1024 * (half + 1)],
                      wo_in[128 * ib:128 * (ib + 1), :],
                      waits=[gx, ("s_dve", t_p1_dve)], sem=sx)
            tre = None
            for db in range(NB):
                ipt, pt, gp = rot_pt.next()
                t = tr(pt[:, 0:128],
                       xstage[:, 1024 * half + 128 * db:
                              1024 * half + 128 * (db + 1)],
                       ident32[:], waits=[(sx, tld), gp])
                te = act(lambda e, S, pt=pt, db=db, ib=ib: e.activation(
                    woT[:, 1024 * db + 128 * ib:1024 * db + 128 * (ib + 1)],
                    pt[:, 0:128], AF.Copy), waits=[("s_pe", t)])
                rot_pt.done(ipt, "s_act", te)
                tre = t
                tlast = te
            rot_xstage.done(ih, "s_pe", tre)
        t_wo = tlast

        for s in (range(NSEG - 1, -1, -1) if stage >= 2 else []):
            nxt = s + 1
            if s == NSEG - 1:
                t_qld = t_kld = None
                W_Q = ("s_dve", st[("qev", NSEG - 1, NB - 1)])
                W_K = ("s_dve", st[("kev", NSEG - 1, NB - 1)])
            else:
                t_qld = dma("sync", qseg[:], q_sp[s],
                            waits=[full("s_qsp"),
                                   ("s_pe", st.get(("p2score", nxt))),
                                   ("s_dve", st.get(("p2qtt", nxt)))],
                            sem="s_ql")
                t_kld = dma("sync", kseg[:], k_sp[s],
                            waits=[full("s_ksp"),
                                   ("s_pe", st.get(("p2score", nxt))),
                                   ("s_dve", t_p1_dve)], sem="s_kl")
                W_Q = ("s_ql", t_qld)
                W_K = ("s_kl", t_kld)
            if s == NSEG - 2:
                for ib in range(NB):
                    dma("sync", nbuf[:, 1024 * ib:1024 * (ib + 1)],
                        g_sp[nxt][:, 1024 * ib:1024 * (ib + 1)],
                        waits=[full("s_gsp0"), full("s_gsp1"),
                               ("s_dve", t_p1_dve)], sem="s_nb")
                nw = full("s_nb")
            elif s < NSEG - 2:
                tnu = None
                for ib in range(NB):
                    ig, gsl, gg = rot_nstage.next()
                    sg = ("s_ga", "s_gb", "s_gc", "s_gd")[gsl]
                    tgl = dma("sync",
                              gstg2[:, 1024 * gsl:1024 * (gsl + 1)],
                              g_sp[nxt][:, 1024 * ib:1024 * (ib + 1)],
                              waits=[full("s_gsp0"), full("s_gsp1"), gg,
                                     ("s_act", t_wo)],
                              sem=sg)
                    tnu = dve(lambda e, S, ib=ib, gsl=gsl:
                              e.scalar_tensor_tensor(
                                  nbuf[:, 1024 * ib:1024 * (ib + 1)],
                                  nbuf[:, 1024 * ib:1024 * (ib + 1)],
                                  alpha_bf[:, APW:APW + 1],
                                  gstg2[:, 1024 * gsl:1024 * (gsl + 1)],
                                  OP.mult, OP.add),
                              waits=[(sg, tgl), W_APW,
                                     ("s_pe", st.get(("p2ret", nxt)))])
                    rot_nstage.done(ig, "s_dve", tnu)
                st[("nscan", s)] = tnu
                nw = ("s_dve", tnu)
            else:
                nw = None

            if s == NSEG - 1:
                W_V = ("s_dve", st[("vev_last", NSEG - 1)])
            else:
                t_vld = dma("sync", vseg[:], v_sp[s],
                            waits=[full("s_vsp"),
                                   ("s_pe", st.get(("p2ret", nxt)))],
                            sem="s_vl")
                W_V = ("s_vl", t_vld)

            t_scev = {}
            tsc_mm = None
            for b in range(4):
                ips, ps, gp = rot_ps.next()
                t = None
                for ib in range(NB):
                    t = mm(ps[:],
                           kseg[:, 512 * ib + 128 * b:512 * ib + 128 * (b + 1)],
                           qseg[:, 512 * ib:512 * (ib + 1)],
                           ib == 0, ib == NB - 1,
                           waits=[W_K, W_Q] +
                                 ([gp] if ib == 0 else []),
                           inc="s_pe" if ib == NB - 1 else None)
                tsc_mm = t
                tev = dve(lambda e, S, ps=ps, b=b: e.tensor_mul(
                    scT[:, 512 * b:512 * (b + 1)], ps[:],
                    dmat[:, 512 * b:512 * (b + 1)]),
                    waits=[("s_pe", t), ("s_dve", t_dmat),
                           ("s_pe", st.get(("p2ret", nxt))),
                           ("s_dve", t_p1_dve)])
                rot_ps.done(ips, "s_dve", tev)
                t_scev[b] = tev
            st[("p2score", s)] = tsc_mm

            t_qt = None
            for ib in range(NB):
                t_qt = dve(lambda e, S, ib=ib: e.tensor_mul(
                    qtil[:, 512 * ib:512 * (ib + 1)],
                    qseg[:, 512 * ib:512 * (ib + 1)], bqtil[:]),
                    waits=[W_Q, ("s_dve", t_bqtil),
                           ("s_dve", t_p1_dve),
                           ("s_pe", st.get(("p2ret", nxt))),
                           ("s_qtsp", st.get(("qtsp", nxt)))])
            st[("p2qtt", s)] = t_qt
            st[("qtsp", s)] = dma("gpsimd", qt_sp[s], qtil[:],
                                  waits=[("s_dve", t_qt)], sem="s_qtsp")

            t_ret = None
            tevl = None
            for j in range(NB):
                ips, ps, gp = rot_ps.next()
                t = None
                for b in range(4):
                    t = mm(ps[:],
                           vseg[:, 1024 * b + 128 * j:1024 * b + 128 * (j + 1)],
                           scT[:, 512 * b:512 * (b + 1)],
                           b == 0, (b == 3 and s == NSEG - 1),
                           waits=[W_V, ("s_dve", t_scev[b])] +
                                 ([gp] if b == 0 else []),
                           inc="s_pe" if (b == 3 and s == NSEG - 1) else None)
                if s < NSEG - 1:
                    for ib in range(NB):
                        t = mm(ps[:],
                               nbuf[:, 1024 * ib + 128 * j:
                                    1024 * ib + 128 * (j + 1)],
                               qtil[:, 512 * ib:512 * (ib + 1)],
                               False, ib == NB - 1,
                               waits=[nw, ("s_dve", t_qt)],
                               inc="s_pe" if ib == NB - 1 else None)
                t_ret = t
                tev = dve(lambda e, S, ps=ps, j=j: e.tensor_copy(
                    retrev[:, 512 * j:512 * (j + 1)], ps[:]),
                    waits=[("s_pe", t),
                           ("s_pe", st.get(("p2woE", nxt)))])
                rot_ps.done(ips, "s_dve", tev)
                st[("rev2a", s, j)] = tev
            st[("p2ret", s)] = t_ret
            tevl = None
            t_woE = None
            for o in range(NB):
                ips, ps, gp = rot_ps.next()
                t = None
                for eb in range(NB):
                    t = mm(ps[:],
                           woT[:, 1024 * eb + 128 * o:1024 * eb + 128 * (o + 1)],
                           retrev[:, 512 * eb:512 * (eb + 1)],
                           eb == 0, eb == NB - 1,
                           waits=[("s_dve", st[("rev2a", s, eb)]),
                                  ("s_act", t_wo)] + ([gp] if eb == 0 else []),
                           inc="s_pe" if eb == NB - 1 else None)
                t_woE = t
                tev = dve(lambda e, S, ps=ps, o=o: e.tensor_copy(
                    oebuf[:, 512 * o:512 * (o + 1)], ps[:]),
                    waits=[("s_pe", t), ("s_rsp", st.get(("rsp", nxt))),
                           ("s_xfa", st.get(("xfsp", 2)))])
                rot_ps.done(ips, "s_dve", tev)
                tevl = tev
            st[("p2woE", s)] = t_woE
            st[("rsp", s)] = dma("gpsimd", retr_sp[s], oebuf[:],
                                 waits=[("s_dve", tevl)], sem="s_rsp")

        t_sc = None
        pair_guard = {}
        for p in range(4 if stage >= 2 else 0):
            tacc0 = tacc1 = None
            for j in range(NCORE):
                sl = j % 4
                sag = ("s_aga", "s_agb", "s_age", "s_agf")[sl]
                tld = dma("gpsimd", agstage2[:, 2048 * sl:2048 * sl + 1024],
                          ag_out[NB * j + 2 * p],
                          waits=[("s_cc", t_cc), full("s_dbg"),
                                 ("s_dve", pair_guard.get(sl))],
                          sem=sag)
                tld = dma("gpsimd", agstage2[:, 2048 * sl + 1024:
                                             2048 * (sl + 1)],
                          ag_out[NB * j + 2 * p + 1],
                          waits=[], sem=sag)
                for h in range(2):
                    stg = agstage2[:, 2048 * sl + 1024 * h:
                                   2048 * sl + 1024 * (h + 1)]
                    ib = 2 * p + h
                    dst = sc_sb[:, 1024 * ib:1024 * (ib + 1)]
                    if j == 0:
                        tacc = dve(lambda e, S, stg=stg, dst=dst:
                                   e.tensor_scalar(dst, stg,
                                                   wcomb_c[:, 0:1], None,
                                                   OP.mult),
                                   waits=[(sag, tld), ("s_dve", t_wcomb),
                                          ("s_dve", t_p1_dve)])
                    else:
                        tacc = dve(lambda e, S, stg=stg, dst=dst, j=j:
                                   e.scalar_tensor_tensor(
                                       dst, stg, wcomb_c[:, j:j + 1], dst,
                                       OP.mult, OP.add),
                                   waits=[(sag, tld),
                                          ("s_dve", tacc0 if h == 0
                                           else tacc1)])
                    if h == 0:
                        tacc0 = tacc
                    else:
                        tacc1 = tacc
                pair_guard[sl] = tacc1
            st[("scdone", 2 * p)] = tacc0
            st[("scdone", 2 * p + 1)] = tacc1
            t_sc = tacc1
        if stage >= 2:
            n_dbg += dump("sc", sc_sb[:], [("s_dve", t_sc)])

        def p2b_x(s):
            xf_s = xf if s % 2 == 0 else xf2
            prev2 = st.get(("p2ln", s + 2))
            xw = [("s_xfa" if s % 2 == 0 else "s_xfb",
                   st[("xfsp", s)])]
            if s == NSEG - 1:
                xw.append(("s_dve", st.get(("nscan", 0))))
            if prev2 is not None:
                xw += [("s_dve", prev2["xf_dve"]),
                       ("s_act", prev2["xf_act"]),
                       ("s_x2sp" if s % 2 == 0 else "s_x2b2",
                        st.get(("x2sp", s + 2)))]
            if s >= NSEG - 2:
                p1l = st.get(("p1ln", 2 + (s % 2)))
                if p1l is not None:
                    xw += [("s_dve", p1l["xf_dve"]),
                           ("s_act", p1l["xf_act"])]
            if s % 2 == 0:
                xw.append(full("s_rsp"))
            t_xl = dma("sync", xf_s[:], xf_sp[s], waits=xw,
                       sem="s_xla" if s % 2 == 0 else "s_xlb")
            st[("p2bx", s)] = (xf_s, None, t_xl)

        for s in (range(NSEG - 1, -1, -1) if stage >= 2 else []):
            first = (s == NSEG - 1)
            nxt = s + 1
            qtil_s = qtil if s % 2 == 0 else qtil2
            rstr_s = rstr if s % 2 == 0 else rstr2
            sql = "s_ql" if s % 2 == 0 else "s_ql2"
            srl = "s_rl" if s % 2 == 0 else "s_rl2"
            t_qtl = dma("sync", qtil_s[:], qt_sp[s],
                        waits=[full("s_qtsp"),
                               ("s_pe", st.get(("p2ret", 0))),
                               ("s_pe", st.get(("p2score", 0))),
                               ("s_pe", st.get(("p2bS", s + 2)))],
                        sem=sql)
            t_rld = dma("sync", rstr_s[:], retr_sp[s],
                        waits=[full("s_rsp"),
                               ("s_dve", st.get(("nscan", 0))),
                               ("s_dve", st.get(("bev", s + 2)))],
                        sem=srl)
            if s == NSEG - 1:
                p2b_x(s)
            xf_s, t_xf, t_xl = st[("p2bx", s)]
            W_XF = ("s_xla" if s % 2 == 0 else "s_xlb", t_xl)

            tbs = None
            tbev = None
            for j in range(NB):
                ips, ps, gp = rot_ps.next()
                t = None
                for ib in range(NB):
                    t = mm(ps[:],
                           sc_sb[:, 1024 * ib + 128 * j:
                                 1024 * ib + 128 * (j + 1)],
                           qtil_s[:, 512 * ib:512 * (ib + 1)],
                           ib == 0, ib == NB - 1,
                           waits=[("s_dve", st[("scdone", ib)]),
                                  (sql, t_qtl)] +
                                 ([gp] if ib == 0 else []),
                           inc="s_pe" if ib == NB - 1 else None)
                tbs = t
                tev = dve(lambda e, S, ps=ps, j=j: e.tensor_copy(
                    retrev[:, 512 * j:512 * (j + 1)], ps[:]),
                    waits=[("s_pe", t),
                           ("s_pe", st.get(("p2wo", nxt))),
                           ("s_pe", st.get(("p2woE", 0)))])
                rot_ps.done(ips, "s_dve", tev)
                st[("rev", s, j)] = tev
                tbev = tev
            st[("p2bS", s)] = tbs
            st[("bev", s)] = tbev

            t1w = {}
            for o in range(NB):
                t1w[o] = dve(lambda e, S, o=o, xf_s=xf_s, rstr_s=rstr_s:
                             e.scalar_tensor_tensor(
                    xf_s[:, 512 * o:512 * (o + 1)],
                    rstr_s[:, 512 * o:512 * (o + 1)],
                    gaeff_c[:, o:o + 1], xf_s[:, 512 * o:512 * (o + 1)],
                    OP.mult, OP.add),
                    waits=[(srl, t_rld), ("s_dve", t_mod), W_XF,
                           ("s_dve", t_gaefa)])

            if s > 0:
                p2b_x(s - 1)

            gcol = (gaeff_c if first else
                    gaefa[:, 8 * (NSEG - 2 - s):8 * (NSEG - 1 - s)])
            t_wo_mm = None
            tx2 = None
            for o in range(NB):
                ips, ps, gp = rot_ps.next()
                t = None
                for eb in range(NB):
                    t = mm(ps[:],
                           woT[:, 1024 * eb + 128 * o:1024 * eb + 128 * (o + 1)],
                           retrev[:, 512 * eb:512 * (eb + 1)],
                           eb == 0, eb == NB - 1,
                           waits=[("s_dve", st[("rev", s, eb)]),
                                  ("s_act", t_wo)] + ([gp] if eb == 0 else []),
                           inc="s_pe" if eb == NB - 1 else None)
                t_wo_mm = t
                tx2 = dve(lambda e, S, ps=ps, o=o, xf_s=xf_s, gcol=gcol:
                          e.scalar_tensor_tensor(
                    xf_s[:, 512 * o:512 * (o + 1)], ps[:],
                    gcol[:, o:o + 1], xf_s[:, 512 * o:512 * (o + 1)],
                    OP.mult, OP.add),
                    waits=[("s_pe", t), ("s_dve", t1w[o])])
                rot_ps.done(ips, "s_dve", tx2)
            st[("p2wo", s)] = t_wo_mm
            st[("x2w", s)] = tx2
            st[("x2sp", s)] = dma("gpsimd", x2_sp[s], xf_s[:],
                                  waits=[("s_dve", tx2)],
                                  sem="s_x2sp" if s % 2 == 0 else "s_x2b2")

            lnres2 = plan_ln(xf_s, hseg, cb1_c, sb_c,
                             [("s_dve", tx2)], ("l2", s),
                             h_guard=("s_h2sp", st.get(("h2sp", nxt)))
                             if not first else
                             ("s_pe", st.get(("vmm_last", NSEG - 1))))
            st[("p2ln", s)] = lnres2
            st[("h2sp", s)] = dma("gpsimd", h2_sp[s], hseg[:],
                                  waits=[("s_dve", lnres2["h_last"])],
                                  sem="s_h2sp")

        W_P2 = ([full("s_x2sp"), full("s_x2b2"), full("s_h2sp"),
                 full("s_dbg")] if stage >= 3 else [])
        W_W1G = []
        if stage >= 3:
            for wq in range(4):
                sw = ("s_w1", "s_w1b", "s_w1c", "s_w1d")[wq]
                for db in range(NB):
                    dma("gpsimd",
                        w1sb[:, FW * db + 1024 * wq:FW * db + 1024 * (wq + 1)],
                        w1bf[128 * db:128 * (db + 1),
                             1024 * wq:1024 * (wq + 1)],
                        waits=W_P2 + [W_WPREP1], sem=sw)
                W_W1G.append(full(sw))
        for wb in range(NWB if stage >= 3 else 0):
            dma("gpsimd",
                w2sb[:, 1024 * wb:1024 * (wb + 1)],
                w2bf[128 * wb:128 * (wb + 1), :],
                waits=W_P2 + [W_WPREP2],
                sem="s_w2" if wb % 2 == 0 else "s_w2b")
        W_W2 = full("s_w2")
        W_W2B = full("s_w2b")

        n_out = 0
        for s in range(NSEG if stage >= 3 else 0):
            t_h2l = dma("sync", h2c[:], h2_sp[s],
                        waits=W_P2 + [("s_pe", st.get(("ffnmid", s - 1)))],
                        sem="s_hl")
            t_x2l = dma("sync", x2c[:], x2_sp[s],
                        waits=W_P2 + [("s_dve", st.get(("ffnx3", s - 1))),
                                      ("s_pe", st.get(("p3tr", s - 1)))],
                        sem="s_x2l")
            tmid = None
            for wb in range(NWB):
                ips, ps, gp = rot_ps.next()
                t = None
                for db in range(NB):
                    t = mm(ps[:],
                           w1sb[:, FW * db + 128 * wb:FW * db + 128 * (wb + 1)],
                           h2c[:, 512 * db:512 * (db + 1)],
                           db == 0, db == NB - 1,
                           waits=[("s_hl", t_h2l), W_W1G[wb // 8]] +
                                 ([gp] if db == 0 else []),
                           inc="s_pe" if db == NB - 1 else None)
                tmid = t
                te = act(lambda e, S, ps=ps, wb=wb: e.activation(
                    midc[:, 512 * wb:512 * (wb + 1)], ps[:], AF.Relu,
                    bias=b1col[:, wb:wb + 1]),
                    waits=[("s_pe", t), ("s_pe", st.get(("ffnout", s - 1)))])
                rot_ps.done(ips, "s_act", te)
                st[("ffnme", s, wb)] = te
            st[("ffnmid", s)] = tmid
            tout = None
            tx3 = None
            for o in range(NB):
                ipg, po, gp = rot_ps.next()
                t = mm(po, b2row[0:1, 128 * o:128 * (o + 1)],
                       onesrow512bf[:], True, False,
                       waits=[W_W2, W_W2B, W_WPREP2, gp, W_CONST])
                for wb in range(NWB):
                    t = mm(po,
                           w2sb[:, 1024 * wb + 128 * o:1024 * wb + 128 * (o + 1)],
                           midc[:, 512 * wb:512 * (wb + 1)],
                           False, wb == NWB - 1,
                           waits=[("s_act", st[("ffnme", s, wb)]), W_W2],
                           inc="s_pe" if wb == NWB - 1 else None)
                tout = t
                tx3 = dve(lambda e, S, po=po, o=o: e.scalar_tensor_tensor(
                    x2c[:, 512 * o:512 * (o + 1)], po,
                    gb_c[:, o:o + 1], x2c[:, 512 * o:512 * (o + 1)],
                    OP.mult, OP.add),
                    waits=[("s_pe", t), ("s_x2l", t_x2l), ("s_dve", t_mod)])
                rot_ps.done(ipg, "s_dve", tx3)
                st[("x3", s, o)] = tx3
            st[("ffnout", s)] = tout
            st[("ffnx3", s)] = tx3
            tp3 = None
            for b in range(4):
                io, osl, go = rot_ostage.next()
                tev = None
                for o in range(NB):
                    ipt, pt, gp = rot_pt.next()
                    t = tr(pt[:, 0:128],
                           x2c[:, 512 * o + 128 * b:512 * o + 128 * (b + 1)],
                           ident32[:],
                           waits=[("s_dve", st[("x3", s, o)]), gp] +
                                 ([go] if o == 0 else []))
                    tp3 = t
                    tev = act(lambda e, S, pt=pt, o=o, osl=osl: e.activation(
                        ostage[:, 1024 * osl + 128 * o:
                               1024 * osl + 128 * (o + 1)],
                        pt[:, 0:128], AF.Copy), waits=[("s_pe", t)])
                    rot_pt.done(ipt, "s_act", tev)
                so = "s_oa" if osl == 0 else "s_ob"
                tod = dma("sync",
                          out_dram[512 * s + 128 * b:512 * s + 128 * (b + 1), :],
                          ostage[:, 1024 * osl:1024 * (osl + 1)],
                          waits=[("s_act", tev)], sem=so)
                rot_ostage.done(io, so, tod)
                n_out += 1
            st[("p3tr", s)] = tp3

        P.emit("sync", lambda e, S: e.nop(),
               waits=[full("s_oa"), full("s_ob"), full("s_dbg")])

        with nc.Block() as block:
            def mk(ename):
                def body(eng):
                    for fn, waits, inc, amt in P.seq[ename]:
                        for semn, val in waits:
                            eng.wait_ge(sems[semn], val)
                        inst = fn(eng, sems)
                        if inc is not None:
                            inst.then_inc(sems[inc], amt)
                return body
            for ename in ENGINES:
                if P.seq[ename]:
                    getattr(block, ename)(mk(ename))

    return nc


_CACHE = {}


def _get_nc(debug=(), stage=3):
    key = (tuple((n, tuple(s), d) for n, s, d in debug), stage)
    if key not in _CACHE:
        _CACHE[key] = build_program(debug, stage)
    return _CACHE[key]


def _consts():
    bf = ml_dtypes.bfloat16
    return {
        "ident_f32": np.eye(128, dtype=np.float32),
        "ident_bf16": np.eye(128, dtype=np.float32).astype(bf),
        "ones_row128": np.ones((1, 128), np.float32),
        "ones_row512_bf": np.ones((1, 512), np.float32).astype(bf),
        "ones_col_bf": np.ones((128, 1), np.float32).astype(bf),
        "iota_col": np.arange(128, dtype=np.float32).reshape(128, 1),
        "iota_row128": np.arange(128, dtype=np.float32).reshape(1, 128),
        "iota_row512": np.arange(512, dtype=np.float32).reshape(1, 512),
        "expo_t": np.concatenate(
            [(128 * b + np.arange(128)[:, None]
              - np.arange(512)[None, :] - 1.0).astype(np.float32)
             for b in range(4)], axis=1),
    }


def kernel(L0_attn_x, L0_attn_Wq, L0_attn_Wk, L0_attn_Wv, L0_attn_Wo,
           L0_attn_decay_logit, L0_attn_out_scale,
           L0_ffn_w1, L0_ffn_b1, L0_ffn_w2, L0_ffn_b2, L0_c, L0_wmod,
           _debug=(), _stage=3):
    nc = _get_nc(_debug, _stage)
    x2d = np.ascontiguousarray(np.asarray(L0_attn_x, np.float32)
                               .reshape(T, D))
    common = dict(
        wq=np.ascontiguousarray(np.asarray(L0_attn_Wq, np.float32)),
        wk=np.ascontiguousarray(np.asarray(L0_attn_Wk, np.float32)),
        wv=np.ascontiguousarray(np.asarray(L0_attn_Wv, np.float32)),
        wo=np.ascontiguousarray(np.asarray(L0_attn_Wo, np.float32)),
        w1=np.ascontiguousarray(np.asarray(L0_ffn_w1, np.float32)),
        w2=np.ascontiguousarray(np.asarray(L0_ffn_w2, np.float32)),
        b1col=np.ascontiguousarray(
            np.asarray(L0_ffn_b1, np.float32).reshape(NWB, 128).T),
        b2row=np.ascontiguousarray(
            np.asarray(L0_ffn_b2, np.float32).reshape(1, D)),
        c_cols=np.ascontiguousarray(
            np.asarray(L0_c, np.float32).reshape(NB, 128).T),
        wmod=np.ascontiguousarray(np.asarray(L0_wmod, np.float32)),
        decay_logit=np.asarray(L0_attn_decay_logit,
                               np.float32).reshape(1, 1),
        out_scale=np.asarray(L0_attn_out_scale, np.float32).reshape(1, 1),
        **_consts(),
    )
    in_maps = []
    for c in range(NCORE):
        ecore = np.zeros((1, NCORE), np.float32)
        emask = np.zeros((1, NCORE), np.float32)
        for j in range(NCORE):
            if j > c:
                ecore[0, j] = float(TC * (j - c - 1))
                emask[0, j] = 1.0
        m = dict(common)
        m["x"] = np.ascontiguousarray(x2d[TC * c:TC * (c + 1)])
        m["ecore"] = ecore
        m["emask"] = emask
        in_maps.append(m)
    res = run_bass_kernel_spmd(nc, in_maps, core_ids=list(range(NCORE)))
    out = np.concatenate([res.results[c]["out"] for c in range(NCORE)],
                         axis=0)
    if _debug:
        return out.reshape(1, T, D), list(res.results)
    return out.reshape(1, T, D)


# version 16
# speedup vs baseline: 1.0641x; 1.0429x over previous
import numpy as np
import ml_dtypes
from contextlib import ExitStack

import concourse.bass as bass
import concourse.mybir as mybir
from concourse.bass_utils import run_bass_kernel_spmd

F32 = mybir.dt.float32
BF16 = mybir.dt.bfloat16
AF = mybir.ActivationFunctionType
OP = mybir.AluOpType

T = 16384
D = 1024
FW = 4096
NCORE = 8
TC = T // NCORE
SEG = 512
NSEG = TC // SEG
NB = D // 128
NWB = FW // 128
LOG_INV_SQRT_D = float(np.log(1.0 / np.sqrt(D)))

ENGINES = ("sync", "tensor", "vector", "scalar", "gpsimd")


class Plan:

    def __init__(self):
        self.seq = {e: [] for e in ENGINES}
        self.cnt = {}
        self.last_wait = {e: {} for e in ENGINES}

    def emit(self, eng, fn, waits=(), inc=None, amt=1):
        w = []
        lw = self.last_wait[eng]
        for item in waits:
            if item is None:
                continue
            sem, val = item
            if val is None:
                continue
            if val > lw.get(sem, 0):
                lw[sem] = val
                w.append((sem, val))
        t = None
        if inc is not None:
            self.cnt[inc] = self.cnt.get(inc, 0) + amt
            t = self.cnt[inc]
        self.seq[eng].append((fn, tuple(w), inc, amt))
        return t


class Rot:

    def __init__(self, slots):
        self.slots = list(slots)
        self.n = 0
        self.last = {}

    def next(self):
        i = self.n % len(self.slots)
        self.n += 1
        return i, self.slots[i], self.last.get(i)

    def done(self, i, sem, ticket):
        self.last[i] = (sem, ticket)


def build_program(debug=(), stage=3):
    nc = bass.Bass("TRN2", target_bir_lowering=False, debug=False)

    def ein(name, shape, dt=F32):
        return nc.dram_tensor(name, shape, dt, kind="ExternalInput").ap()

    x_in = ein("x", [TC, D])
    wq_in = ein("wq", [D, D])
    wk_in = ein("wk", [D, D])
    wv_in = ein("wv", [D, D])
    wo_in = ein("wo", [D, D])
    w1_in = ein("w1", [D, FW])
    b1_in = ein("b1col", [128, NWB])
    w2_in = ein("w2", [FW, D])
    b2_in = ein("b2row", [1, D])
    c_in = ein("c_cols", [128, NB])
    wmod_in = ein("wmod", [D, 6 * D])
    dlog_in = ein("decay_logit", [1, 1])
    osc_in = ein("out_scale", [1, 1])
    id32_in = ein("ident_f32", [128, 128])
    idbf_in = ein("ident_bf16", [128, 128], BF16)
    onesrow128_in = ein("ones_row128", [1, 128])
    onesrow512bf_in = ein("ones_row512_bf", [1, 512], BF16)
    onescol_bf_in = ein("ones_col_bf", [128, 1], BF16)
    iotacol_in = ein("iota_col", [128, 1])
    iotarow128_in = ein("iota_row128", [1, 128])
    iotarow512_in = ein("iota_row512", [1, 512])
    ecore_in = ein("ecore", [1, NCORE])
    emask_in = ein("emask", [1, NCORE])
    expo_in = ein("expo_t", [128, 4 * SEG])

    out_dram = nc.dram_tensor("out", [TC, D], F32, kind="ExternalOutput").ap()

    def scr(name, shape, dt):
        return nc.dram_tensor(name, shape, dt, kind="Internal").ap()

    q_sp = scr("q_sp", [NSEG, 128, NB * SEG], BF16)
    k_sp = scr("k_sp", [NSEG, 128, NB * SEG], BF16)
    v_sp = scr("v_sp", [NSEG, 128, NSEG * D], BF16)
    g_sp = scr("g_sp", [NSEG, 128, NB * D], BF16)
    x2_sp = scr("x2_sp", [NSEG, 128, NB * SEG], F32)
    retr_sp = scr("retr_sp", [NSEG, 128, NB * SEG], BF16)
    qt_sp = scr("qt_sp", [NSEG, 128, NB * SEG], BF16)
    xf_sp = scr("xf_sp", [NSEG, 128, NB * SEG], F32)
    w1bf = scr("w1bf", [D, FW], BF16)
    w2bf = scr("w2bf", [FW, D], BF16)
    h2_sp = scr("h2_sp", [NSEG, 128, NB * SEG], BF16)
    ag_in = scr("ag_in", [NB, 128, D], BF16)
    ag_out = nc.dram_tensor("ag_out", [NCORE * NB, 128, D], BF16,
                            kind="Internal", addr_space="Shared").ap()

    dbg = {}
    for name, shape, dt in debug:
        dbg[name] = nc.dram_tensor("dbg_" + name, shape, dt,
                                   kind="ExternalOutput").ap()

    P = Plan()
    st = {}

    with ExitStack() as ctx:
        SB_BASE = 18432
        SB_LIMIT = 16384 + 212863
        offmap = {}
        cur = [SB_BASE]

        def sb(name, shape, dt, at=None):
            size = shape[1] * mybir.dt.size(dt)
            if at is None:
                at = cur[0]
                cur[0] = (at + size + 63) & ~63
            offmap[name] = at
            return nc.alloc_sbuf_tensor_at(name, shape, dt, offset=at)

        ident32 = sb("ident32", [128, 128], F32)
        identbf = sb("identbf", [128, 128], BF16)
        onesrow128 = sb("onesrow128", [1, 128], F32)
        onesrow512bf = sb("onesrow512bf", [1, 512], BF16)
        onescolbf = sb("onescolbf", [128, 1], BF16)
        iotacol = sb("iotacol", [128, 1], F32)
        iotarow128 = sb("iotarow128", [1, 128], F32)
        iotarow512 = sb("iotarow512", [1, 512], F32)
        b2row = sb("b2row", [1, D], BF16)
        b1col = sb("b1col", [128, NWB], F32)
        c_cols = sb("c_cols", [128, NB], F32)
        mod_sb = sb("mod_sb", [128, 48], F32)
        derv = sb("derv", [128, 32], F32)
        dcol = sb("dcol", [128, 32], F32)
        rows2 = sb("rows2", [1, 64], F32)
        PERM_END = cur[0]
        rows = sb("rows", [1, 512 * 5], F32)
        dmat = sb("dmat", [128, 4 * SEG], BF16)
        bqtil = sb("bqtil", [128, SEG], F32)
        tstage = sb("tstage", [128, 512], F32)
        tstage2 = sb("tstage2", [128, 512], F32)
        expo_sb = sb("expo_sb", [128, 4 * SEG], F32)

        wqT = sb("wqT", [128, NB * D], BF16)
        wkT = sb("wkT", [128, NB * D], BF16)
        wvT = sb("wvT", [128, NB * D], BF16)
        xstage = sb("xstage", [128, 2 * D], F32)
        xf = sb("xf", [128, NB * SEG], F32)
        hseg = sb("hseg", [128, NB * SEG], BF16)
        sblk = sb("sblk", [128, 4 * 512], BF16)
        qseg = sb("qseg", [128, NB * SEG], BF16)
        kseg = sb("kseg", [128, NB * SEG], BF16)
        vseg = sb("vseg", [128, NSEG * D], BF16)
        ktil = sb("ktil", [128, 4 * D], BF16)
        khat = sb("khat", [128, 4 * D], BF16)
        ghat = sb("ghat", [128, NB * D], BF16)
        gstage = sb("gstage", [128, 2 * D], BF16)
        retrev = sb("retrev", [128, NB * SEG], BF16)
        xf2 = sb("xf2", [128, NB * SEG], F32)
        A_END = cur[0]
        assert A_END <= SB_LIMIT, A_END

        woT = sb("woT", [128, NB * D], BF16, at=offmap["wqT"])
        nbuf = sb("nbuf", [128, NB * D], BF16, at=offmap["wkT"])
        sc_sb = sb("sc_sb", [128, NB * D], BF16, at=offmap["wvT"])
        mbuf = sb("mbuf", [128, NB * D], BF16, at=offmap["ghat"])
        scT = sb("scT", [128, 4 * SEG], BF16, at=offmap["ktil"])
        qtil = sb("qtil", [128, NB * SEG], BF16, at=offmap["khat"])
        agstage = sb("agstage", [128, 2 * D], BF16, at=offmap["gstage"])
        rstr = sb("rstr", [128, NB * SEG], BF16, at=offmap["ktil"])
        gstg2 = sb("gstg2", [128, 4 * D], BF16, at=offmap["xstage"])
        rstr2 = sb("rstr2", [128, NB * SEG], BF16, at=offmap["kseg"])
        nscan_sb = sb("nscan_sb", [128, NB * D], BF16, at=offmap["ghat"])
        qtil2 = sb("qtil2", [128, NB * SEG], BF16, at=offmap["vseg"])
        agstage2 = sb("agstage2", [128, NB * D], BF16, at=offmap["ghat"])

        fcur = [offmap["rows"]]

        def sbf(name, shape, dt):
            size = shape[1] * mybir.dt.size(dt)
            at = fcur[0]
            fcur[0] = (at + size + 63) & ~63
            offmap[name] = at
            return nc.alloc_sbuf_tensor_at(name, shape, dt, offset=at)

        w1sb = sbf("w1sb", [128, NB * FW], BF16)
        w2sb = sbf("w2sb", [128, NWB * D], BF16)
        midc = sbf("midc", [128, NWB * SEG], BF16)
        h2c = sbf("h2c", [128, NB * SEG], BF16)
        x2c = sbf("x2c", [128, NB * SEG], F32)
        ostage = sbf("ostage", [128, 2 * D], F32)
        assert fcur[0] <= SB_LIMIT, fcur[0]

        ps_a = ctx.enter_context(nc.psum_tensor("ps_a", [128, 512], F32))
        ps_b = ctx.enter_context(nc.psum_tensor("ps_b", [128, 512], F32))
        pg_a = ctx.enter_context(nc.psum_tensor("pg_a", [128, 1024], F32))
        pg_b = ctx.enter_context(nc.psum_tensor("pg_b", [128, 1024], F32))
        pk_a = ctx.enter_context(nc.psum_tensor("pk_a", [128, 128], BF16))
        pk_b = ctx.enter_context(nc.psum_tensor("pk_b", [128, 128], BF16))

        sem_names = ("s_c", "s_xa", "s_xb", "s_ql", "s_kl", "s_vl",
                     "s_nb", "s_ga", "s_gb", "s_aga", "s_agb",
                     "s_hl", "s_x2l", "s_oa", "s_ob",
                     "s_qsp", "s_ksp", "s_vsp", "s_gsp0", "s_gsp1",
                     "s_x2sp", "s_h2sp", "s_agin", "s_w1", "s_w2", "s_dbg",
                     "s_rsp", "s_rl", "s_x2b2", "s_qtsp", "s_w1b", "s_w1c", "s_w1d", "s_gc", "s_gd", "s_age", "s_agf", "s_ql2", "s_rl2", "s_wp1", "s_wp2", "s_w2b", "s_xfa", "s_xfb", "s_xla", "s_xlb",
                     "s_pe", "s_dve", "s_act", "s_cc")
        sems = {n: ctx.enter_context(nc.semaphore(n)) for n in sem_names}

        rot_ps = Rot([ps_a[:, 0:512], ps_b[:, 0:512],
                      pg_a[:, 0:512], pg_a[:, 512:1024],
                      pg_b[:, 0:512], pg_b[:, 512:1024]])
        rot_pt = rot_ps
        rot_pk = Rot([pk_a, pk_b])
        rot_xstage = Rot([0, 1])
        rot_nstage = Rot([0, 1, 2, 3])
        rot_gstage = Rot([0, 1])
        rot_ostage = Rot([0, 1])
        rot_sblk1 = Rot([0, 1])
        rot_sblk2 = Rot([2, 3])

        def dma(eng, out_ap, in_ap, waits=(), sem=None):
            assert sem is not None
            return P.emit(eng,
                          lambda e, S, o=out_ap, i=in_ap:
                              e.dma_start(out=o, in_=i),
                          waits, sem, 16)

        def full(sem):
            return (sem, P.cnt.get(sem, 0))

        def mm(out_ap, a_ap, b_ap, start, stop, waits=(), inc=None):
            return P.emit("tensor",
                          lambda e, S, o=out_ap, a=a_ap, b=b_ap, s1=start,
                          s2=stop: e.matmul(o, a, b, start=s1, stop=s2),
                          waits, inc)

        def tr(out_ap, in_ap, ident_ap, waits=(), inc="s_pe"):
            return P.emit("tensor",
                          lambda e, S, o=out_ap, i=in_ap, d=ident_ap:
                              e.transpose(o, i, d),
                          waits, inc)

        def dve(fn, waits=(), inc="s_dve"):
            return P.emit("vector", fn, waits, inc)

        def act(fn, waits=(), inc="s_act"):
            return P.emit("scalar", fn, waits, inc)

        def dump(name, src_ap, waits):
            if name in dbg:
                t = dma("sync", dbg[name], src_ap,
                        waits=list(waits) + [st.get("dump_prev")],
                        sem="s_dbg")
                st["dump_prev"] = ("s_dbg", t)
                return 1
            return 0

        n_dbg = 0

        t_cc0 = dma("sync", c_cols[:], c_in, sem="s_hl")
        for ap_sb, ap_in in ((ident32, id32_in), (identbf, idbf_in),
                             (onesrow128, onesrow128_in),
                             (onesrow512bf, onesrow512bf_in),
                             (onescolbf, onescol_bf_in),
                             (iotacol, iotacol_in),
                             (iotarow128, iotarow128_in),
                             (iotarow512, iotarow512_in),
                             (b1col, b1_in)):
            dma("sync", ap_sb[:], ap_in, sem="s_c")
        dma("sync", rows2[0:1, 0:NCORE], ecore_in, sem="s_c")
        dma("sync", rows2[0:1, 8:8 + NCORE], emask_in, sem="s_c")
        dma("sync", rows2[0:1, 16:17], dlog_in, sem="s_c")
        dma("sync", rows2[0:1, 17:18], osc_in, sem="s_c")
        dma("sync", expo_sb[:], expo_in, sem="s_c")
        t_b2 = dma("gpsimd", b2row[:], b2_in, sem="s_wp2")
        if stage >= 3:
            for ch in range(4):
                dma("gpsimd", w1bf[256 * ch:256 * (ch + 1), :],
                    w1_in[256 * ch:256 * (ch + 1), :], sem="s_wp1")
                dma("gpsimd", w2bf[1024 * ch:1024 * (ch + 1), :],
                    w2_in[1024 * ch:1024 * (ch + 1), :], sem="s_wp2")
        W_WPREP1 = full("s_wp1")
        W_WPREP2 = full("s_wp2")
        W_CONST = full("s_c")

        LND, TMP, APW, OSCC, BIA, ONE, KH0 = 0, 1, 2, 5, 6, 7, 8
        t_bia = dve(lambda e, S: e.memset(dcol[:, BIA:BIA + 1],
                                          LOG_INV_SQRT_D))
        t_one = dve(lambda e, S: e.memset(dcol[:, ONE:ONE + 1], 1.0))
        alpha_bf = sb("alpha_bf", [128, 32], BF16)
        gaefa = sb("gaefa", [128, 24], F32)

        i0, pt, g = rot_pt.next()
        t = mm(pt[:, 0:1], onesrow128[:], rows2[0:1, 16:17], True, False,
               waits=[W_CONST, g])
        t = mm(pt[:, 1:2], onesrow128[:], rows2[0:1, 17:18], False, True,
               waits=[], inc="s_pe")
        t_sg = act(lambda e, S, pt=pt: e.activation(
            dcol[:, TMP:TMP + 1], pt[:, 0:1], AF.Sigmoid),
            waits=[("s_pe", t)])
        t_oc = dve(lambda e, S, pt=pt: e.tensor_copy(
            dcol[:, OSCC:OSCC + 1], pt[:, 1:2]), waits=[("s_pe", t)])
        rot_pt.done(i0, "s_dve", t_oc)
        t_lnd = act(lambda e, S: e.activation(
            dcol[:, LND:LND + 1], dcol[:, TMP:TMP + 1], AF.Ln),
            waits=[("s_act", t_sg)])
        t_apw = None
        for p in (1, 2, 3):
            t_apw = act(lambda e, S, p=p: e.activation(
                dcol[:, APW + p - 1:APW + p], dcol[:, LND:LND + 1], AF.Exp,
                scale=float(SEG * p)), waits=[("s_act", t_lnd)])
        t_apwbf = dve(lambda e, S: e.tensor_copy(
            alpha_bf[:, APW:APW + 1], dcol[:, APW:APW + 1]),
            waits=[("s_act", t_apw)], inc="s_dve")
        W_APW = ("s_dve", t_apwbf)
        t_kh = None
        for s in range(NSEG):
            for b in range(4):
                tt = dve(lambda e, S, s=s, b=b: e.tensor_scalar(
                    dcol[:, TMP:TMP + 1], iotacol[:], 1.0,
                    float(SEG * s + 128 * b), OP.mult, OP.add),
                    waits=[W_CONST, ("s_act", t_kh if t_kh else t_lnd)])
                t_kh = act(lambda e, S, s=s, b=b: e.activation(
                    dcol[:, KH0 + 4 * s + b:KH0 + 4 * s + b + 1],
                    dcol[:, TMP:TMP + 1], AF.Exp,
                    scale=dcol[:, LND:LND + 1]), waits=[("s_dve", tt)])

        tt = dve(lambda e, S: e.tensor_scalar(
            rows[0:1, 0:512], iotarow512[:], -1.0, float(SEG - 1),
            OP.mult, OP.add), waits=[W_CONST])
        tt = act(lambda e, S: e.activation(
            rows[0:1, 512:1024], rows[0:1, 0:512], AF.Exp,
            scale=dcol[0:1, LND:LND + 1], bias=dcol[0:1, BIA:BIA + 1]),
            waits=[("s_dve", tt), ("s_act", t_lnd), ("s_dve", t_bia)])
        i0, ps, g = rot_ps.next()
        t = mm(ps[:], onesrow128[:], rows[0:1, 512:1024], True, True,
               waits=[("s_act", tt), g], inc="s_pe")
        t_bqtil = dve(lambda e, S, ps=ps: e.tensor_copy(bqtil[:], ps[:]),
                      waits=[("s_pe", t)])
        rot_ps.done(i0, "s_dve", t_bqtil)

        tt = dve(lambda e, S: e.tensor_scalar(
            rows2[0:1, 24:32], rows2[0:1, 0:8], dcol[0:1, LND:LND + 1],
            None, OP.mult), waits=[W_CONST, ("s_act", t_lnd)])
        tt = act(lambda e, S: e.activation(
            rows2[0:1, 32:40], rows2[0:1, 24:32], AF.Exp),
            waits=[("s_dve", tt)])
        tt = dve(lambda e, S: e.tensor_mul(
            rows2[0:1, 40:48], rows2[0:1, 32:40], rows2[0:1, 8:16]),
            waits=[("s_act", tt)])
        i0, ps, g = rot_ps.next()
        t = mm(ps[:, 0:8], onesrow128[:], rows2[0:1, 40:48], True, True,
               waits=[("s_dve", tt), g], inc="s_pe")
        t_wcomb = dve(lambda e, S, ps=ps: e.tensor_copy(
            derv[:, 24:32], ps[:, 0:8]), waits=[("s_pe", t)])
        rot_ps.done(i0, "s_dve", t_wcomb)

        t_dmat = None
        for b in range(4):
            eb = expo_sb[:, 512 * b:512 * (b + 1)]
            tm = dve(lambda e, S, eb=eb: e.tensor_scalar(
                tstage[:], eb, 0.0, None, OP.is_ge),
                waits=[W_CONST, ("s_dve", t_dmat)])
            tm2 = dve(lambda e, S, eb=eb: e.tensor_scalar(
                tstage2[:], eb, 0.0, None, OP.max),
                waits=[("s_dve", tm)])
            te = act(lambda e, S: e.activation(
                tstage2[:], tstage2[:], AF.Exp,
                scale=dcol[:, LND:LND + 1], bias=dcol[:, BIA:BIA + 1]),
                waits=[("s_dve", tm2), ("s_act", t_lnd), ("s_dve", t_bia)])
            t_dmat = dve(lambda e, S, b=b: e.tensor_mul(
                dmat[:, 512 * b:512 * (b + 1)], tstage2[:], tstage[:]),
                waits=[("s_act", te)])

        t_cs = act(lambda e, S: e.activation(c_cols[:], c_cols[:], AF.Silu),
                   waits=[("s_hl", t_cc0)])
        imod, pmod, gmod = rot_ps.next()
        first_mod = True
        t_mod_mm = None
        for k in range(NB):
            for ch in range(6):
                ih, half, gx = rot_xstage.next()
                tld = dma("sync", xstage[:, 1024 * half:1024 * (half + 1)],
                          wmod_in[128 * k:128 * (k + 1),
                                  1024 * ch:1024 * (ch + 1)],
                          waits=[gx] + ([gmod] if first_mod else []),
                          sem="s_xa" if half == 0 else "s_xb")
                for j2 in range(8):
                    col = 8 * ch + j2
                    last = (k == NB - 1 and col == 47)
                    t_mod_mm = mm(pmod[:, col:col + 1],
                                  xstage[:, 1024 * half + 128 * j2:
                                         1024 * half + 128 * (j2 + 1)],
                                  c_cols[:, k:k + 1],
                                  first_mod, last,
                                  waits=[("s_xa" if half == 0 else "s_xb",
                                          tld), ("s_act", t_cs)],
                                  inc="s_pe" if j2 == 7 else None)
                    first_mod = False
                rot_xstage.done(ih, "s_pe", t_mod_mm)
        t_modev = dve(lambda e, S, pmod=pmod: e.tensor_copy(
            mod_sb[:], pmod[:, 0:48]), waits=[("s_pe", t_mod_mm)])
        rot_ps.done(imod, "s_dve", t_modev)

        tv = dve(lambda e, S: e.tensor_copy(derv[:, 0:8], mod_sb[:, 0:8]),
                 waits=[("s_dve", t_modev)])
        tv = dve(lambda e, S: e.tensor_scalar(
            derv[:, 8:16], mod_sb[:, 8:16], 1.0, 1.0, OP.mult, OP.add),
            waits=[("s_dve", tv)])

        def plan_clamp(src_sl, dst_sl, tprev):
            tv = dve(lambda e, S: e.tensor_scalar(
                tstage[:, 0:8], src_sl, 0.0, -1.0, OP.is_ge, OP.add),
                waits=[("s_dve", tprev)])
            tv = dve(lambda e, S: e.tensor_scalar(
                tstage[:, 0:8], tstage[:, 0:8], 2.0, 1.0, OP.mult, OP.add),
                waits=[("s_dve", tv)])
            tv = dve(lambda e, S: e.tensor_mul(
                dst_sl, src_sl, tstage[:, 0:8]), waits=[("s_dve", tv)])
            tv = dve(lambda e, S: e.tensor_scalar(
                dst_sl, dst_sl, 0.25, None, OP.max), waits=[("s_dve", tv)])
            tv = dve(lambda e, S: e.tensor_mul(
                dst_sl, dst_sl, tstage[:, 0:8]), waits=[("s_dve", tv)])
            return tv

        tv = plan_clamp(mod_sb[:, 16:24], derv[:, 16:24], tv)
        tv = dve(lambda e, S: e.tensor_scalar(
            derv[:, 16:24], derv[:, 16:24], dcol[:, OSCC:OSCC + 1], None,
            OP.mult), waits=[("s_dve", tv), ("s_dve", t_oc)])
        tv = plan_clamp(mod_sb[:, 40:48], mod_sb[:, 40:48], tv)
        t_mod = dve(lambda e, S: e.tensor_scalar(
            mod_sb[:, 32:40], mod_sb[:, 32:40], 1.0, 1.0, OP.mult, OP.add),
            waits=[("s_dve", tv)])
        sa_c, ca1_c, gaeff_c = derv[:, 0:8], derv[:, 8:16], derv[:, 16:24]
        t_gaefa = None
        for k in range(3):
            t_gaefa = dve(lambda e, S, k=k: e.tensor_scalar(
                gaefa[:, 8 * k:8 * (k + 1)], derv[:, 16:24],
                dcol[:, APW + k:APW + k + 1], None, OP.mult),
                waits=[("s_dve", t_mod), ("s_act", t_apw)])
        wcomb_c = derv[:, 24:32]
        sb_c, cb1_c, gb_c = mod_sb[:, 24:32], mod_sb[:, 32:48 - 8], mod_sb[:, 40:48]
        n_dbg += dump("mod", mod_sb[:], [("s_dve", t_mod)])
        n_dbg += dump("derv", derv[:], [("s_dve", t_mod)])
        n_dbg += dump("dcol", dcol[:], [("s_act", t_kh), ("s_act", t_apw)])
        n_dbg += dump("dmat", dmat[:], [("s_dve", t_dmat)])

        def plan_wT(w_in_, wT_):
            tlast = None
            for ib in range(NB):
                ih, half, gx = rot_xstage.next()
                sx = "s_xa" if half == 0 else "s_xb"
                tld = dma("sync", xstage[:, 1024 * half:1024 * (half + 1)],
                          w_in_[128 * ib:128 * (ib + 1), :], waits=[gx],
                          sem=sx)
                tre = None
                for db in range(NB):
                    ipt, pt, gp = rot_pt.next()
                    t = tr(pt[:, 0:128],
                           xstage[:, 1024 * half + 128 * db:
                                  1024 * half + 128 * (db + 1)],
                           ident32[:], waits=[(sx, tld), gp])
                    te = act(lambda e, S, pt=pt, db=db, ib=ib, wT_=wT_:
                             e.activation(
                                 wT_[:, 1024 * db + 128 * ib:
                                     1024 * db + 128 * (ib + 1)],
                                 pt[:, 0:128], AF.Copy), waits=[("s_pe", t)])
                    rot_pt.done(ipt, "s_act", te)
                    tre = t
                    tlast = te
                rot_xstage.done(ih, "s_pe", tre)
            return tlast

        t_wq = plan_wT(wq_in, wqT)
        t_wk = plan_wT(wk_in, wkT)
        t_wv = plan_wT(wv_in, wvT)

        def plan_ln_stats(xf_t, tag, ready):
            isum, psum_s, gsum = rot_ps.next()
            isq, psum_q, gsq = rot_ps.next()
            t_summ = t_sqm = None
            for k in range(NB):
                i1, sl1, g1 = rot_sblk1.next()
                tc1 = dve(lambda e, S, k=k, sl1=sl1: e.tensor_copy(
                    sblk[:, 512 * sl1:512 * (sl1 + 1)],
                    xf_t[:, 512 * k:512 * (k + 1)]),
                    waits=[g1] + ready)
                t_summ = mm(psum_s[0:1, :], onescolbf[:],
                            sblk[:, 512 * sl1:512 * (sl1 + 1)],
                            k == 0, k == NB - 1,
                            waits=[("s_dve", tc1)] + ([gsum] if k == 0 else []),
                            inc="s_pe")
                rot_sblk1.done(i1, "s_pe", t_summ)
                i2, sl2, g2 = rot_sblk2.next()
                tc2 = act(lambda e, S, k=k, sl2=sl2: e.activation(
                    sblk[:, 512 * sl2:512 * (sl2 + 1)],
                    xf_t[:, 512 * k:512 * (k + 1)], AF.Square),
                    waits=[g2] + ready)
                t_sqm = mm(psum_q[0:1, :], onescolbf[:],
                           sblk[:, 512 * sl2:512 * (sl2 + 1)],
                           k == 0, k == NB - 1,
                           waits=[("s_act", tc2)] + ([gsq] if k == 0 else []),
                           inc="s_pe")
                rot_sblk2.done(i2, "s_pe", t_sqm)
                st[(tag, "c2", k)] = tc2

            R = lambda i: rows[0:1, 512 * i:512 * (i + 1)]
            t1 = dve(lambda e, S: e.tensor_scalar(
                R(0), psum_s[0:1, :], 1.0 / D, None, OP.mult),
                waits=[("s_pe", t_summ)])
            rot_ps.done(isum, "s_dve", t1)
            t2 = dve(lambda e, S: e.tensor_scalar(
                R(1), psum_q[0:1, :], 1.0 / D, None, OP.mult),
                waits=[("s_pe", t_sqm)])
            rot_ps.done(isq, "s_dve", t2)
            t3 = dve(lambda e, S: e.tensor_mul(R(2), R(0), R(0)),
                     waits=[("s_dve", t2)])
            t3 = dve(lambda e, S: e.tensor_sub(R(2), R(1), R(2)),
                     waits=[("s_dve", t3)])
            t3 = dve(lambda e, S: e.tensor_scalar(
                R(2), R(2), 1e-5, None, OP.add), waits=[("s_dve", t3)])
            t3 = dve(lambda e, S: e.reciprocal(R(3), R(2)),
                     waits=[("s_dve", t3)])
            t4 = act(lambda e, S: e.activation(R(3), R(3), AF.Sqrt),
                     waits=[("s_dve", t3)])
            t5 = dve(lambda e, S: e.tensor_mul(R(4), R(0), R(3)),
                     waits=[("s_act", t4)])
            return {"xf_t": xf_t, "tag": tag, "t4": t4, "t5": t5, "t1": t1,
                    "R": R, "xf_act": st[(tag, "c2", NB - 1)]}

        def plan_ln_apply(ctx, h_t, ca1col, sacol, h_guard):
            xf_t, tag, R = ctx["xf_t"], ctx["tag"], ctx["R"]
            ibr, ps_br, gbr = rot_ps.next()
            tba = mm(ps_br[:], onesrow128[:], R(3), True, True,
                     waits=[("s_act", ctx["t4"]), ("s_dve", ctx["t1"]), gbr],
                     inc="s_pe")
            ibm, ps_bm, gbm = rot_ps.next()
            tbb = mm(ps_bm[:], onesrow128[:], R(4), True, True,
                     waits=[("s_dve", ctx["t5"]), gbm], inc="s_pe")
            th = None
            tu_last = None
            for k in range(NB):
                tu = dve(lambda e, S, k=k, ps_br=ps_br: e.tensor_mul(
                    tstage[:], xf_t[:, 512 * k:512 * (k + 1)], ps_br[:]),
                    waits=[("s_pe", tba)])
                tu_last = tu
                tn = dve(lambda e, S, ps_bm=ps_bm: e.tensor_sub(
                    tstage2[:], tstage[:], ps_bm[:]),
                    waits=[("s_pe", tbb), ("s_dve", tu)])
                th = dve(lambda e, S, k=k: e.tensor_scalar(
                    h_t[:, 512 * k:512 * (k + 1)], tstage2[:],
                    ca1col[:, k:k + 1], sacol[:, k:k + 1], OP.mult, OP.add),
                    waits=[("s_dve", tn), ("s_dve", t_mod)] +
                          ([h_guard] if k == 0 else []))
                st[(tag, "h", k)] = th
            rot_ps.done(ibr, "s_dve", th)
            rot_ps.done(ibm, "s_dve", th)
            return {"h_last": th, "xf_dve": tu_last,
                    "xf_act": ctx["xf_act"]}

        def plan_ln(xf_t, h_t, ca1col, sacol, ready, tag, h_guard):
            ctx = plan_ln_stats(xf_t, tag, ready)
            return plan_ln_apply(ctx, h_t, ca1col, sacol, h_guard)

        def p1_x_ln(s):
            xf_s = xf if s % 2 == 0 else xf2
            prev2 = st.get(("p1ln", s + 2))
            t_xf = None
            for b in range(4):
                ih, half, gx = rot_xstage.next()
                xw = [gx]
                if s == NSEG - 1 and b == 0:
                    xw.append(("s_act", t_wv))
                if prev2 is not None and b == 0:
                    xw += [("s_dve", prev2["xf_dve"]),
                           ("s_act", prev2["xf_act"]),
                           ("s_xfa" if s % 2 == 0 else "s_xfb",
                            st.get(("xfsp", s + 2)))]
                sx = "s_xa" if half == 0 else "s_xb"
                tld = dma("sync", xstage[:, 1024 * half:1024 * (half + 1)],
                          x_in[512 * s + 128 * b:512 * s + 128 * (b + 1), :],
                          waits=xw, sem=sx)
                tre = None
                for db in range(NB):
                    ipt, pt, gp = rot_pt.next()
                    t = tr(pt[:, 0:128],
                           xstage[:, 1024 * half + 128 * db:
                                  1024 * half + 128 * (db + 1)],
                           ident32[:], waits=[(sx, tld), gp])
                    te = act(lambda e, S, pt=pt, db=db, b=b, xf_s=xf_s:
                             e.activation(
                        xf_s[:, 512 * db + 128 * b:512 * db + 128 * (b + 1)],
                        pt[:, 0:128], AF.Copy), waits=[("s_pe", t)])
                    rot_pt.done(ipt, "s_act", te)
                    tre = t
                    t_xf = te
                rot_xstage.done(ih, "s_pe", tre)
            ctx = plan_ln_stats(xf_s, ("l1", s), [("s_act", t_xf)])
            st[("p1ctx", s)] = ctx
            st[("xfsp", s)] = dma("gpsimd", xf_sp[s], xf_s[:],
                                  waits=[("s_act", t_xf)],
                                  sem="s_xfa" if s % 2 == 0 else "s_xfb")

        def p1_apply(s):
            lnres = plan_ln_apply(st[("p1ctx", s)], hseg, ca1_c, sa_c,
                                  h_guard=("s_pe",
                                           st.get(("vmm_last", s + 1))))
            st[("p1ln", s)] = lnres

        if stage >= 1:
            p1_x_ln(NSEG - 1)
            p1_apply(NSEG - 1)
        for s in (range(NSEG - 1, -1, -1) if stage >= 1 else []):
            lnres = st[("p1ln", s)]

            for tag, wT_, dest, spill in (("q", wqT, qseg, q_sp),
                                          ("k", wkT, kseg, k_sp)):
                tev = None
                spsem = "s_qsp" if tag == "q" else "s_ksp"
                for ib in range(NB):
                    ips, ps, gp = rot_ps.next()
                    w0 = [gp, (spsem, st.get((tag + "sp", s + 1)))]
                    t = None
                    for db in range(NB):
                        t = mm(ps[:],
                               wT_[:, 1024 * db + 128 * ib:
                                   1024 * db + 128 * (ib + 1)],
                               hseg[:, 512 * db:512 * (db + 1)],
                               db == 0, db == NB - 1,
                               waits=[("s_dve", st[(("l1", s), "h", db)])] +
                                     (w0 if db == 0 else []),
                               inc="s_pe" if db == NB - 1 else None)
                    tev = dve(lambda e, S, ps=ps, ib=ib, dest=dest:
                              e.tensor_copy(
                                  dest[:, 512 * ib:512 * (ib + 1)], ps[:]),
                              waits=[("s_pe", t)])
                    rot_ps.done(ips, "s_dve", tev)
                    st[(tag + "ev", s, ib)] = tev
                st[(tag + "sp", s)] = dma("gpsimd", spill[s], dest[:],
                                          waits=[("s_dve", tev)], sem=spsem)
            tvl = None
            for b in range(4):
                for eh in range(2):
                    ips, ps, gp = rot_ps.next()
                    t = None
                    for db in range(NB):
                        t = mm(ps[:],
                               hseg[:, 512 * db + 128 * b:
                                    512 * db + 128 * (b + 1)],
                               wvT[:, 1024 * db + 512 * eh:
                                   1024 * db + 512 * (eh + 1)],
                               db == 0, db == NB - 1,
                               waits=[("s_dve", st[(("l1", s), "h", NB - 1)])]
                                     + ([gp, ("s_vsp", st.get(("vsp", s + 1)))]
                                        if db == 0 else []),
                               inc="s_pe" if db == NB - 1 else None)
                    tev = dve(lambda e, S, ps=ps, b=b, eh=eh: e.tensor_copy(
                        vseg[:, 1024 * b + 512 * eh:1024 * b + 512 * (eh + 1)],
                        ps[:]), waits=[("s_pe", t)])
                    rot_ps.done(ips, "s_dve", tev)
                    tvl = tev
            st[("vmm_last", s)] = t
            st[("vev_last", s)] = tvl
            st[("vsp", s)] = dma("gpsimd", v_sp[s], vseg[:],
                                 waits=[("s_dve", tvl)], sem="s_vsp")

            if s - 1 >= 0:
                p1_x_ln(s - 1)

            tkh = None
            for b in range(4):
                for ib in range(NB):
                    ipk, pk, gp = rot_pk.next()
                    t = tr(pk[:],
                           kseg[:, 512 * ib + 128 * b:512 * ib + 128 * (b + 1)],
                           identbf[:],
                           waits=[("s_dve", st[("kev", s, ib)]), gp])
                    tt1 = dve(lambda e, S, pk=pk, b=b, ib=ib: e.tensor_scalar(
                        ktil[:, 1024 * b + 128 * ib:1024 * b + 128 * (ib + 1)],
                        pk[:], dcol[:, KH0 + b:KH0 + b + 1], None, OP.mult),
                        waits=[("s_pe", t), ("s_act", t_kh),
                               ("s_pe", st.get(("gmm_last", s + 1)))])
                    rot_pk.done(ipk, "s_dve", tt1)
                    tkh = tt1
            st[("kh_done", s)] = tkh

            tgm = None
            tnl = None
            for ib in range(NB):
                slots = [rot_ps.next(), rot_ps.next()]
                tstop = []
                for eh in range(2):
                    ipg, pg, gp = slots[eh]
                    t = None
                    for b in range(4):
                        t = mm(pg,
                               ktil[:, 1024 * b + 128 * ib:
                                    1024 * b + 128 * (ib + 1)],
                               vseg[:, 1024 * b + 512 * eh:
                                    1024 * b + 512 * (eh + 1)],
                               b == 0, b == 3,
                               waits=[("s_dve", tkh),
                                      ("s_dve", st[("vev_last", s)])] +
                                     ([gp] if b == 0 else []),
                               inc="s_pe" if b == 3 else None)
                    tstop.append(t)
                tgm = t
                for eh in range(2):
                    sl = nscan_sb[:, 1024 * ib + 512 * eh:
                                  1024 * ib + 512 * (eh + 1)]
                    spw = ("s_gsp0" if (s + 1) % 2 == 0 else "s_gsp1",
                           st.get(("nsp", s + 1)))
                    if s == NSEG - 1:
                        tnl = dve(lambda e, S, pg=slots[eh][1], sl=sl:
                                  e.tensor_copy(sl, pg),
                                  waits=[("s_pe", tstop[eh])])
                    else:
                        tnl = dve(lambda e, S, pg=slots[eh][1], sl=sl:
                                  e.scalar_tensor_tensor(
                                      sl, sl, alpha_bf[:, APW:APW + 1],
                                      pg, OP.mult, OP.add),
                                  waits=[("s_pe", tstop[eh]), W_APW, spw])
                    rot_ps.done(slots[eh][0], "s_dve", tnl)
            st[("gmm_last", s)] = tgm
            st[("nsc", s)] = tnl
            if s >= 1:
                st[("nsp", s)] = dma(
                    "gpsimd", g_sp[s], nscan_sb[:],
                    waits=[("s_dve", tnl)],
                    sem="s_gsp0" if s % 2 == 0 else "s_gsp1")
            if s - 1 >= 0:
                p1_apply(s - 1)

        t_p1_dve = (st[("nsc", 0)] if stage >= 1 else None)
        if stage >= 1:
            n_dbg += dump("ghat", nscan_sb[:], [("s_dve", t_p1_dve)])

        t_agd = None
        for ib in range(NB if stage >= 2 else 0):
            t_agd = dma("gpsimd", ag_in[ib],
                        nscan_sb[:, 1024 * ib:1024 * (ib + 1)],
                        waits=[("s_dve", t_p1_dve)],
                        sem="s_agin")
        t_cc = (P.emit("gpsimd",
                       lambda e, S: e.collective_compute(
                           "AllGather", OP.bypass,
                           replica_groups=[list(range(NCORE))],
                           ins=[ag_in], outs=[ag_out]),
                       waits=[full("s_agin")], inc="s_cc")
                if stage >= 2 else None)

        tlast = None
        for ib in range(NB if stage >= 2 else 0):
            ih, half, gx = rot_xstage.next()
            sx = "s_xa" if half == 0 else "s_xb"
            tld = dma("sync", xstage[:, 1024 * half:1024 * (half + 1)],
                      wo_in[128 * ib:128 * (ib + 1), :],
                      waits=[gx, ("s_dve", t_p1_dve)], sem=sx)
            tre = None
            for db in range(NB):
                ipt, pt, gp = rot_pt.next()
                t = tr(pt[:, 0:128],
                       xstage[:, 1024 * half + 128 * db:
                              1024 * half + 128 * (db + 1)],
                       ident32[:], waits=[(sx, tld), gp])
                te = act(lambda e, S, pt=pt, db=db, ib=ib: e.activation(
                    woT[:, 1024 * db + 128 * ib:1024 * db + 128 * (ib + 1)],
                    pt[:, 0:128], AF.Copy), waits=[("s_pe", t)])
                rot_pt.done(ipt, "s_act", te)
                tre = t
                tlast = te
            rot_xstage.done(ih, "s_pe", tre)
        t_wo = tlast

        for s in (range(NSEG) if stage >= 2 else []):
            prv = s - 1
            if s == 0:
                t_qld = t_kld = None
                W_Q = ("s_dve", st[("qev", 0, NB - 1)])
                W_K = ("s_dve", st[("kev", 0, NB - 1)])
            else:
                t_qld = dma("sync", qseg[:], q_sp[s],
                            waits=[full("s_qsp"),
                                   ("s_pe", st.get(("p2score", prv))),
                                   ("s_dve", st.get(("p2qtt", prv)))],
                            sem="s_ql")
                t_kld = dma("sync", kseg[:], k_sp[s],
                            waits=[full("s_ksp"),
                                   ("s_pe", st.get(("p2score", prv))),
                                   ("s_dve", t_p1_dve)], sem="s_kl")
                W_Q = ("s_ql", t_qld)
                W_K = ("s_kl", t_kld)
            if s < NSEG - 1:
                tnb = dma("sync", nbuf[:], g_sp[s + 1],
                          waits=[("s_gsp0" if (s + 1) % 2 == 0 else "s_gsp1",
                                  st[("nsp", s + 1)]),
                                 ("s_dve", t_p1_dve),
                                 ("s_pe", st.get(("p2ret", prv)))],
                          sem="s_nb")
                nw = ("s_nb", tnb)
            else:
                nw = None

            if s == 0:
                W_V = ("s_dve", st[("vev_last", 0)])
            else:
                t_vld = dma("sync", vseg[:], v_sp[s],
                            waits=[full("s_vsp"),
                                   ("s_pe", st.get(("p2ret", prv)))],
                            sem="s_vl")
                W_V = ("s_vl", t_vld)

            t_scev = {}
            tsc_mm = None
            for b in range(4):
                ips, ps, gp = rot_ps.next()
                t = None
                for ib in range(NB):
                    t = mm(ps[:],
                           kseg[:, 512 * ib + 128 * b:512 * ib + 128 * (b + 1)],
                           qseg[:, 512 * ib:512 * (ib + 1)],
                           ib == 0, ib == NB - 1,
                           waits=[W_K, W_Q] +
                                 ([gp] if ib == 0 else []),
                           inc="s_pe" if ib == NB - 1 else None)
                tsc_mm = t
                tev = dve(lambda e, S, ps=ps, b=b: e.tensor_mul(
                    scT[:, 512 * b:512 * (b + 1)], ps[:],
                    dmat[:, 512 * b:512 * (b + 1)]),
                    waits=[("s_pe", t), ("s_dve", t_dmat),
                           ("s_pe", st.get(("p2ret", prv))),
                           ("s_dve", t_p1_dve)])
                rot_ps.done(ips, "s_dve", tev)
                t_scev[b] = tev
            st[("p2score", s)] = tsc_mm

            t_qt = None
            for ib in range(NB):
                t_qt = dve(lambda e, S, ib=ib: e.tensor_mul(
                    qtil[:, 512 * ib:512 * (ib + 1)],
                    qseg[:, 512 * ib:512 * (ib + 1)], bqtil[:]),
                    waits=[W_Q, ("s_dve", t_bqtil),
                           ("s_dve", t_p1_dve),
                           ("s_pe", st.get(("p2ret", prv))),
                           ("s_qtsp", st.get(("qtsp", prv)))])
            st[("p2qtt", s)] = t_qt
            st[("qtsp", s)] = dma("gpsimd", qt_sp[s], qtil[:],
                                  waits=[("s_dve", t_qt)], sem="s_qtsp")

            t_ret = None
            tevl = None
            for j in range(NB):
                ips, ps, gp = rot_ps.next()
                t = None
                for b in range(4):
                    t = mm(ps[:],
                           vseg[:, 1024 * b + 128 * j:1024 * b + 128 * (j + 1)],
                           scT[:, 512 * b:512 * (b + 1)],
                           b == 0, (b == 3 and s == NSEG - 1),
                           waits=[W_V, ("s_dve", t_scev[b])] +
                                 ([gp] if b == 0 else []),
                           inc="s_pe" if (b == 3 and s == NSEG - 1) else None)
                if s < NSEG - 1:
                    for ib in range(NB):
                        t = mm(ps[:],
                               nbuf[:, 1024 * ib + 128 * j:
                                    1024 * ib + 128 * (j + 1)],
                               qtil[:, 512 * ib:512 * (ib + 1)],
                               False, ib == NB - 1,
                               waits=[nw, ("s_dve", t_qt)],
                               inc="s_pe" if ib == NB - 1 else None)
                t_ret = t
                tev = dve(lambda e, S, ps=ps, j=j: e.tensor_copy(
                    retrev[:, 512 * j:512 * (j + 1)], ps[:]),
                    waits=[("s_pe", t),
                           ("s_rsp", st.get(("rsp", prv)))])
                rot_ps.done(ips, "s_dve", tev)
                tevl = tev
            st[("p2ret", s)] = t_ret
            st[("rsp", s)] = dma("gpsimd", retr_sp[s], retrev[:],
                                 waits=[("s_dve", tevl)], sem="s_rsp")

        t_sc = None
        pair_guard = {}
        for p in range(4 if stage >= 2 else 0):
            tacc0 = tacc1 = None
            for j in range(NCORE):
                sl = j % 4
                sag = ("s_aga", "s_agb", "s_age", "s_agf")[sl]
                tld = dma("gpsimd", agstage2[:, 2048 * sl:2048 * sl + 1024],
                          ag_out[NB * j + 2 * p],
                          waits=[("s_cc", t_cc), full("s_dbg"),
                                 ("s_dve", pair_guard.get(sl))],
                          sem=sag)
                tld = dma("gpsimd", agstage2[:, 2048 * sl + 1024:
                                             2048 * (sl + 1)],
                          ag_out[NB * j + 2 * p + 1],
                          waits=[], sem=sag)
                for h in range(2):
                    stg = agstage2[:, 2048 * sl + 1024 * h:
                                   2048 * sl + 1024 * (h + 1)]
                    ib = 2 * p + h
                    dst = sc_sb[:, 1024 * ib:1024 * (ib + 1)]
                    if j == 0:
                        tacc = dve(lambda e, S, stg=stg, dst=dst:
                                   e.tensor_scalar(dst, stg,
                                                   wcomb_c[:, 0:1], None,
                                                   OP.mult),
                                   waits=[(sag, tld), ("s_dve", t_wcomb),
                                          ("s_dve", t_p1_dve)])
                    else:
                        tacc = dve(lambda e, S, stg=stg, dst=dst, j=j:
                                   e.scalar_tensor_tensor(
                                       dst, stg, wcomb_c[:, j:j + 1], dst,
                                       OP.mult, OP.add),
                                   waits=[(sag, tld),
                                          ("s_dve", tacc0 if h == 0
                                           else tacc1)])
                    if h == 0:
                        tacc0 = tacc
                    else:
                        tacc1 = tacc
                pair_guard[sl] = tacc1
            st[("scdone", 2 * p)] = tacc0
            st[("scdone", 2 * p + 1)] = tacc1
            t_sc = tacc1
        if stage >= 2:
            n_dbg += dump("sc", sc_sb[:], [("s_dve", t_sc)])

        def p2b_x(s):
            xf_s = xf if s % 2 == 0 else xf2
            prev2 = st.get(("p2ln", s - 2))
            xw = [("s_xfa" if s % 2 == 0 else "s_xfb",
                   st[("xfsp", s)])]
            if prev2 is not None:
                xw += [("s_dve", prev2["xf_dve"]),
                       ("s_act", prev2["xf_act"]),
                       ("s_x2sp" if s % 2 == 0 else "s_x2b2",
                        st.get(("x2sp", s - 2)))]
            else:
                p1l = st.get(("p1ln", s))
                if p1l is not None:
                    xw += [("s_dve", p1l["xf_dve"]),
                           ("s_act", p1l["xf_act"])]
            t_xl = dma("sync", xf_s[:], xf_sp[s], waits=xw,
                       sem="s_xla" if s % 2 == 0 else "s_xlb")
            st[("p2bx", s)] = (xf_s, t_xl)

        for s in (range(NSEG) if stage >= 2 else []):
            prv = s - 1
            qtil_s = qtil if s % 2 == 0 else qtil2
            rstr_s = rstr if s % 2 == 0 else rstr2
            sql = "s_ql" if s % 2 == 0 else "s_ql2"
            srl = "s_rl" if s % 2 == 0 else "s_rl2"
            alias_guards = ([("s_pe", st.get(("p2ret", NSEG - 1))),
                            ("s_pe", st.get(("p2score", NSEG - 1)))]
                           if s < 2 else [])
            t_qtl = dma("sync", qtil_s[:], qt_sp[s],
                        waits=[full("s_qtsp"),
                               ("s_pe", st.get(("p2bS", s - 2)))] +
                              alias_guards,
                        sem=sql)
            t_rld = dma("sync", rstr_s[:], retr_sp[s],
                        waits=[full("s_rsp"),
                               ("s_dve", st.get(("bev", s - 2)))] +
                              alias_guards,
                        sem=srl)
            if s < 2:
                p2b_x(s)
            xf_s, t_xl = st[("p2bx", s)]
            W_XF = ("s_xla" if s % 2 == 0 else "s_xlb", t_xl)

            apcol = (dcol[:, ONE:ONE + 1] if s == NSEG - 1 else
                     dcol[:, APW + (NSEG - 2 - s):APW + (NSEG - 1 - s)])
            tbs = None
            tbev = None
            for j in range(NB):
                ips, ps, gp = rot_ps.next()
                t = None
                for ib in range(NB):
                    t = mm(ps[:],
                           sc_sb[:, 1024 * ib + 128 * j:
                                 1024 * ib + 128 * (j + 1)],
                           qtil_s[:, 512 * ib:512 * (ib + 1)],
                           ib == 0, ib == NB - 1,
                           waits=[("s_dve", st[("scdone", ib)]),
                                  (sql, t_qtl)] +
                                 ([gp] if ib == 0 else []),
                           inc="s_pe" if ib == NB - 1 else None)
                tbs = t
                tev = dve(lambda e, S, ps=ps, j=j, apcol=apcol,
                          rstr_s=rstr_s:
                          e.scalar_tensor_tensor(
                              retrev[:, 512 * j:512 * (j + 1)], ps[:],
                              apcol, rstr_s[:, 512 * j:512 * (j + 1)],
                              OP.mult, OP.add),
                          waits=[("s_pe", t), (srl, t_rld),
                                 ("s_act", t_apw), ("s_dve", t_one),
                                 ("s_pe", st.get(("p2wo", prv))),
                                 ("s_rsp", st.get(("rsp", NSEG - 1)))])
                rot_ps.done(ips, "s_dve", tev)
                st[("rev", s, j)] = tev
                tbev = tev
            st[("p2bS", s)] = tbs
            st[("bev", s)] = tbev

            t_wo_mm = None
            tx2 = None
            for o in range(NB):
                ips, ps, gp = rot_ps.next()
                t = None
                for eb in range(NB):
                    t = mm(ps[:],
                           woT[:, 1024 * eb + 128 * o:1024 * eb + 128 * (o + 1)],
                           retrev[:, 512 * eb:512 * (eb + 1)],
                           eb == 0, eb == NB - 1,
                           waits=[("s_dve", st[("rev", s, eb)]),
                                  ("s_act", t_wo)] + ([gp] if eb == 0 else []),
                           inc="s_pe" if eb == NB - 1 else None)
                t_wo_mm = t
                tx2 = dve(lambda e, S, ps=ps, o=o, xf_s=xf_s:
                          e.scalar_tensor_tensor(
                    xf_s[:, 512 * o:512 * (o + 1)], ps[:],
                    gaeff_c[:, o:o + 1], xf_s[:, 512 * o:512 * (o + 1)],
                    OP.mult, OP.add),
                    waits=[("s_pe", t), ("s_dve", t_mod), W_XF])
                rot_ps.done(ips, "s_dve", tx2)
            st[("p2wo", s)] = t_wo_mm
            st[("x2w", s)] = tx2
            st[("x2sp", s)] = dma("gpsimd", x2_sp[s], xf_s[:],
                                  waits=[("s_dve", tx2)],
                                  sem="s_x2sp" if s % 2 == 0 else "s_x2b2")

            lnres2 = plan_ln(xf_s, hseg, cb1_c, sb_c,
                             [("s_dve", tx2)], ("l2", s),
                             h_guard=("s_h2sp", st.get(("h2sp", prv)))
                             if s > 0 else
                             ("s_pe", st.get(("vmm_last", 0))))
            st[("p2ln", s)] = lnres2
            st[("h2sp", s)] = dma("gpsimd", h2_sp[s], hseg[:],
                                  waits=[("s_dve", lnres2["h_last"])],
                                  sem="s_h2sp")
            if s + 2 <= NSEG - 1:
                p2b_x(s + 2)

        W_P2 = ([full("s_x2sp"), full("s_x2b2"), full("s_h2sp"),
                 full("s_dbg")] if stage >= 3 else [])
        W_W1G = []
        if stage >= 3:
            for wq in range(4):
                sw = ("s_w1", "s_w1b", "s_w1c", "s_w1d")[wq]
                for db in range(NB):
                    dma("gpsimd",
                        w1sb[:, FW * db + 1024 * wq:FW * db + 1024 * (wq + 1)],
                        w1bf[128 * db:128 * (db + 1),
                             1024 * wq:1024 * (wq + 1)],
                        waits=W_P2 + [W_WPREP1], sem=sw)
                W_W1G.append(full(sw))
        for wb in range(NWB if stage >= 3 else 0):
            dma("gpsimd",
                w2sb[:, 1024 * wb:1024 * (wb + 1)],
                w2bf[128 * wb:128 * (wb + 1), :],
                waits=W_P2 + [W_WPREP2],
                sem="s_w2" if wb % 2 == 0 else "s_w2b")
        W_W2 = full("s_w2")
        W_W2B = full("s_w2b")

        n_out = 0
        for s in range(NSEG if stage >= 3 else 0):
            t_h2l = dma("sync", h2c[:], h2_sp[s],
                        waits=W_P2 + [("s_pe", st.get(("ffnmid", s - 1)))],
                        sem="s_hl")
            t_x2l = dma("sync", x2c[:], x2_sp[s],
                        waits=W_P2 + [("s_dve", st.get(("ffnx3", s - 1))),
                                      ("s_pe", st.get(("p3tr", s - 1)))],
                        sem="s_x2l")
            tmid = None
            for wb in range(NWB):
                ips, ps, gp = rot_ps.next()
                t = None
                for db in range(NB):
                    t = mm(ps[:],
                           w1sb[:, FW * db + 128 * wb:FW * db + 128 * (wb + 1)],
                           h2c[:, 512 * db:512 * (db + 1)],
                           db == 0, db == NB - 1,
                           waits=[("s_hl", t_h2l), W_W1G[wb // 8]] +
                                 ([gp] if db == 0 else []),
                           inc="s_pe" if db == NB - 1 else None)
                tmid = t
                te = act(lambda e, S, ps=ps, wb=wb: e.activation(
                    midc[:, 512 * wb:512 * (wb + 1)], ps[:], AF.Relu,
                    bias=b1col[:, wb:wb + 1]),
                    waits=[("s_pe", t), ("s_pe", st.get(("ffnout", s - 1)))])
                rot_ps.done(ips, "s_act", te)
                st[("ffnme", s, wb)] = te
            st[("ffnmid", s)] = tmid
            tout = None
            tx3 = None
            for o in range(NB):
                ipg, po, gp = rot_ps.next()
                t = mm(po, b2row[0:1, 128 * o:128 * (o + 1)],
                       onesrow512bf[:], True, False,
                       waits=[W_W2, W_W2B, W_WPREP2, gp, W_CONST])
                for wb in range(NWB):
                    t = mm(po,
                           w2sb[:, 1024 * wb + 128 * o:1024 * wb + 128 * (o + 1)],
                           midc[:, 512 * wb:512 * (wb + 1)],
                           False, wb == NWB - 1,
                           waits=[("s_act", st[("ffnme", s, wb)]), W_W2],
                           inc="s_pe" if wb == NWB - 1 else None)
                tout = t
                tx3 = dve(lambda e, S, po=po, o=o: e.scalar_tensor_tensor(
                    x2c[:, 512 * o:512 * (o + 1)], po,
                    gb_c[:, o:o + 1], x2c[:, 512 * o:512 * (o + 1)],
                    OP.mult, OP.add),
                    waits=[("s_pe", t), ("s_x2l", t_x2l), ("s_dve", t_mod)])
                rot_ps.done(ipg, "s_dve", tx3)
                st[("x3", s, o)] = tx3
            st[("ffnout", s)] = tout
            st[("ffnx3", s)] = tx3
            tp3 = None
            for b in range(4):
                io, osl, go = rot_ostage.next()
                tev = None
                for o in range(NB):
                    ipt, pt, gp = rot_pt.next()
                    t = tr(pt[:, 0:128],
                           x2c[:, 512 * o + 128 * b:512 * o + 128 * (b + 1)],
                           ident32[:],
                           waits=[("s_dve", st[("x3", s, o)]), gp] +
                                 ([go] if o == 0 else []))
                    tp3 = t
                    tev = act(lambda e, S, pt=pt, o=o, osl=osl: e.activation(
                        ostage[:, 1024 * osl + 128 * o:
                               1024 * osl + 128 * (o + 1)],
                        pt[:, 0:128], AF.Copy), waits=[("s_pe", t)])
                    rot_pt.done(ipt, "s_act", tev)
                so = "s_oa" if osl == 0 else "s_ob"
                tod = dma("sync",
                          out_dram[512 * s + 128 * b:512 * s + 128 * (b + 1), :],
                          ostage[:, 1024 * osl:1024 * (osl + 1)],
                          waits=[("s_act", tev)], sem=so)
                rot_ostage.done(io, so, tod)
                n_out += 1
            st[("p3tr", s)] = tp3

        P.emit("sync", lambda e, S: e.nop(),
               waits=[full("s_oa"), full("s_ob"), full("s_dbg")])

        with nc.Block() as block:
            def mk(ename):
                def body(eng):
                    for fn, waits, inc, amt in P.seq[ename]:
                        for semn, val in waits:
                            eng.wait_ge(sems[semn], val)
                        inst = fn(eng, sems)
                        if inc is not None:
                            inst.then_inc(sems[inc], amt)
                return body
            for ename in ENGINES:
                if P.seq[ename]:
                    getattr(block, ename)(mk(ename))

    return nc


_CACHE = {}


def _get_nc(debug=(), stage=3):
    key = (tuple((n, tuple(s), d) for n, s, d in debug), stage)
    if key not in _CACHE:
        _CACHE[key] = build_program(debug, stage)
    return _CACHE[key]


def _consts():
    bf = ml_dtypes.bfloat16
    return {
        "ident_f32": np.eye(128, dtype=np.float32),
        "ident_bf16": np.eye(128, dtype=np.float32).astype(bf),
        "ones_row128": np.ones((1, 128), np.float32),
        "ones_row512_bf": np.ones((1, 512), np.float32).astype(bf),
        "ones_col_bf": np.ones((128, 1), np.float32).astype(bf),
        "iota_col": np.arange(128, dtype=np.float32).reshape(128, 1),
        "iota_row128": np.arange(128, dtype=np.float32).reshape(1, 128),
        "iota_row512": np.arange(512, dtype=np.float32).reshape(1, 512),
        "expo_t": np.concatenate(
            [(128 * b + np.arange(128)[:, None]
              - np.arange(512)[None, :] - 1.0).astype(np.float32)
             for b in range(4)], axis=1),
    }


def kernel(L0_attn_x, L0_attn_Wq, L0_attn_Wk, L0_attn_Wv, L0_attn_Wo,
           L0_attn_decay_logit, L0_attn_out_scale,
           L0_ffn_w1, L0_ffn_b1, L0_ffn_w2, L0_ffn_b2, L0_c, L0_wmod,
           _debug=(), _stage=3):
    nc = _get_nc(_debug, _stage)
    x2d = np.ascontiguousarray(np.asarray(L0_attn_x, np.float32)
                               .reshape(T, D))
    common = dict(
        wq=np.ascontiguousarray(np.asarray(L0_attn_Wq, np.float32)),
        wk=np.ascontiguousarray(np.asarray(L0_attn_Wk, np.float32)),
        wv=np.ascontiguousarray(np.asarray(L0_attn_Wv, np.float32)),
        wo=np.ascontiguousarray(np.asarray(L0_attn_Wo, np.float32)),
        w1=np.ascontiguousarray(np.asarray(L0_ffn_w1, np.float32)),
        w2=np.ascontiguousarray(np.asarray(L0_ffn_w2, np.float32)),
        b1col=np.ascontiguousarray(
            np.asarray(L0_ffn_b1, np.float32).reshape(NWB, 128).T),
        b2row=np.ascontiguousarray(
            np.asarray(L0_ffn_b2, np.float32).reshape(1, D)),
        c_cols=np.ascontiguousarray(
            np.asarray(L0_c, np.float32).reshape(NB, 128).T),
        wmod=np.ascontiguousarray(np.asarray(L0_wmod, np.float32)),
        decay_logit=np.asarray(L0_attn_decay_logit,
                               np.float32).reshape(1, 1),
        out_scale=np.asarray(L0_attn_out_scale, np.float32).reshape(1, 1),
        **_consts(),
    )
    in_maps = []
    for c in range(NCORE):
        ecore = np.zeros((1, NCORE), np.float32)
        emask = np.zeros((1, NCORE), np.float32)
        for j in range(NCORE):
            if j > c:
                ecore[0, j] = float(TC * (j - c - 1))
                emask[0, j] = 1.0
        m = dict(common)
        m["x"] = np.ascontiguousarray(x2d[TC * c:TC * (c + 1)])
        m["ecore"] = ecore
        m["emask"] = emask
        in_maps.append(m)
    res = run_bass_kernel_spmd(nc, in_maps, core_ids=list(range(NCORE)))
    out = np.concatenate([res.results[c]["out"] for c in range(NCORE)],
                         axis=0)
    if _debug:
        return out.reshape(1, T, D), list(res.results)
    return out.reshape(1, T, D)


# version 17
# speedup vs baseline: 1.0712x; 1.0067x over previous
import numpy as np
import ml_dtypes
from contextlib import ExitStack

import concourse.bass as bass
import concourse.mybir as mybir
from concourse.bass_utils import run_bass_kernel_spmd

F32 = mybir.dt.float32
BF16 = mybir.dt.bfloat16
AF = mybir.ActivationFunctionType
OP = mybir.AluOpType

T = 16384
D = 1024
FW = 4096
NCORE = 8
TC = T // NCORE
SEG = 512
NSEG = TC // SEG
NB = D // 128
NWB = FW // 128
LOG_INV_SQRT_D = float(np.log(1.0 / np.sqrt(D)))

ENGINES = ("sync", "tensor", "vector", "scalar", "gpsimd")


class Plan:

    def __init__(self):
        self.seq = {e: [] for e in ENGINES}
        self.cnt = {}
        self.last_wait = {e: {} for e in ENGINES}

    def emit(self, eng, fn, waits=(), inc=None, amt=1):
        w = []
        lw = self.last_wait[eng]
        for item in waits:
            if item is None:
                continue
            sem, val = item
            if val is None:
                continue
            if val > lw.get(sem, 0):
                lw[sem] = val
                w.append((sem, val))
        t = None
        if inc is not None:
            self.cnt[inc] = self.cnt.get(inc, 0) + amt
            t = self.cnt[inc]
        self.seq[eng].append((fn, tuple(w), inc, amt))
        return t


class Rot:

    def __init__(self, slots):
        self.slots = list(slots)
        self.n = 0
        self.last = {}

    def next(self):
        i = self.n % len(self.slots)
        self.n += 1
        return i, self.slots[i], self.last.get(i)

    def done(self, i, sem, ticket):
        self.last[i] = (sem, ticket)


def build_program(debug=(), stage=3):
    nc = bass.Bass("TRN2", target_bir_lowering=False, debug=False)

    def ein(name, shape, dt=F32):
        return nc.dram_tensor(name, shape, dt, kind="ExternalInput").ap()

    x_in = ein("x", [TC, D])
    wq_in = ein("wq", [D, D])
    wk_in = ein("wk", [D, D])
    wv_in = ein("wv", [D, D])
    wo_in = ein("wo", [D, D])
    w1_in = ein("w1", [D, FW])
    b1_in = ein("b1col", [128, NWB])
    w2_in = ein("w2", [FW, D])
    b2_in = ein("b2row", [1, D])
    c_in = ein("c_cols", [128, NB])
    wmod_in = ein("wmod", [D, 6 * D])
    dlog_in = ein("decay_logit", [1, 1])
    osc_in = ein("out_scale", [1, 1])
    id32_in = ein("ident_f32", [128, 128])
    idbf_in = ein("ident_bf16", [128, 128], BF16)
    onesrow128_in = ein("ones_row128", [1, 128])
    onesrow512bf_in = ein("ones_row512_bf", [1, 512], BF16)
    onescol_bf_in = ein("ones_col_bf", [128, 1], BF16)
    iotacol_in = ein("iota_col", [128, 1])
    iotarow128_in = ein("iota_row128", [1, 128])
    iotarow512_in = ein("iota_row512", [1, 512])
    ecore_in = ein("ecore", [1, NCORE])
    emask_in = ein("emask", [1, NCORE])
    expo_in = ein("expo_t", [128, 4 * SEG])

    out_dram = nc.dram_tensor("out", [TC, D], F32, kind="ExternalOutput").ap()

    def scr(name, shape, dt):
        return nc.dram_tensor(name, shape, dt, kind="Internal").ap()

    q_sp = scr("q_sp", [NSEG, 128, NB * SEG], BF16)
    k_sp = scr("k_sp", [NSEG, 128, NB * SEG], BF16)
    v_sp = scr("v_sp", [NSEG, 128, NSEG * D], BF16)
    g_sp = scr("g_sp", [NSEG, 128, NB * D], BF16)
    x2_sp = scr("x2_sp", [NSEG, 128, NB * SEG], F32)
    retr_sp = scr("retr_sp", [NSEG, 128, NB * SEG], BF16)
    qt_sp = scr("qt_sp", [NSEG, 128, NB * SEG], BF16)
    xf_sp = scr("xf_sp", [NSEG, 128, NB * SEG], F32)
    w1bf = scr("w1bf", [D, FW], BF16)
    w2bf = scr("w2bf", [FW, D], BF16)
    h2_sp = scr("h2_sp", [NSEG, 128, NB * SEG], BF16)
    ag_in = scr("ag_in", [NB, 128, D], BF16)
    ag_out = nc.dram_tensor("ag_out", [NCORE * NB, 128, D], BF16,
                            kind="Internal", addr_space="Shared").ap()

    dbg = {}
    for name, shape, dt in debug:
        dbg[name] = nc.dram_tensor("dbg_" + name, shape, dt,
                                   kind="ExternalOutput").ap()

    P = Plan()
    st = {}

    with ExitStack() as ctx:
        SB_BASE = 18432
        SB_LIMIT = 16384 + 212863
        offmap = {}
        cur = [SB_BASE]

        def sb(name, shape, dt, at=None):
            size = shape[1] * mybir.dt.size(dt)
            if at is None:
                at = cur[0]
                cur[0] = (at + size + 63) & ~63
            offmap[name] = at
            return nc.alloc_sbuf_tensor_at(name, shape, dt, offset=at)

        ident32 = sb("ident32", [128, 128], F32)
        identbf = sb("identbf", [128, 128], BF16)
        onesrow128 = sb("onesrow128", [1, 128], F32)
        onesrow512bf = sb("onesrow512bf", [1, 512], BF16)
        onescolbf = sb("onescolbf", [128, 1], BF16)
        iotacol = sb("iotacol", [128, 1], F32)
        iotarow128 = sb("iotarow128", [1, 128], F32)
        iotarow512 = sb("iotarow512", [1, 512], F32)
        b2row = sb("b2row", [1, D], BF16)
        b1col = sb("b1col", [128, NWB], F32)
        c_cols = sb("c_cols", [128, NB], F32)
        mod_sb = sb("mod_sb", [128, 48], F32)
        derv = sb("derv", [128, 32], F32)
        dcol = sb("dcol", [128, 32], F32)
        rows2 = sb("rows2", [1, 64], F32)
        PERM_END = cur[0]
        rows = sb("rows", [1, 512 * 5], F32)
        dmat = sb("dmat", [128, 4 * SEG], BF16)
        bqtil = sb("bqtil", [128, SEG], F32)
        tstage = sb("tstage", [128, 512], F32)
        tstage2 = sb("tstage2", [128, 512], F32)
        expo_sb = sb("expo_sb", [128, 4 * SEG], F32)

        wqT = sb("wqT", [128, NB * D], BF16)
        wkT = sb("wkT", [128, NB * D], BF16)
        wvT = sb("wvT", [128, NB * D], BF16)
        xstage = sb("xstage", [128, 2 * D], F32)
        xf = sb("xf", [128, NB * SEG], F32)
        hseg = sb("hseg", [128, NB * SEG], BF16)
        sblk = sb("sblk", [128, 4 * 512], BF16)
        qseg = sb("qseg", [128, NB * SEG], BF16)
        kseg = sb("kseg", [128, NB * SEG], BF16)
        vseg = sb("vseg", [128, NSEG * D], BF16)
        ktil = sb("ktil", [128, 4 * D], BF16)
        khat = sb("khat", [128, 4 * D], BF16)
        ghat = sb("ghat", [128, NB * D], BF16)
        gstage = sb("gstage", [128, 2 * D], BF16)
        retrev = sb("retrev", [128, NB * SEG], BF16)
        xf2 = sb("xf2", [128, NB * SEG], F32)
        A_END = cur[0]
        assert A_END <= SB_LIMIT, A_END

        woT = sb("woT", [128, NB * D], BF16, at=offmap["wqT"])
        nbuf = sb("nbuf", [128, NB * D], BF16, at=offmap["wkT"])
        sc_sb = sb("sc_sb", [128, NB * D], BF16, at=offmap["wvT"])
        mbuf = sb("mbuf", [128, NB * D], BF16, at=offmap["ghat"])
        scT = sb("scT", [128, 4 * SEG], BF16, at=offmap["ktil"])
        qtil = sb("qtil", [128, NB * SEG], BF16, at=offmap["khat"])
        agstage = sb("agstage", [128, 2 * D], BF16, at=offmap["gstage"])
        rstr = sb("rstr", [128, NB * SEG], BF16, at=offmap["ktil"])
        gstg2 = sb("gstg2", [128, 4 * D], BF16, at=offmap["xstage"])
        rstr2 = sb("rstr2", [128, NB * SEG], BF16, at=offmap["kseg"])
        nscan_sb = sb("nscan_sb", [128, NB * D], BF16, at=offmap["ghat"])
        scT2 = sb("scT2", [128, 4 * SEG], BF16, at=offmap["xstage"])
        retrevb = sb("retrevb", [128, NB * SEG], BF16, at=offmap["wkT"])
        qtil2 = sb("qtil2", [128, NB * SEG], BF16, at=offmap["vseg"])
        agstage2 = sb("agstage2", [128, NB * D], BF16, at=offmap["ghat"])

        fcur = [offmap["rows"]]

        def sbf(name, shape, dt):
            size = shape[1] * mybir.dt.size(dt)
            at = fcur[0]
            fcur[0] = (at + size + 63) & ~63
            offmap[name] = at
            return nc.alloc_sbuf_tensor_at(name, shape, dt, offset=at)

        w1sb = sbf("w1sb", [128, NB * FW], BF16)
        w2sb = sbf("w2sb", [128, NWB * D], BF16)
        midc = sbf("midc", [128, NWB * SEG], BF16)
        h2c = sbf("h2c", [128, NB * SEG], BF16)
        x2c = sbf("x2c", [128, NB * SEG], F32)
        ostage = sbf("ostage", [128, 2 * D], F32)
        assert fcur[0] <= SB_LIMIT, fcur[0]

        ps_a = ctx.enter_context(nc.psum_tensor("ps_a", [128, 512], F32))
        ps_b = ctx.enter_context(nc.psum_tensor("ps_b", [128, 512], F32))
        pg_a = ctx.enter_context(nc.psum_tensor("pg_a", [128, 1024], F32))
        pg_b = ctx.enter_context(nc.psum_tensor("pg_b", [128, 1024], F32))
        pk_a = ctx.enter_context(nc.psum_tensor("pk_a", [128, 128], BF16))
        pk_b = ctx.enter_context(nc.psum_tensor("pk_b", [128, 128], BF16))

        sem_names = ("s_c", "s_xa", "s_xb", "s_ql", "s_kl", "s_vl",
                     "s_nb", "s_ga", "s_gb", "s_aga", "s_agb",
                     "s_hl", "s_x2l", "s_oa", "s_ob",
                     "s_qsp", "s_ksp", "s_vsp", "s_gsp0", "s_gsp1",
                     "s_x2sp", "s_h2sp", "s_agin", "s_w1", "s_w2", "s_dbg",
                     "s_rsp", "s_rl", "s_x2b2", "s_qtsp", "s_w1b", "s_w1c", "s_w1d", "s_gc", "s_gd", "s_age", "s_agf", "s_ql2", "s_rl2", "s_wp1", "s_wp2", "s_w2b", "s_xfa", "s_xfb", "s_xla", "s_xlb",
                     "s_pe", "s_dve", "s_act", "s_cc")
        sems = {n: ctx.enter_context(nc.semaphore(n)) for n in sem_names}

        rot_ps = Rot([ps_a[:, 0:512], ps_b[:, 0:512],
                      pg_a[:, 0:512], pg_a[:, 512:1024],
                      pg_b[:, 0:512], pg_b[:, 512:1024]])
        rot_pt = rot_ps
        rot_pk = Rot([pk_a, pk_b])
        rot_xstage = Rot([0, 1])
        rot_nstage = Rot([0, 1, 2, 3])
        rot_gstage = Rot([0, 1])
        rot_ostage = Rot([0, 1])
        rot_sblk1 = Rot([0, 1])
        rot_sblk2 = Rot([2, 3])

        def dma(eng, out_ap, in_ap, waits=(), sem=None):
            assert sem is not None
            return P.emit(eng,
                          lambda e, S, o=out_ap, i=in_ap:
                              e.dma_start(out=o, in_=i),
                          waits, sem, 16)

        def full(sem):
            return (sem, P.cnt.get(sem, 0))

        def mm(out_ap, a_ap, b_ap, start, stop, waits=(), inc=None):
            return P.emit("tensor",
                          lambda e, S, o=out_ap, a=a_ap, b=b_ap, s1=start,
                          s2=stop: e.matmul(o, a, b, start=s1, stop=s2),
                          waits, inc)

        def tr(out_ap, in_ap, ident_ap, waits=(), inc="s_pe"):
            return P.emit("tensor",
                          lambda e, S, o=out_ap, i=in_ap, d=ident_ap:
                              e.transpose(o, i, d),
                          waits, inc)

        def dve(fn, waits=(), inc="s_dve"):
            return P.emit("vector", fn, waits, inc)

        def act(fn, waits=(), inc="s_act"):
            return P.emit("scalar", fn, waits, inc)

        def dump(name, src_ap, waits):
            if name in dbg:
                t = dma("sync", dbg[name], src_ap,
                        waits=list(waits) + [st.get("dump_prev")],
                        sem="s_dbg")
                st["dump_prev"] = ("s_dbg", t)
                return 1
            return 0

        n_dbg = 0

        t_cc0 = dma("sync", c_cols[:], c_in, sem="s_hl")
        for ap_sb, ap_in in ((ident32, id32_in), (identbf, idbf_in),
                             (onesrow128, onesrow128_in),
                             (onesrow512bf, onesrow512bf_in),
                             (onescolbf, onescol_bf_in),
                             (iotacol, iotacol_in),
                             (iotarow128, iotarow128_in),
                             (iotarow512, iotarow512_in),
                             (b1col, b1_in)):
            dma("sync", ap_sb[:], ap_in, sem="s_c")
        dma("sync", rows2[0:1, 0:NCORE], ecore_in, sem="s_c")
        dma("sync", rows2[0:1, 8:8 + NCORE], emask_in, sem="s_c")
        dma("sync", rows2[0:1, 16:17], dlog_in, sem="s_c")
        dma("sync", rows2[0:1, 17:18], osc_in, sem="s_c")
        dma("sync", expo_sb[:], expo_in, sem="s_c")
        t_b2 = dma("gpsimd", b2row[:], b2_in, sem="s_wp2")
        if stage >= 3:
            for ch in range(4):
                dma("gpsimd", w1bf[256 * ch:256 * (ch + 1), :],
                    w1_in[256 * ch:256 * (ch + 1), :], sem="s_wp1")
                dma("gpsimd", w2bf[1024 * ch:1024 * (ch + 1), :],
                    w2_in[1024 * ch:1024 * (ch + 1), :], sem="s_wp2")
        W_WPREP1 = full("s_wp1")
        W_WPREP2 = full("s_wp2")
        W_CONST = full("s_c")

        LND, TMP, APW, OSCC, BIA, ONE, KH0 = 0, 1, 2, 5, 6, 7, 8
        t_bia = dve(lambda e, S: e.memset(dcol[:, BIA:BIA + 1],
                                          LOG_INV_SQRT_D))
        t_one = dve(lambda e, S: e.memset(dcol[:, ONE:ONE + 1], 1.0))
        alpha_bf = sb("alpha_bf", [128, 32], BF16)
        gaefa = sb("gaefa", [128, 24], F32)

        i0, pt, g = rot_pt.next()
        t = mm(pt[:, 0:1], onesrow128[:], rows2[0:1, 16:17], True, False,
               waits=[W_CONST, g])
        t = mm(pt[:, 1:2], onesrow128[:], rows2[0:1, 17:18], False, True,
               waits=[], inc="s_pe")
        t_sg = act(lambda e, S, pt=pt: e.activation(
            dcol[:, TMP:TMP + 1], pt[:, 0:1], AF.Sigmoid),
            waits=[("s_pe", t)])
        t_oc = dve(lambda e, S, pt=pt: e.tensor_copy(
            dcol[:, OSCC:OSCC + 1], pt[:, 1:2]), waits=[("s_pe", t)])
        rot_pt.done(i0, "s_dve", t_oc)
        t_lnd = act(lambda e, S: e.activation(
            dcol[:, LND:LND + 1], dcol[:, TMP:TMP + 1], AF.Ln),
            waits=[("s_act", t_sg)])
        t_apw = None
        for p in (1, 2, 3):
            t_apw = act(lambda e, S, p=p: e.activation(
                dcol[:, APW + p - 1:APW + p], dcol[:, LND:LND + 1], AF.Exp,
                scale=float(SEG * p)), waits=[("s_act", t_lnd)])
        t_apwbf = dve(lambda e, S: e.tensor_copy(
            alpha_bf[:, APW:APW + 1], dcol[:, APW:APW + 1]),
            waits=[("s_act", t_apw)], inc="s_dve")
        W_APW = ("s_dve", t_apwbf)
        t_kh = None
        for s in range(NSEG):
            for b in range(4):
                tt = dve(lambda e, S, s=s, b=b: e.tensor_scalar(
                    dcol[:, TMP:TMP + 1], iotacol[:], 1.0,
                    float(SEG * s + 128 * b), OP.mult, OP.add),
                    waits=[W_CONST, ("s_act", t_kh if t_kh else t_lnd)])
                t_kh = act(lambda e, S, s=s, b=b: e.activation(
                    dcol[:, KH0 + 4 * s + b:KH0 + 4 * s + b + 1],
                    dcol[:, TMP:TMP + 1], AF.Exp,
                    scale=dcol[:, LND:LND + 1]), waits=[("s_dve", tt)])

        tt = dve(lambda e, S: e.tensor_scalar(
            rows[0:1, 0:512], iotarow512[:], -1.0, float(SEG - 1),
            OP.mult, OP.add), waits=[W_CONST])
        tt = act(lambda e, S: e.activation(
            rows[0:1, 512:1024], rows[0:1, 0:512], AF.Exp,
            scale=dcol[0:1, LND:LND + 1], bias=dcol[0:1, BIA:BIA + 1]),
            waits=[("s_dve", tt), ("s_act", t_lnd), ("s_dve", t_bia)])
        i0, ps, g = rot_ps.next()
        t = mm(ps[:], onesrow128[:], rows[0:1, 512:1024], True, True,
               waits=[("s_act", tt), g], inc="s_pe")
        t_bqtil = dve(lambda e, S, ps=ps: e.tensor_copy(bqtil[:], ps[:]),
                      waits=[("s_pe", t)])
        rot_ps.done(i0, "s_dve", t_bqtil)

        tt = dve(lambda e, S: e.tensor_scalar(
            rows2[0:1, 24:32], rows2[0:1, 0:8], dcol[0:1, LND:LND + 1],
            None, OP.mult), waits=[W_CONST, ("s_act", t_lnd)])
        tt = act(lambda e, S: e.activation(
            rows2[0:1, 32:40], rows2[0:1, 24:32], AF.Exp),
            waits=[("s_dve", tt)])
        tt = dve(lambda e, S: e.tensor_mul(
            rows2[0:1, 40:48], rows2[0:1, 32:40], rows2[0:1, 8:16]),
            waits=[("s_act", tt)])
        i0, ps, g = rot_ps.next()
        t = mm(ps[:, 0:8], onesrow128[:], rows2[0:1, 40:48], True, True,
               waits=[("s_dve", tt), g], inc="s_pe")
        t_wcomb = dve(lambda e, S, ps=ps: e.tensor_copy(
            derv[:, 24:32], ps[:, 0:8]), waits=[("s_pe", t)])
        rot_ps.done(i0, "s_dve", t_wcomb)

        t_dmat = None
        for b in range(4):
            eb = expo_sb[:, 512 * b:512 * (b + 1)]
            tm = dve(lambda e, S, eb=eb: e.tensor_scalar(
                tstage[:], eb, 0.0, None, OP.is_ge),
                waits=[W_CONST, ("s_dve", t_dmat)])
            tm2 = dve(lambda e, S, eb=eb: e.tensor_scalar(
                tstage2[:], eb, 0.0, None, OP.max),
                waits=[("s_dve", tm)])
            te = act(lambda e, S: e.activation(
                tstage2[:], tstage2[:], AF.Exp,
                scale=dcol[:, LND:LND + 1], bias=dcol[:, BIA:BIA + 1]),
                waits=[("s_dve", tm2), ("s_act", t_lnd), ("s_dve", t_bia)])
            t_dmat = dve(lambda e, S, b=b: e.tensor_mul(
                dmat[:, 512 * b:512 * (b + 1)], tstage2[:], tstage[:]),
                waits=[("s_act", te)])

        t_cs = act(lambda e, S: e.activation(c_cols[:], c_cols[:], AF.Silu),
                   waits=[("s_hl", t_cc0)])
        imod, pmod, gmod = rot_ps.next()
        first_mod = True
        t_mod_mm = None
        for k in range(NB):
            for ch in range(6):
                ih, half, gx = rot_xstage.next()
                tld = dma("sync", xstage[:, 1024 * half:1024 * (half + 1)],
                          wmod_in[128 * k:128 * (k + 1),
                                  1024 * ch:1024 * (ch + 1)],
                          waits=[gx] + ([gmod] if first_mod else []),
                          sem="s_xa" if half == 0 else "s_xb")
                for j2 in range(8):
                    col = 8 * ch + j2
                    last = (k == NB - 1 and col == 47)
                    t_mod_mm = mm(pmod[:, col:col + 1],
                                  xstage[:, 1024 * half + 128 * j2:
                                         1024 * half + 128 * (j2 + 1)],
                                  c_cols[:, k:k + 1],
                                  first_mod, last,
                                  waits=[("s_xa" if half == 0 else "s_xb",
                                          tld), ("s_act", t_cs)],
                                  inc="s_pe" if j2 == 7 else None)
                    first_mod = False
                rot_xstage.done(ih, "s_pe", t_mod_mm)
        t_modev = dve(lambda e, S, pmod=pmod: e.tensor_copy(
            mod_sb[:], pmod[:, 0:48]), waits=[("s_pe", t_mod_mm)])
        rot_ps.done(imod, "s_dve", t_modev)

        tv = dve(lambda e, S: e.tensor_copy(derv[:, 0:8], mod_sb[:, 0:8]),
                 waits=[("s_dve", t_modev)])
        tv = dve(lambda e, S: e.tensor_scalar(
            derv[:, 8:16], mod_sb[:, 8:16], 1.0, 1.0, OP.mult, OP.add),
            waits=[("s_dve", tv)])

        def plan_clamp(src_sl, dst_sl, tprev):
            tv = dve(lambda e, S: e.tensor_scalar(
                tstage[:, 0:8], src_sl, 0.0, -1.0, OP.is_ge, OP.add),
                waits=[("s_dve", tprev)])
            tv = dve(lambda e, S: e.tensor_scalar(
                tstage[:, 0:8], tstage[:, 0:8], 2.0, 1.0, OP.mult, OP.add),
                waits=[("s_dve", tv)])
            tv = dve(lambda e, S: e.tensor_mul(
                dst_sl, src_sl, tstage[:, 0:8]), waits=[("s_dve", tv)])
            tv = dve(lambda e, S: e.tensor_scalar(
                dst_sl, dst_sl, 0.25, None, OP.max), waits=[("s_dve", tv)])
            tv = dve(lambda e, S: e.tensor_mul(
                dst_sl, dst_sl, tstage[:, 0:8]), waits=[("s_dve", tv)])
            return tv

        tv = plan_clamp(mod_sb[:, 16:24], derv[:, 16:24], tv)
        tv = dve(lambda e, S: e.tensor_scalar(
            derv[:, 16:24], derv[:, 16:24], dcol[:, OSCC:OSCC + 1], None,
            OP.mult), waits=[("s_dve", tv), ("s_dve", t_oc)])
        tv = plan_clamp(mod_sb[:, 40:48], mod_sb[:, 40:48], tv)
        t_mod = dve(lambda e, S: e.tensor_scalar(
            mod_sb[:, 32:40], mod_sb[:, 32:40], 1.0, 1.0, OP.mult, OP.add),
            waits=[("s_dve", tv)])
        sa_c, ca1_c, gaeff_c = derv[:, 0:8], derv[:, 8:16], derv[:, 16:24]
        t_gaefa = None
        for k in range(3):
            t_gaefa = dve(lambda e, S, k=k: e.tensor_scalar(
                gaefa[:, 8 * k:8 * (k + 1)], derv[:, 16:24],
                dcol[:, APW + k:APW + k + 1], None, OP.mult),
                waits=[("s_dve", t_mod), ("s_act", t_apw)])
        wcomb_c = derv[:, 24:32]
        sb_c, cb1_c, gb_c = mod_sb[:, 24:32], mod_sb[:, 32:48 - 8], mod_sb[:, 40:48]
        n_dbg += dump("mod", mod_sb[:], [("s_dve", t_mod)])
        n_dbg += dump("derv", derv[:], [("s_dve", t_mod)])
        n_dbg += dump("dcol", dcol[:], [("s_act", t_kh), ("s_act", t_apw)])
        n_dbg += dump("dmat", dmat[:], [("s_dve", t_dmat)])

        def plan_wT(w_in_, wT_):
            tlast = None
            for ib in range(NB):
                ih, half, gx = rot_xstage.next()
                sx = "s_xa" if half == 0 else "s_xb"
                tld = dma("sync", xstage[:, 1024 * half:1024 * (half + 1)],
                          w_in_[128 * ib:128 * (ib + 1), :], waits=[gx],
                          sem=sx)
                tre = None
                for db in range(NB):
                    ipt, pt, gp = rot_pt.next()
                    t = tr(pt[:, 0:128],
                           xstage[:, 1024 * half + 128 * db:
                                  1024 * half + 128 * (db + 1)],
                           ident32[:], waits=[(sx, tld), gp])
                    te = act(lambda e, S, pt=pt, db=db, ib=ib, wT_=wT_:
                             e.activation(
                                 wT_[:, 1024 * db + 128 * ib:
                                     1024 * db + 128 * (ib + 1)],
                                 pt[:, 0:128], AF.Copy), waits=[("s_pe", t)])
                    rot_pt.done(ipt, "s_act", te)
                    tre = t
                    tlast = te
                rot_xstage.done(ih, "s_pe", tre)
            return tlast

        t_wq = plan_wT(wq_in, wqT)
        t_wk = plan_wT(wk_in, wkT)
        t_wv = plan_wT(wv_in, wvT)

        def plan_ln_stats(xf_t, tag, ready):
            isum, psum_s, gsum = rot_ps.next()
            isq, psum_q, gsq = rot_ps.next()
            t_summ = t_sqm = None
            for k in range(NB):
                i1, sl1, g1 = rot_sblk1.next()
                tc1 = dve(lambda e, S, k=k, sl1=sl1: e.tensor_copy(
                    sblk[:, 512 * sl1:512 * (sl1 + 1)],
                    xf_t[:, 512 * k:512 * (k + 1)]),
                    waits=[g1] + ready)
                t_summ = mm(psum_s[0:1, :], onescolbf[:],
                            sblk[:, 512 * sl1:512 * (sl1 + 1)],
                            k == 0, k == NB - 1,
                            waits=[("s_dve", tc1)] + ([gsum] if k == 0 else []),
                            inc="s_pe")
                rot_sblk1.done(i1, "s_pe", t_summ)
                i2, sl2, g2 = rot_sblk2.next()
                tc2 = act(lambda e, S, k=k, sl2=sl2: e.activation(
                    sblk[:, 512 * sl2:512 * (sl2 + 1)],
                    xf_t[:, 512 * k:512 * (k + 1)], AF.Square),
                    waits=[g2] + ready)
                t_sqm = mm(psum_q[0:1, :], onescolbf[:],
                           sblk[:, 512 * sl2:512 * (sl2 + 1)],
                           k == 0, k == NB - 1,
                           waits=[("s_act", tc2)] + ([gsq] if k == 0 else []),
                           inc="s_pe")
                rot_sblk2.done(i2, "s_pe", t_sqm)
                st[(tag, "c2", k)] = tc2

            R = lambda i: rows[0:1, 512 * i:512 * (i + 1)]
            t1 = dve(lambda e, S: e.tensor_scalar(
                R(0), psum_s[0:1, :], 1.0 / D, None, OP.mult),
                waits=[("s_pe", t_summ)])
            rot_ps.done(isum, "s_dve", t1)
            t2 = dve(lambda e, S: e.tensor_scalar(
                R(1), psum_q[0:1, :], 1.0 / D, None, OP.mult),
                waits=[("s_pe", t_sqm)])
            rot_ps.done(isq, "s_dve", t2)
            t3 = dve(lambda e, S: e.tensor_mul(R(2), R(0), R(0)),
                     waits=[("s_dve", t2)])
            t3 = dve(lambda e, S: e.tensor_sub(R(2), R(1), R(2)),
                     waits=[("s_dve", t3)])
            t3 = dve(lambda e, S: e.tensor_scalar(
                R(2), R(2), 1e-5, None, OP.add), waits=[("s_dve", t3)])
            t3 = dve(lambda e, S: e.reciprocal(R(3), R(2)),
                     waits=[("s_dve", t3)])
            t4 = act(lambda e, S: e.activation(R(3), R(3), AF.Sqrt),
                     waits=[("s_dve", t3)])
            t5 = dve(lambda e, S: e.tensor_mul(R(4), R(0), R(3)),
                     waits=[("s_act", t4)])
            return {"xf_t": xf_t, "tag": tag, "t4": t4, "t5": t5, "t1": t1,
                    "R": R, "xf_act": st[(tag, "c2", NB - 1)]}

        def plan_ln_apply(ctx, h_t, ca1col, sacol, h_guard):
            xf_t, tag, R = ctx["xf_t"], ctx["tag"], ctx["R"]
            ibr, ps_br, gbr = rot_ps.next()
            tba = mm(ps_br[:], onesrow128[:], R(3), True, True,
                     waits=[("s_act", ctx["t4"]), ("s_dve", ctx["t1"]), gbr],
                     inc="s_pe")
            ibm, ps_bm, gbm = rot_ps.next()
            tbb = mm(ps_bm[:], onesrow128[:], R(4), True, True,
                     waits=[("s_dve", ctx["t5"]), gbm], inc="s_pe")
            th = None
            tu_last = None
            for k in range(NB):
                tu = dve(lambda e, S, k=k, ps_br=ps_br: e.tensor_mul(
                    tstage[:], xf_t[:, 512 * k:512 * (k + 1)], ps_br[:]),
                    waits=[("s_pe", tba)])
                tu_last = tu
                tn = dve(lambda e, S, ps_bm=ps_bm: e.tensor_sub(
                    tstage2[:], tstage[:], ps_bm[:]),
                    waits=[("s_pe", tbb), ("s_dve", tu)])
                th = dve(lambda e, S, k=k: e.tensor_scalar(
                    h_t[:, 512 * k:512 * (k + 1)], tstage2[:],
                    ca1col[:, k:k + 1], sacol[:, k:k + 1], OP.mult, OP.add),
                    waits=[("s_dve", tn), ("s_dve", t_mod)] +
                          ([h_guard] if k == 0 else []))
                st[(tag, "h", k)] = th
            rot_ps.done(ibr, "s_dve", th)
            rot_ps.done(ibm, "s_dve", th)
            return {"h_last": th, "xf_dve": tu_last,
                    "xf_act": ctx["xf_act"]}

        def plan_ln(xf_t, h_t, ca1col, sacol, ready, tag, h_guard):
            ctx = plan_ln_stats(xf_t, tag, ready)
            return plan_ln_apply(ctx, h_t, ca1col, sacol, h_guard)

        def p1_x_ln(s):
            xf_s = xf if s % 2 == 0 else xf2
            prev2 = st.get(("p1ln", s + 2))
            t_xf = None
            for b in range(4):
                ih, half, gx = rot_xstage.next()
                xw = [gx]
                if s == NSEG - 1 and b == 0:
                    xw.append(("s_act", t_wv))
                if prev2 is not None and b == 0:
                    xw += [("s_dve", prev2["xf_dve"]),
                           ("s_act", prev2["xf_act"]),
                           ("s_xfa" if s % 2 == 0 else "s_xfb",
                            st.get(("xfsp", s + 2)))]
                sx = "s_xa" if half == 0 else "s_xb"
                tld = dma("sync", xstage[:, 1024 * half:1024 * (half + 1)],
                          x_in[512 * s + 128 * b:512 * s + 128 * (b + 1), :],
                          waits=xw, sem=sx)
                tre = None
                for db in range(NB):
                    ipt, pt, gp = rot_pt.next()
                    t = tr(pt[:, 0:128],
                           xstage[:, 1024 * half + 128 * db:
                                  1024 * half + 128 * (db + 1)],
                           ident32[:], waits=[(sx, tld), gp])
                    te = act(lambda e, S, pt=pt, db=db, b=b, xf_s=xf_s:
                             e.activation(
                        xf_s[:, 512 * db + 128 * b:512 * db + 128 * (b + 1)],
                        pt[:, 0:128], AF.Copy), waits=[("s_pe", t)])
                    rot_pt.done(ipt, "s_act", te)
                    tre = t
                    t_xf = te
                rot_xstage.done(ih, "s_pe", tre)
            ctx = plan_ln_stats(xf_s, ("l1", s), [("s_act", t_xf)])
            st[("p1ctx", s)] = ctx
            st[("xfsp", s)] = dma("gpsimd", xf_sp[s], xf_s[:],
                                  waits=[("s_act", t_xf)],
                                  sem="s_xfa" if s % 2 == 0 else "s_xfb")

        def p1_apply(s):
            lnres = plan_ln_apply(st[("p1ctx", s)], hseg, ca1_c, sa_c,
                                  h_guard=("s_pe",
                                           st.get(("vmm_last", s + 1))))
            st[("p1ln", s)] = lnres

        if stage >= 1:
            p1_x_ln(NSEG - 1)
            p1_apply(NSEG - 1)
        for s in (range(NSEG - 1, -1, -1) if stage >= 1 else []):
            lnres = st[("p1ln", s)]

            for tag, wT_, dest, spill in (("q", wqT, qseg, q_sp),
                                          ("k", wkT, kseg, k_sp)):
                tev = None
                spsem = "s_qsp" if tag == "q" else "s_ksp"
                for ib in range(NB):
                    ips, ps, gp = rot_ps.next()
                    w0 = [gp, (spsem, st.get((tag + "sp", s + 1)))]
                    t = None
                    for db in range(NB):
                        t = mm(ps[:],
                               wT_[:, 1024 * db + 128 * ib:
                                   1024 * db + 128 * (ib + 1)],
                               hseg[:, 512 * db:512 * (db + 1)],
                               db == 0, db == NB - 1,
                               waits=[("s_dve", st[(("l1", s), "h", db)])] +
                                     (w0 if db == 0 else []),
                               inc="s_pe" if db == NB - 1 else None)
                    tev = dve(lambda e, S, ps=ps, ib=ib, dest=dest:
                              e.tensor_copy(
                                  dest[:, 512 * ib:512 * (ib + 1)], ps[:]),
                              waits=[("s_pe", t)])
                    rot_ps.done(ips, "s_dve", tev)
                    st[(tag + "ev", s, ib)] = tev
                st[(tag + "sp", s)] = dma("gpsimd", spill[s], dest[:],
                                          waits=[("s_dve", tev)], sem=spsem)
            tvl = None
            for b in range(4):
                for eh in range(2):
                    ips, ps, gp = rot_ps.next()
                    t = None
                    for db in range(NB):
                        t = mm(ps[:],
                               hseg[:, 512 * db + 128 * b:
                                    512 * db + 128 * (b + 1)],
                               wvT[:, 1024 * db + 512 * eh:
                                   1024 * db + 512 * (eh + 1)],
                               db == 0, db == NB - 1,
                               waits=[("s_dve", st[(("l1", s), "h", NB - 1)])]
                                     + ([gp, ("s_vsp", st.get(("vsp", s + 1)))]
                                        if db == 0 else []),
                               inc="s_pe" if db == NB - 1 else None)
                    tev = dve(lambda e, S, ps=ps, b=b, eh=eh: e.tensor_copy(
                        vseg[:, 1024 * b + 512 * eh:1024 * b + 512 * (eh + 1)],
                        ps[:]), waits=[("s_pe", t)])
                    rot_ps.done(ips, "s_dve", tev)
                    tvl = tev
            st[("vmm_last", s)] = t
            st[("vev_last", s)] = tvl
            st[("vsp", s)] = dma("gpsimd", v_sp[s], vseg[:],
                                 waits=[("s_dve", tvl)], sem="s_vsp")

            if s - 1 >= 0:
                p1_x_ln(s - 1)

            tkh = None
            for b in range(4):
                for ib in range(NB):
                    ipk, pk, gp = rot_pk.next()
                    t = tr(pk[:],
                           kseg[:, 512 * ib + 128 * b:512 * ib + 128 * (b + 1)],
                           identbf[:],
                           waits=[("s_dve", st[("kev", s, ib)]), gp])
                    tt1 = dve(lambda e, S, pk=pk, b=b, ib=ib: e.tensor_scalar(
                        ktil[:, 1024 * b + 128 * ib:1024 * b + 128 * (ib + 1)],
                        pk[:], dcol[:, KH0 + b:KH0 + b + 1], None, OP.mult),
                        waits=[("s_pe", t), ("s_act", t_kh),
                               ("s_pe", st.get(("gmm_last", s + 1)))])
                    rot_pk.done(ipk, "s_dve", tt1)
                    tkh = tt1
            st[("kh_done", s)] = tkh

            tgm = None
            tnl = None
            for ib in range(NB):
                slots = [rot_ps.next(), rot_ps.next()]
                tstop = []
                for eh in range(2):
                    ipg, pg, gp = slots[eh]
                    t = None
                    for b in range(4):
                        t = mm(pg,
                               ktil[:, 1024 * b + 128 * ib:
                                    1024 * b + 128 * (ib + 1)],
                               vseg[:, 1024 * b + 512 * eh:
                                    1024 * b + 512 * (eh + 1)],
                               b == 0, b == 3,
                               waits=[("s_dve", tkh),
                                      ("s_dve", st[("vev_last", s)])] +
                                     ([gp] if b == 0 else []),
                               inc="s_pe" if b == 3 else None)
                    tstop.append(t)
                tgm = t
                for eh in range(2):
                    sl = nscan_sb[:, 1024 * ib + 512 * eh:
                                  1024 * ib + 512 * (eh + 1)]
                    spw = ("s_gsp0" if (s + 1) % 2 == 0 else "s_gsp1",
                           st.get(("nsp", s + 1)))
                    if s == NSEG - 1:
                        tnl = dve(lambda e, S, pg=slots[eh][1], sl=sl:
                                  e.tensor_copy(sl, pg),
                                  waits=[("s_pe", tstop[eh])])
                    else:
                        tnl = dve(lambda e, S, pg=slots[eh][1], sl=sl:
                                  e.scalar_tensor_tensor(
                                      sl, sl, alpha_bf[:, APW:APW + 1],
                                      pg, OP.mult, OP.add),
                                  waits=[("s_pe", tstop[eh]), W_APW, spw])
                    rot_ps.done(slots[eh][0], "s_dve", tnl)
            st[("gmm_last", s)] = tgm
            st[("nsc", s)] = tnl
            if s >= 1:
                st[("nsp", s)] = dma(
                    "gpsimd", g_sp[s], nscan_sb[:],
                    waits=[("s_dve", tnl)],
                    sem="s_gsp0" if s % 2 == 0 else "s_gsp1")
            if s - 1 >= 0:
                p1_apply(s - 1)

        t_p1_dve = (st[("nsc", 0)] if stage >= 1 else None)
        if stage >= 1:
            n_dbg += dump("ghat", nscan_sb[:], [("s_dve", t_p1_dve)])

        t_agd = None
        for ib in range(NB if stage >= 2 else 0):
            t_agd = dma("gpsimd", ag_in[ib],
                        nscan_sb[:, 1024 * ib:1024 * (ib + 1)],
                        waits=[("s_dve", t_p1_dve)],
                        sem="s_agin")
        t_cc = (P.emit("gpsimd",
                       lambda e, S: e.collective_compute(
                           "AllGather", OP.bypass,
                           replica_groups=[list(range(NCORE))],
                           ins=[ag_in], outs=[ag_out]),
                       waits=[full("s_agin")], inc="s_cc")
                if stage >= 2 else None)

        tlast = None
        for ib in range(NB if stage >= 2 else 0):
            ih, half, gx = rot_xstage.next()
            sx = "s_xa" if half == 0 else "s_xb"
            tld = dma("sync", xstage[:, 1024 * half:1024 * (half + 1)],
                      wo_in[128 * ib:128 * (ib + 1), :],
                      waits=[gx, ("s_dve", t_p1_dve)], sem=sx)
            tre = None
            for db in range(NB):
                ipt, pt, gp = rot_pt.next()
                t = tr(pt[:, 0:128],
                       xstage[:, 1024 * half + 128 * db:
                              1024 * half + 128 * (db + 1)],
                       ident32[:], waits=[(sx, tld), gp])
                te = act(lambda e, S, pt=pt, db=db, ib=ib: e.activation(
                    woT[:, 1024 * db + 128 * ib:1024 * db + 128 * (ib + 1)],
                    pt[:, 0:128], AF.Copy), waits=[("s_pe", t)])
                rot_pt.done(ipt, "s_act", te)
                tre = t
                tlast = te
            rot_xstage.done(ih, "s_pe", tre)
        t_wo = tlast

        for s in (range(NSEG) if stage >= 2 else []):
            prv = s - 1
            if s == 0:
                t_qld = t_kld = None
                W_Q = ("s_dve", st[("qev", 0, NB - 1)])
                W_K = ("s_dve", st[("kev", 0, NB - 1)])
            else:
                t_qld = dma("sync", qseg[:], q_sp[s],
                            waits=[full("s_qsp"),
                                   ("s_pe", st.get(("p2score", prv))),
                                   ("s_dve", st.get(("p2qtt", prv)))],
                            sem="s_ql")
                t_kld = dma("sync", kseg[:], k_sp[s],
                            waits=[full("s_ksp"),
                                   ("s_pe", st.get(("p2score", prv))),
                                   ("s_dve", t_p1_dve)], sem="s_kl")
                W_Q = ("s_ql", t_qld)
                W_K = ("s_kl", t_kld)
            if s < NSEG - 1:
                tnb = dma("sync", nbuf[:], g_sp[s + 1],
                          waits=[("s_gsp0" if (s + 1) % 2 == 0 else "s_gsp1",
                                  st[("nsp", s + 1)]),
                                 ("s_dve", t_p1_dve),
                                 ("s_pe", st.get(("p2ret", prv)))],
                          sem="s_nb")
                nw = ("s_nb", tnb)
            else:
                nw = None

            if s == 0:
                W_V = ("s_dve", st[("vev_last", 0)])
            else:
                t_vld = dma("sync", vseg[:], v_sp[s],
                            waits=[full("s_vsp"),
                                   ("s_pe", st.get(("p2ret", prv)))],
                            sem="s_vl")
                W_V = ("s_vl", t_vld)

            scT_s = scT if s % 2 == 0 else scT2
            t_scev = {}
            tsc_mm = None
            for b in range(4):
                ips, ps, gp = rot_ps.next()
                t = None
                for ib in range(NB):
                    t = mm(ps[:],
                           kseg[:, 512 * ib + 128 * b:512 * ib + 128 * (b + 1)],
                           qseg[:, 512 * ib:512 * (ib + 1)],
                           ib == 0, ib == NB - 1,
                           waits=[W_K, W_Q] +
                                 ([gp] if ib == 0 else []),
                           inc="s_pe" if ib == NB - 1 else None)
                tsc_mm = t
                tev = dve(lambda e, S, ps=ps, b=b, scT_s=scT_s: e.tensor_mul(
                    scT_s[:, 512 * b:512 * (b + 1)], ps[:],
                    dmat[:, 512 * b:512 * (b + 1)]),
                    waits=[("s_pe", t), ("s_dve", t_dmat),
                           ("s_pe", st.get(("p2ret", s - 2))),
                           ("s_dve", t_p1_dve)])
                rot_ps.done(ips, "s_dve", tev)
                t_scev[b] = tev
            st[("p2score", s)] = tsc_mm

            t_qt = None
            for ib in range(NB):
                t_qt = dve(lambda e, S, ib=ib: e.tensor_mul(
                    qtil[:, 512 * ib:512 * (ib + 1)],
                    qseg[:, 512 * ib:512 * (ib + 1)], bqtil[:]),
                    waits=[W_Q, ("s_dve", t_bqtil),
                           ("s_dve", t_p1_dve),
                           ("s_pe", st.get(("p2ret", prv))),
                           ("s_qtsp", st.get(("qtsp", prv)))])
            st[("p2qtt", s)] = t_qt
            st[("qtsp", s)] = dma("gpsimd", qt_sp[s], qtil[:],
                                  waits=[("s_dve", t_qt)], sem="s_qtsp")

            t_ret = None
            tevl = None
            for j in range(NB):
                ips, ps, gp = rot_ps.next()
                t = None
                for b in range(4):
                    t = mm(ps[:],
                           vseg[:, 1024 * b + 128 * j:1024 * b + 128 * (j + 1)],
                           scT_s[:, 512 * b:512 * (b + 1)],
                           b == 0, (b == 3 and s == NSEG - 1),
                           waits=[W_V, ("s_dve", t_scev[b])] +
                                 ([gp] if b == 0 else []),
                           inc="s_pe" if (b == 3 and s == NSEG - 1) else None)
                if s < NSEG - 1:
                    for ib in range(NB):
                        t = mm(ps[:],
                               nbuf[:, 1024 * ib + 128 * j:
                                    1024 * ib + 128 * (j + 1)],
                               qtil[:, 512 * ib:512 * (ib + 1)],
                               False, ib == NB - 1,
                               waits=[nw, ("s_dve", t_qt)],
                               inc="s_pe" if ib == NB - 1 else None)
                t_ret = t
                tev = dve(lambda e, S, ps=ps, j=j: e.tensor_copy(
                    retrev[:, 512 * j:512 * (j + 1)], ps[:]),
                    waits=[("s_pe", t),
                           ("s_rsp", st.get(("rsp", prv)))])
                rot_ps.done(ips, "s_dve", tev)
                tevl = tev
            st[("p2ret", s)] = t_ret
            st[("rsp", s)] = dma("gpsimd", retr_sp[s], retrev[:],
                                 waits=[("s_dve", tevl)], sem="s_rsp")

        t_sc = None
        pair_guard = {}
        for p in range(4 if stage >= 2 else 0):
            tacc0 = tacc1 = None
            for j in range(NCORE):
                sl = j % 4
                sag = ("s_aga", "s_agb", "s_age", "s_agf")[sl]
                tld = dma("gpsimd", agstage2[:, 2048 * sl:2048 * sl + 1024],
                          ag_out[NB * j + 2 * p],
                          waits=[("s_cc", t_cc), full("s_dbg"),
                                 ("s_dve", pair_guard.get(sl))],
                          sem=sag)
                tld = dma("gpsimd", agstage2[:, 2048 * sl + 1024:
                                             2048 * (sl + 1)],
                          ag_out[NB * j + 2 * p + 1],
                          waits=[], sem=sag)
                for h in range(2):
                    stg = agstage2[:, 2048 * sl + 1024 * h:
                                   2048 * sl + 1024 * (h + 1)]
                    ib = 2 * p + h
                    dst = sc_sb[:, 1024 * ib:1024 * (ib + 1)]
                    if j == 0:
                        tacc = dve(lambda e, S, stg=stg, dst=dst:
                                   e.tensor_scalar(dst, stg,
                                                   wcomb_c[:, 0:1], None,
                                                   OP.mult),
                                   waits=[(sag, tld), ("s_dve", t_wcomb),
                                          ("s_dve", t_p1_dve)])
                    else:
                        tacc = dve(lambda e, S, stg=stg, dst=dst, j=j:
                                   e.scalar_tensor_tensor(
                                       dst, stg, wcomb_c[:, j:j + 1], dst,
                                       OP.mult, OP.add),
                                   waits=[(sag, tld),
                                          ("s_dve", tacc0 if h == 0
                                           else tacc1)])
                    if h == 0:
                        tacc0 = tacc
                    else:
                        tacc1 = tacc
                pair_guard[sl] = tacc1
            st[("scdone", 2 * p)] = tacc0
            st[("scdone", 2 * p + 1)] = tacc1
            t_sc = tacc1
        if stage >= 2:
            n_dbg += dump("sc", sc_sb[:], [("s_dve", t_sc)])

        def p2b_x(s):
            xf_s = xf if s % 2 == 0 else xf2
            prev2 = st.get(("p2ln", s - 2))
            xw = [("s_xfa" if s % 2 == 0 else "s_xfb",
                   st[("xfsp", s)])]
            if prev2 is not None:
                xw += [("s_dve", prev2["xf_dve"]),
                       ("s_act", prev2["xf_act"]),
                       ("s_x2sp" if s % 2 == 0 else "s_x2b2",
                        st.get(("x2sp", s - 2)))]
            else:
                p1l = st.get(("p1ln", s))
                if p1l is not None:
                    xw += [("s_dve", p1l["xf_dve"]),
                           ("s_act", p1l["xf_act"])]
            t_xl = dma("sync", xf_s[:], xf_sp[s], waits=xw,
                       sem="s_xla" if s % 2 == 0 else "s_xlb")
            st[("p2bx", s)] = (xf_s, t_xl)

        for s in (range(NSEG) if stage >= 2 else []):
            prv = s - 1
            qtil_s = qtil if s % 2 == 0 else qtil2
            rstr_s = rstr if s % 2 == 0 else rstr2
            retrev_s = retrev if s % 2 == 0 else retrevb
            sql = "s_ql" if s % 2 == 0 else "s_ql2"
            srl = "s_rl" if s % 2 == 0 else "s_rl2"
            alias_guards = ([("s_pe", st.get(("p2ret", NSEG - 1))),
                            ("s_pe", st.get(("p2score", NSEG - 1)))]
                           if s < 2 else [])
            t_qtl = dma("sync", qtil_s[:], qt_sp[s],
                        waits=[full("s_qtsp"),
                               ("s_pe", st.get(("p2bS", s - 2)))] +
                              alias_guards,
                        sem=sql)
            t_rld = dma("sync", rstr_s[:], retr_sp[s],
                        waits=[full("s_rsp"),
                               ("s_dve", st.get(("bev", s - 2)))] +
                              alias_guards,
                        sem=srl)
            if s < 2:
                p2b_x(s)
            xf_s, t_xl = st[("p2bx", s)]
            W_XF = ("s_xla" if s % 2 == 0 else "s_xlb", t_xl)

            apcol = (dcol[:, ONE:ONE + 1] if s == NSEG - 1 else
                     dcol[:, APW + (NSEG - 2 - s):APW + (NSEG - 1 - s)])
            tbs = None
            tbev = None
            for j in range(NB):
                ips, ps, gp = rot_ps.next()
                t = None
                for ib in range(NB):
                    t = mm(ps[:],
                           sc_sb[:, 1024 * ib + 128 * j:
                                 1024 * ib + 128 * (j + 1)],
                           qtil_s[:, 512 * ib:512 * (ib + 1)],
                           ib == 0, ib == NB - 1,
                           waits=[("s_dve", st[("scdone", ib)]),
                                  (sql, t_qtl)] +
                                 ([gp] if ib == 0 else []),
                           inc="s_pe" if ib == NB - 1 else None)
                tbs = t
                tev = dve(lambda e, S, ps=ps, j=j, apcol=apcol,
                          rstr_s=rstr_s, retrev_s=retrev_s:
                          e.scalar_tensor_tensor(
                              retrev_s[:, 512 * j:512 * (j + 1)], ps[:],
                              apcol, rstr_s[:, 512 * j:512 * (j + 1)],
                              OP.mult, OP.add),
                          waits=[("s_pe", t), (srl, t_rld),
                                 ("s_act", t_apw), ("s_dve", t_one),
                                 ("s_pe", st.get(("p2wo", s - 2))),
                                 ("s_pe", st.get(("p2ret", NSEG - 1))),
                                 ("s_rsp", st.get(("rsp", NSEG - 1)))])
                rot_ps.done(ips, "s_dve", tev)
                st[("rev", s, j)] = tev
                tbev = tev
            st[("p2bS", s)] = tbs
            st[("bev", s)] = tbev

            t_wo_mm = None
            tx2 = None
            for o in range(NB):
                ips, ps, gp = rot_ps.next()
                t = None
                for eb in range(NB):
                    t = mm(ps[:],
                           woT[:, 1024 * eb + 128 * o:1024 * eb + 128 * (o + 1)],
                           retrev_s[:, 512 * eb:512 * (eb + 1)],
                           eb == 0, eb == NB - 1,
                           waits=[("s_dve", st[("rev", s, eb)]),
                                  ("s_act", t_wo)] + ([gp] if eb == 0 else []),
                           inc="s_pe" if eb == NB - 1 else None)
                t_wo_mm = t
                tx2 = dve(lambda e, S, ps=ps, o=o, xf_s=xf_s:
                          e.scalar_tensor_tensor(
                    xf_s[:, 512 * o:512 * (o + 1)], ps[:],
                    gaeff_c[:, o:o + 1], xf_s[:, 512 * o:512 * (o + 1)],
                    OP.mult, OP.add),
                    waits=[("s_pe", t), ("s_dve", t_mod), W_XF])
                rot_ps.done(ips, "s_dve", tx2)
            st[("p2wo", s)] = t_wo_mm
            st[("x2w", s)] = tx2
            st[("x2sp", s)] = dma("gpsimd", x2_sp[s], xf_s[:],
                                  waits=[("s_dve", tx2)],
                                  sem="s_x2sp" if s % 2 == 0 else "s_x2b2")

            lnres2 = plan_ln(xf_s, hseg, cb1_c, sb_c,
                             [("s_dve", tx2)], ("l2", s),
                             h_guard=("s_h2sp", st.get(("h2sp", prv)))
                             if s > 0 else
                             ("s_pe", st.get(("vmm_last", 0))))
            st[("p2ln", s)] = lnres2
            st[("h2sp", s)] = dma("gpsimd", h2_sp[s], hseg[:],
                                  waits=[("s_dve", lnres2["h_last"])],
                                  sem="s_h2sp")
            if s + 2 <= NSEG - 1:
                p2b_x(s + 2)

        W_P2 = ([full("s_x2sp"), full("s_x2b2"), full("s_h2sp"),
                 full("s_dbg")] if stage >= 3 else [])
        W_W1G = []
        if stage >= 3:
            for wq in range(4):
                sw = ("s_w1", "s_w1b", "s_w1c", "s_w1d")[wq]
                for db in range(NB):
                    dma("gpsimd",
                        w1sb[:, FW * db + 1024 * wq:FW * db + 1024 * (wq + 1)],
                        w1bf[128 * db:128 * (db + 1),
                             1024 * wq:1024 * (wq + 1)],
                        waits=W_P2 + [W_WPREP1], sem=sw)
                W_W1G.append(full(sw))
        for wb in range(NWB if stage >= 3 else 0):
            dma("gpsimd",
                w2sb[:, 1024 * wb:1024 * (wb + 1)],
                w2bf[128 * wb:128 * (wb + 1), :],
                waits=W_P2 + [W_WPREP2],
                sem="s_w2" if wb % 2 == 0 else "s_w2b")
        W_W2 = full("s_w2")
        W_W2B = full("s_w2b")

        n_out = 0
        for s in range(NSEG if stage >= 3 else 0):
            t_h2l = dma("sync", h2c[:], h2_sp[s],
                        waits=W_P2 + [("s_pe", st.get(("ffnmid", s - 1)))],
                        sem="s_hl")
            t_x2l = dma("sync", x2c[:], x2_sp[s],
                        waits=W_P2 + [("s_dve", st.get(("ffnx3", s - 1))),
                                      ("s_pe", st.get(("p3tr", s - 1)))],
                        sem="s_x2l")
            tmid = None
            for wb in range(NWB):
                ips, ps, gp = rot_ps.next()
                t = None
                for db in range(NB):
                    t = mm(ps[:],
                           w1sb[:, FW * db + 128 * wb:FW * db + 128 * (wb + 1)],
                           h2c[:, 512 * db:512 * (db + 1)],
                           db == 0, db == NB - 1,
                           waits=[("s_hl", t_h2l), W_W1G[wb // 8]] +
                                 ([gp] if db == 0 else []),
                           inc="s_pe" if db == NB - 1 else None)
                tmid = t
                te = act(lambda e, S, ps=ps, wb=wb: e.activation(
                    midc[:, 512 * wb:512 * (wb + 1)], ps[:], AF.Relu,
                    bias=b1col[:, wb:wb + 1]),
                    waits=[("s_pe", t), ("s_pe", st.get(("ffnout", s - 1)))])
                rot_ps.done(ips, "s_act", te)
                st[("ffnme", s, wb)] = te
            st[("ffnmid", s)] = tmid
            tout = None
            tx3 = None
            for o in range(NB):
                ipg, po, gp = rot_ps.next()
                t = mm(po, b2row[0:1, 128 * o:128 * (o + 1)],
                       onesrow512bf[:], True, False,
                       waits=[W_W2, W_W2B, W_WPREP2, gp, W_CONST])
                for wb in range(NWB):
                    t = mm(po,
                           w2sb[:, 1024 * wb + 128 * o:1024 * wb + 128 * (o + 1)],
                           midc[:, 512 * wb:512 * (wb + 1)],
                           False, wb == NWB - 1,
                           waits=[("s_act", st[("ffnme", s, wb)]), W_W2],
                           inc="s_pe" if wb == NWB - 1 else None)
                tout = t
                tx3 = dve(lambda e, S, po=po, o=o: e.scalar_tensor_tensor(
                    x2c[:, 512 * o:512 * (o + 1)], po,
                    gb_c[:, o:o + 1], x2c[:, 512 * o:512 * (o + 1)],
                    OP.mult, OP.add),
                    waits=[("s_pe", t), ("s_x2l", t_x2l), ("s_dve", t_mod)])
                rot_ps.done(ipg, "s_dve", tx3)
                st[("x3", s, o)] = tx3
            st[("ffnout", s)] = tout
            st[("ffnx3", s)] = tx3
            tp3 = None
            for b in range(4):
                io, osl, go = rot_ostage.next()
                tev = None
                for o in range(NB):
                    ipt, pt, gp = rot_pt.next()
                    t = tr(pt[:, 0:128],
                           x2c[:, 512 * o + 128 * b:512 * o + 128 * (b + 1)],
                           ident32[:],
                           waits=[("s_dve", st[("x3", s, o)]), gp] +
                                 ([go] if o == 0 else []))
                    tp3 = t
                    tev = act(lambda e, S, pt=pt, o=o, osl=osl: e.activation(
                        ostage[:, 1024 * osl + 128 * o:
                               1024 * osl + 128 * (o + 1)],
                        pt[:, 0:128], AF.Copy), waits=[("s_pe", t)])
                    rot_pt.done(ipt, "s_act", tev)
                so = "s_oa" if osl == 0 else "s_ob"
                tod = dma("sync",
                          out_dram[512 * s + 128 * b:512 * s + 128 * (b + 1), :],
                          ostage[:, 1024 * osl:1024 * (osl + 1)],
                          waits=[("s_act", tev)], sem=so)
                rot_ostage.done(io, so, tod)
                n_out += 1
            st[("p3tr", s)] = tp3

        P.emit("sync", lambda e, S: e.nop(),
               waits=[full("s_oa"), full("s_ob"), full("s_dbg")])

        with nc.Block() as block:
            def mk(ename):
                def body(eng):
                    for fn, waits, inc, amt in P.seq[ename]:
                        for semn, val in waits:
                            eng.wait_ge(sems[semn], val)
                        inst = fn(eng, sems)
                        if inc is not None:
                            inst.then_inc(sems[inc], amt)
                return body
            for ename in ENGINES:
                if P.seq[ename]:
                    getattr(block, ename)(mk(ename))

    return nc


_CACHE = {}


def _get_nc(debug=(), stage=3):
    key = (tuple((n, tuple(s), d) for n, s, d in debug), stage)
    if key not in _CACHE:
        _CACHE[key] = build_program(debug, stage)
    return _CACHE[key]


def _consts():
    bf = ml_dtypes.bfloat16
    return {
        "ident_f32": np.eye(128, dtype=np.float32),
        "ident_bf16": np.eye(128, dtype=np.float32).astype(bf),
        "ones_row128": np.ones((1, 128), np.float32),
        "ones_row512_bf": np.ones((1, 512), np.float32).astype(bf),
        "ones_col_bf": np.ones((128, 1), np.float32).astype(bf),
        "iota_col": np.arange(128, dtype=np.float32).reshape(128, 1),
        "iota_row128": np.arange(128, dtype=np.float32).reshape(1, 128),
        "iota_row512": np.arange(512, dtype=np.float32).reshape(1, 512),
        "expo_t": np.concatenate(
            [(128 * b + np.arange(128)[:, None]
              - np.arange(512)[None, :] - 1.0).astype(np.float32)
             for b in range(4)], axis=1),
    }


def kernel(L0_attn_x, L0_attn_Wq, L0_attn_Wk, L0_attn_Wv, L0_attn_Wo,
           L0_attn_decay_logit, L0_attn_out_scale,
           L0_ffn_w1, L0_ffn_b1, L0_ffn_w2, L0_ffn_b2, L0_c, L0_wmod,
           _debug=(), _stage=3):
    nc = _get_nc(_debug, _stage)
    x2d = np.ascontiguousarray(np.asarray(L0_attn_x, np.float32)
                               .reshape(T, D))
    common = dict(
        wq=np.ascontiguousarray(np.asarray(L0_attn_Wq, np.float32)),
        wk=np.ascontiguousarray(np.asarray(L0_attn_Wk, np.float32)),
        wv=np.ascontiguousarray(np.asarray(L0_attn_Wv, np.float32)),
        wo=np.ascontiguousarray(np.asarray(L0_attn_Wo, np.float32)),
        w1=np.ascontiguousarray(np.asarray(L0_ffn_w1, np.float32)),
        w2=np.ascontiguousarray(np.asarray(L0_ffn_w2, np.float32)),
        b1col=np.ascontiguousarray(
            np.asarray(L0_ffn_b1, np.float32).reshape(NWB, 128).T),
        b2row=np.ascontiguousarray(
            np.asarray(L0_ffn_b2, np.float32).reshape(1, D)),
        c_cols=np.ascontiguousarray(
            np.asarray(L0_c, np.float32).reshape(NB, 128).T),
        wmod=np.ascontiguousarray(np.asarray(L0_wmod, np.float32)),
        decay_logit=np.asarray(L0_attn_decay_logit,
                               np.float32).reshape(1, 1),
        out_scale=np.asarray(L0_attn_out_scale, np.float32).reshape(1, 1),
        **_consts(),
    )
    in_maps = []
    for c in range(NCORE):
        ecore = np.zeros((1, NCORE), np.float32)
        emask = np.zeros((1, NCORE), np.float32)
        for j in range(NCORE):
            if j > c:
                ecore[0, j] = float(TC * (j - c - 1))
                emask[0, j] = 1.0
        m = dict(common)
        m["x"] = np.ascontiguousarray(x2d[TC * c:TC * (c + 1)])
        m["ecore"] = ecore
        m["emask"] = emask
        in_maps.append(m)
    res = run_bass_kernel_spmd(nc, in_maps, core_ids=list(range(NCORE)))
    out = np.concatenate([res.results[c]["out"] for c in range(NCORE)],
                         axis=0)
    if _debug:
        return out.reshape(1, T, D), list(res.results)
    return out.reshape(1, T, D)


# version 18
# speedup vs baseline: 1.0931x; 1.0204x over previous
import numpy as np
import ml_dtypes
from contextlib import ExitStack

import concourse.bass as bass
import concourse.mybir as mybir
from concourse.bass_utils import run_bass_kernel_spmd

F32 = mybir.dt.float32
BF16 = mybir.dt.bfloat16
AF = mybir.ActivationFunctionType
OP = mybir.AluOpType

T = 16384
D = 1024
FW = 4096
NCORE = 8
TC = T // NCORE
SEG = 512
NSEG = TC // SEG
NB = D // 128
NWB = FW // 128
LOG_INV_SQRT_D = float(np.log(1.0 / np.sqrt(D)))

ENGINES = ("sync", "tensor", "vector", "scalar", "gpsimd")


class Plan:

    def __init__(self):
        self.seq = {e: [] for e in ENGINES}
        self.cnt = {}
        self.last_wait = {e: {} for e in ENGINES}

    def emit(self, eng, fn, waits=(), inc=None, amt=1):
        w = []
        lw = self.last_wait[eng]
        for item in waits:
            if item is None:
                continue
            sem, val = item
            if val is None:
                continue
            if val > lw.get(sem, 0):
                lw[sem] = val
                w.append((sem, val))
        t = None
        if inc is not None:
            self.cnt[inc] = self.cnt.get(inc, 0) + amt
            t = self.cnt[inc]
        self.seq[eng].append((fn, tuple(w), inc, amt))
        return t


class Rot:

    def __init__(self, slots):
        self.slots = list(slots)
        self.n = 0
        self.last = {}

    def next(self):
        i = self.n % len(self.slots)
        self.n += 1
        return i, self.slots[i], self.last.get(i)

    def done(self, i, sem, ticket):
        self.last[i] = (sem, ticket)


def build_program(debug=(), stage=3):
    nc = bass.Bass("TRN2", target_bir_lowering=False, debug=False)

    def ein(name, shape, dt=F32):
        return nc.dram_tensor(name, shape, dt, kind="ExternalInput").ap()

    x_in = ein("x", [TC, D])
    wq_in = ein("wq", [D, D])
    wk_in = ein("wk", [D, D])
    wv_in = ein("wv", [D, D])
    wo_in = ein("wo", [D, D])
    w1_in = ein("w1", [D, FW])
    b1_in = ein("b1col", [128, NWB])
    w2_in = ein("w2", [FW, D])
    b2_in = ein("b2row", [1, D])
    c_in = ein("c_cols", [128, NB])
    wmod_in = ein("wmod", [D, 6 * D])
    dlog_in = ein("decay_logit", [1, 1])
    osc_in = ein("out_scale", [1, 1])
    id32_in = ein("ident_f32", [128, 128])
    idbf_in = ein("ident_bf16", [128, 128], BF16)
    onesrow128_in = ein("ones_row128", [1, 128])
    onesrow512bf_in = ein("ones_row512_bf", [1, 512], BF16)
    onescol_bf_in = ein("ones_col_bf", [128, 1], BF16)
    iotacol_in = ein("iota_col", [128, 1])
    iotarow128_in = ein("iota_row128", [1, 128])
    iotarow512_in = ein("iota_row512", [1, 512])
    ecore_in = ein("ecore", [1, NCORE])
    emask_in = ein("emask", [1, NCORE])
    expo_in = ein("expo_t", [128, 4 * SEG])

    out_dram = nc.dram_tensor("out", [TC, D], F32, kind="ExternalOutput").ap()

    def scr(name, shape, dt):
        return nc.dram_tensor(name, shape, dt, kind="Internal").ap()

    q_sp = scr("q_sp", [NSEG, 128, NB * SEG], BF16)
    k_sp = scr("k_sp", [NSEG, 128, NB * SEG], BF16)
    v_sp = scr("v_sp", [NSEG, 128, NSEG * D], BF16)
    g_sp = scr("g_sp", [NSEG, 128, NB * D], BF16)
    x2_sp = scr("x2_sp", [NSEG, 128, NB * SEG], F32)
    retr_sp = scr("retr_sp", [NSEG, 128, NB * SEG], BF16)
    qt_sp = scr("qt_sp", [NSEG, 128, NB * SEG], BF16)
    xf_sp = scr("xf_sp", [NSEG, 128, NB * SEG], F32)
    w1bf = scr("w1bf", [D, FW], BF16)
    w2bf = scr("w2bf", [FW, D], BF16)
    h2_sp = scr("h2_sp", [NSEG, 128, NB * SEG], BF16)
    ag_in = scr("ag_in", [NB, 128, D], BF16)
    ag_out = nc.dram_tensor("ag_out", [NCORE * NB, 128, D], BF16,
                            kind="Internal", addr_space="Shared").ap()

    dbg = {}
    for name, shape, dt in debug:
        dbg[name] = nc.dram_tensor("dbg_" + name, shape, dt,
                                   kind="ExternalOutput").ap()

    P = Plan()
    st = {}

    with ExitStack() as ctx:
        SB_BASE = 18432
        SB_LIMIT = 16384 + 212863
        offmap = {}
        cur = [SB_BASE]

        def sb(name, shape, dt, at=None):
            size = shape[1] * mybir.dt.size(dt)
            if at is None:
                at = cur[0]
                cur[0] = (at + size + 63) & ~63
            offmap[name] = at
            return nc.alloc_sbuf_tensor_at(name, shape, dt, offset=at)

        ident32 = sb("ident32", [128, 128], F32)
        identbf = sb("identbf", [128, 128], BF16)
        onesrow128 = sb("onesrow128", [1, 128], F32)
        onesrow512bf = sb("onesrow512bf", [1, 512], BF16)
        onescolbf = sb("onescolbf", [128, 1], BF16)
        iotacol = sb("iotacol", [128, 1], F32)
        iotarow128 = sb("iotarow128", [1, 128], F32)
        iotarow512 = sb("iotarow512", [1, 512], F32)
        b2row = sb("b2row", [1, D], BF16)
        b1col = sb("b1col", [128, NWB], F32)
        c_cols = sb("c_cols", [128, NB], F32)
        mod_sb = sb("mod_sb", [128, 48], F32)
        derv = sb("derv", [128, 32], F32)
        dcol = sb("dcol", [128, 32], F32)
        rows2 = sb("rows2", [1, 64], F32)
        PERM_END = cur[0]
        rows = sb("rows", [1, 512 * 5], F32)
        dmat = sb("dmat", [128, 4 * SEG], BF16)
        bqtil = sb("bqtil", [128, SEG], F32)
        tstage = sb("tstage", [128, 512], F32)
        tstage2 = sb("tstage2", [128, 512], F32)
        expo_sb = sb("expo_sb", [128, 4 * SEG], F32)

        wqT = sb("wqT", [128, NB * D], BF16)
        wkT = sb("wkT", [128, NB * D], BF16)
        wvT = sb("wvT", [128, NB * D], BF16)
        xstage = sb("xstage", [128, 2 * D], F32)
        xf = sb("xf", [128, NB * SEG], F32)
        hseg = sb("hseg", [128, NB * SEG], BF16)
        sblk = sb("sblk", [128, 4 * 512], BF16)
        qseg = sb("qseg", [128, NB * SEG], BF16)
        kseg = sb("kseg", [128, NB * SEG], BF16)
        vseg = sb("vseg", [128, NSEG * D], BF16)
        ktil = sb("ktil", [128, 4 * D], BF16)
        khat = sb("khat", [128, 4 * D], BF16)
        ghat = sb("ghat", [128, NB * D], BF16)
        gstage = sb("gstage", [128, 2 * D], BF16)
        retrev = sb("retrev", [128, NB * SEG], BF16)
        xf2 = sb("xf2", [128, NB * SEG], F32)
        A_END = cur[0]
        assert A_END <= SB_LIMIT, A_END

        woT = sb("woT", [128, NB * D], BF16, at=offmap["wqT"])
        nbuf = sb("nbuf", [128, NB * D], BF16, at=offmap["wkT"])
        sc_sb = sb("sc_sb", [128, NB * D], BF16, at=offmap["wvT"])
        mbuf = sb("mbuf", [128, NB * D], BF16, at=offmap["ghat"])
        scT = sb("scT", [128, 4 * SEG], BF16, at=offmap["ktil"])
        qtil = sb("qtil", [128, NB * SEG], BF16, at=offmap["khat"])
        agstage = sb("agstage", [128, 2 * D], BF16, at=offmap["gstage"])
        rstr = sb("rstr", [128, NB * SEG], BF16, at=offmap["ktil"])
        gstg2 = sb("gstg2", [128, 4 * D], BF16, at=offmap["xstage"])
        rstr2 = sb("rstr2", [128, NB * SEG], BF16, at=offmap["kseg"])
        nscan_sb = sb("nscan_sb", [128, NB * D], BF16, at=offmap["ghat"])
        scT2 = sb("scT2", [128, 4 * SEG], BF16, at=offmap["xstage"])
        retrevb = sb("retrevb", [128, NB * SEG], BF16, at=offmap["wkT"])
        qtil2 = sb("qtil2", [128, NB * SEG], BF16, at=offmap["vseg"])
        agstage2 = sb("agstage2", [128, NB * D], BF16, at=offmap["ghat"])

        fcur = [offmap["rows"]]

        def sbf(name, shape, dt):
            size = shape[1] * mybir.dt.size(dt)
            at = fcur[0]
            fcur[0] = (at + size + 63) & ~63
            offmap[name] = at
            return nc.alloc_sbuf_tensor_at(name, shape, dt, offset=at)

        w1sb = sbf("w1sb", [128, NB * FW], BF16)
        w2sb = sbf("w2sb", [128, NWB * D], BF16)
        midc = sbf("midc", [128, NWB * SEG], BF16)
        h2c = sbf("h2c", [128, NB * SEG], BF16)
        x2c = sbf("x2c", [128, NB * SEG], F32)
        ostage = sbf("ostage", [128, 2 * D], F32)
        assert fcur[0] <= SB_LIMIT, fcur[0]

        ps_a = ctx.enter_context(nc.psum_tensor("ps_a", [128, 512], F32))
        ps_b = ctx.enter_context(nc.psum_tensor("ps_b", [128, 512], F32))
        pg_a = ctx.enter_context(nc.psum_tensor("pg_a", [128, 1024], F32))
        pg_b = ctx.enter_context(nc.psum_tensor("pg_b", [128, 1024], F32))
        pk_a = ctx.enter_context(nc.psum_tensor("pk_a", [128, 128], BF16))
        pk_b = ctx.enter_context(nc.psum_tensor("pk_b", [128, 128], BF16))

        sem_names = ("s_c", "s_xa", "s_xb", "s_ql", "s_kl", "s_vl",
                     "s_nb", "s_ga", "s_gb", "s_aga", "s_agb",
                     "s_hl", "s_x2l", "s_oa", "s_ob",
                     "s_qsp", "s_ksp", "s_vsp", "s_gsp0", "s_gsp1",
                     "s_x2sp", "s_h2sp", "s_agin", "s_w1", "s_w2", "s_dbg",
                     "s_rsp", "s_rl", "s_x2b2", "s_qtsp", "s_w1b", "s_w1c", "s_w1d", "s_gc", "s_gd", "s_age", "s_agf", "s_ql2", "s_rl2", "s_wp1", "s_wp2", "s_w2b", "s_xfa", "s_xfb", "s_xla", "s_xlb",
                     "s_pe", "s_dve", "s_act", "s_cc")
        sems = {n: ctx.enter_context(nc.semaphore(n)) for n in sem_names}

        rot_ps = Rot([ps_a[:, 0:512], ps_b[:, 0:512],
                      pg_a[:, 0:512], pg_a[:, 512:1024],
                      pg_b[:, 0:512], pg_b[:, 512:1024]])
        rot_pt = rot_ps
        rot_pk = Rot([pk_a, pk_b])
        rot_xstage = Rot([0, 1])
        rot_nstage = Rot([0, 1, 2, 3])
        rot_gstage = Rot([0, 1])
        rot_ostage = Rot([0, 1])
        rot_sblk1 = Rot([0, 1])
        rot_sblk2 = Rot([2, 3])

        def dma(eng, out_ap, in_ap, waits=(), sem=None):
            assert sem is not None
            return P.emit(eng,
                          lambda e, S, o=out_ap, i=in_ap:
                              e.dma_start(out=o, in_=i),
                          waits, sem, 16)

        def full(sem):
            return (sem, P.cnt.get(sem, 0))

        def mm(out_ap, a_ap, b_ap, start, stop, waits=(), inc=None):
            return P.emit("tensor",
                          lambda e, S, o=out_ap, a=a_ap, b=b_ap, s1=start,
                          s2=stop: e.matmul(o, a, b, start=s1, stop=s2),
                          waits, inc)

        def tr(out_ap, in_ap, ident_ap, waits=(), inc="s_pe"):
            return P.emit("tensor",
                          lambda e, S, o=out_ap, i=in_ap, d=ident_ap:
                              e.transpose(o, i, d),
                          waits, inc)

        def dve(fn, waits=(), inc="s_dve"):
            return P.emit("vector", fn, waits, inc)

        def act(fn, waits=(), inc="s_act"):
            return P.emit("scalar", fn, waits, inc)

        def dump(name, src_ap, waits):
            if name in dbg:
                t = dma("sync", dbg[name], src_ap,
                        waits=list(waits) + [st.get("dump_prev")],
                        sem="s_dbg")
                st["dump_prev"] = ("s_dbg", t)
                return 1
            return 0

        n_dbg = 0

        t_cc0 = dma("sync", c_cols[:], c_in, sem="s_hl")
        for ap_sb, ap_in in ((ident32, id32_in), (identbf, idbf_in),
                             (onesrow128, onesrow128_in),
                             (onesrow512bf, onesrow512bf_in),
                             (onescolbf, onescol_bf_in),
                             (iotacol, iotacol_in),
                             (iotarow128, iotarow128_in),
                             (iotarow512, iotarow512_in),
                             (b1col, b1_in)):
            dma("sync", ap_sb[:], ap_in, sem="s_c")
        dma("sync", rows2[0:1, 0:NCORE], ecore_in, sem="s_c")
        dma("sync", rows2[0:1, 8:8 + NCORE], emask_in, sem="s_c")
        dma("sync", rows2[0:1, 16:17], dlog_in, sem="s_c")
        dma("sync", rows2[0:1, 17:18], osc_in, sem="s_c")
        dma("sync", expo_sb[:], expo_in, sem="s_c")
        t_b2 = dma("gpsimd", b2row[:], b2_in, sem="s_wp2")
        if stage >= 3:
            for ch in range(4):
                dma("gpsimd", w1bf[256 * ch:256 * (ch + 1), :],
                    w1_in[256 * ch:256 * (ch + 1), :], sem="s_wp1")
                dma("gpsimd", w2bf[1024 * ch:1024 * (ch + 1), :],
                    w2_in[1024 * ch:1024 * (ch + 1), :], sem="s_wp2")
        W_WPREP1 = full("s_wp1")
        W_WPREP2 = full("s_wp2")
        W_CONST = full("s_c")

        LND, TMP, APW, OSCC, BIA, ONE, KH0 = 0, 1, 2, 5, 6, 7, 8
        t_bia = dve(lambda e, S: e.memset(dcol[:, BIA:BIA + 1],
                                          LOG_INV_SQRT_D))
        t_one = dve(lambda e, S: e.memset(dcol[:, ONE:ONE + 1], 1.0))
        alpha_bf = sb("alpha_bf", [128, 32], BF16)
        gaefa = sb("gaefa", [128, 24], F32)

        i0, pt, g = rot_pt.next()
        t = mm(pt[:, 0:1], onesrow128[:], rows2[0:1, 16:17], True, False,
               waits=[W_CONST, g])
        t = mm(pt[:, 1:2], onesrow128[:], rows2[0:1, 17:18], False, True,
               waits=[], inc="s_pe")
        t_sg = act(lambda e, S, pt=pt: e.activation(
            dcol[:, TMP:TMP + 1], pt[:, 0:1], AF.Sigmoid),
            waits=[("s_pe", t)])
        t_oc = dve(lambda e, S, pt=pt: e.tensor_copy(
            dcol[:, OSCC:OSCC + 1], pt[:, 1:2]), waits=[("s_pe", t)])
        rot_pt.done(i0, "s_dve", t_oc)
        t_lnd = act(lambda e, S: e.activation(
            dcol[:, LND:LND + 1], dcol[:, TMP:TMP + 1], AF.Ln),
            waits=[("s_act", t_sg)])
        t_apw = None
        for p in (1, 2, 3):
            t_apw = act(lambda e, S, p=p: e.activation(
                dcol[:, APW + p - 1:APW + p], dcol[:, LND:LND + 1], AF.Exp,
                scale=float(SEG * p)), waits=[("s_act", t_lnd)])
        t_apwbf = dve(lambda e, S: e.tensor_copy(
            alpha_bf[:, APW:APW + 1], dcol[:, APW:APW + 1]),
            waits=[("s_act", t_apw)], inc="s_dve")
        W_APW = ("s_dve", t_apwbf)
        t_kh = None
        for s in range(NSEG):
            for b in range(4):
                tt = dve(lambda e, S, s=s, b=b: e.tensor_scalar(
                    dcol[:, TMP:TMP + 1], iotacol[:], 1.0,
                    float(SEG * s + 128 * b), OP.mult, OP.add),
                    waits=[W_CONST, ("s_act", t_kh if t_kh else t_lnd)])
                t_kh = act(lambda e, S, s=s, b=b: e.activation(
                    dcol[:, KH0 + 4 * s + b:KH0 + 4 * s + b + 1],
                    dcol[:, TMP:TMP + 1], AF.Exp,
                    scale=dcol[:, LND:LND + 1]), waits=[("s_dve", tt)])

        tt = dve(lambda e, S: e.tensor_scalar(
            rows[0:1, 0:512], iotarow512[:], -1.0, float(SEG - 1),
            OP.mult, OP.add), waits=[W_CONST])
        tt = act(lambda e, S: e.activation(
            rows[0:1, 512:1024], rows[0:1, 0:512], AF.Exp,
            scale=dcol[0:1, LND:LND + 1], bias=dcol[0:1, BIA:BIA + 1]),
            waits=[("s_dve", tt), ("s_act", t_lnd), ("s_dve", t_bia)])
        i0, ps, g = rot_ps.next()
        t = mm(ps[:], onesrow128[:], rows[0:1, 512:1024], True, True,
               waits=[("s_act", tt), g], inc="s_pe")
        t_bqtil = dve(lambda e, S, ps=ps: e.tensor_copy(bqtil[:], ps[:]),
                      waits=[("s_pe", t)])
        rot_ps.done(i0, "s_dve", t_bqtil)

        tt = dve(lambda e, S: e.tensor_scalar(
            rows2[0:1, 24:32], rows2[0:1, 0:8], dcol[0:1, LND:LND + 1],
            None, OP.mult), waits=[W_CONST, ("s_act", t_lnd)])
        tt = act(lambda e, S: e.activation(
            rows2[0:1, 32:40], rows2[0:1, 24:32], AF.Exp),
            waits=[("s_dve", tt)])
        tt = dve(lambda e, S: e.tensor_mul(
            rows2[0:1, 40:48], rows2[0:1, 32:40], rows2[0:1, 8:16]),
            waits=[("s_act", tt)])
        i0, ps, g = rot_ps.next()
        t = mm(ps[:, 0:8], onesrow128[:], rows2[0:1, 40:48], True, True,
               waits=[("s_dve", tt), g], inc="s_pe")
        t_wcomb = dve(lambda e, S, ps=ps: e.tensor_copy(
            derv[:, 24:32], ps[:, 0:8]), waits=[("s_pe", t)])
        rot_ps.done(i0, "s_dve", t_wcomb)

        t_dmat = None
        for b in range(4):
            eb = expo_sb[:, 512 * b:512 * (b + 1)]
            tm = dve(lambda e, S, eb=eb: e.tensor_scalar(
                tstage[:], eb, 0.0, None, OP.is_ge),
                waits=[W_CONST, ("s_dve", t_dmat)])
            tm2 = dve(lambda e, S, eb=eb: e.tensor_scalar(
                tstage2[:], eb, 0.0, None, OP.max),
                waits=[("s_dve", tm)])
            te = act(lambda e, S: e.activation(
                tstage2[:], tstage2[:], AF.Exp,
                scale=dcol[:, LND:LND + 1], bias=dcol[:, BIA:BIA + 1]),
                waits=[("s_dve", tm2), ("s_act", t_lnd), ("s_dve", t_bia)])
            t_dmat = dve(lambda e, S, b=b: e.tensor_mul(
                dmat[:, 512 * b:512 * (b + 1)], tstage2[:], tstage[:]),
                waits=[("s_act", te)])

        t_cs = act(lambda e, S: e.activation(c_cols[:], c_cols[:], AF.Silu),
                   waits=[("s_hl", t_cc0)])
        imod, pmod, gmod = rot_ps.next()
        first_mod = True
        t_mod_mm = None
        for k in range(NB):
            for ch in range(6):
                ih, half, gx = rot_xstage.next()
                tld = dma("sync", xstage[:, 1024 * half:1024 * (half + 1)],
                          wmod_in[128 * k:128 * (k + 1),
                                  1024 * ch:1024 * (ch + 1)],
                          waits=[gx] + ([gmod] if first_mod else []),
                          sem="s_xa" if half == 0 else "s_xb")
                for j2 in range(8):
                    col = 8 * ch + j2
                    last = (k == NB - 1 and col == 47)
                    t_mod_mm = mm(pmod[:, col:col + 1],
                                  xstage[:, 1024 * half + 128 * j2:
                                         1024 * half + 128 * (j2 + 1)],
                                  c_cols[:, k:k + 1],
                                  first_mod, last,
                                  waits=[("s_xa" if half == 0 else "s_xb",
                                          tld), ("s_act", t_cs)],
                                  inc="s_pe" if j2 == 7 else None)
                    first_mod = False
                rot_xstage.done(ih, "s_pe", t_mod_mm)
        t_modev = dve(lambda e, S, pmod=pmod: e.tensor_copy(
            mod_sb[:], pmod[:, 0:48]), waits=[("s_pe", t_mod_mm)])
        rot_ps.done(imod, "s_dve", t_modev)

        tv = dve(lambda e, S: e.tensor_copy(derv[:, 0:8], mod_sb[:, 0:8]),
                 waits=[("s_dve", t_modev)])
        tv = dve(lambda e, S: e.tensor_scalar(
            derv[:, 8:16], mod_sb[:, 8:16], 1.0, 1.0, OP.mult, OP.add),
            waits=[("s_dve", tv)])

        def plan_clamp(src_sl, dst_sl, tprev):
            tv = dve(lambda e, S: e.tensor_scalar(
                tstage[:, 0:8], src_sl, 0.0, -1.0, OP.is_ge, OP.add),
                waits=[("s_dve", tprev)])
            tv = dve(lambda e, S: e.tensor_scalar(
                tstage[:, 0:8], tstage[:, 0:8], 2.0, 1.0, OP.mult, OP.add),
                waits=[("s_dve", tv)])
            tv = dve(lambda e, S: e.tensor_mul(
                dst_sl, src_sl, tstage[:, 0:8]), waits=[("s_dve", tv)])
            tv = dve(lambda e, S: e.tensor_scalar(
                dst_sl, dst_sl, 0.25, None, OP.max), waits=[("s_dve", tv)])
            tv = dve(lambda e, S: e.tensor_mul(
                dst_sl, dst_sl, tstage[:, 0:8]), waits=[("s_dve", tv)])
            return tv

        tv = plan_clamp(mod_sb[:, 16:24], derv[:, 16:24], tv)
        tv = dve(lambda e, S: e.tensor_scalar(
            derv[:, 16:24], derv[:, 16:24], dcol[:, OSCC:OSCC + 1], None,
            OP.mult), waits=[("s_dve", tv), ("s_dve", t_oc)])
        tv = plan_clamp(mod_sb[:, 40:48], mod_sb[:, 40:48], tv)
        t_mod = dve(lambda e, S: e.tensor_scalar(
            mod_sb[:, 32:40], mod_sb[:, 32:40], 1.0, 1.0, OP.mult, OP.add),
            waits=[("s_dve", tv)])
        sa_c, ca1_c, gaeff_c = derv[:, 0:8], derv[:, 8:16], derv[:, 16:24]
        t_gaefa = None
        for k in range(3):
            t_gaefa = dve(lambda e, S, k=k: e.tensor_scalar(
                gaefa[:, 8 * k:8 * (k + 1)], derv[:, 16:24],
                dcol[:, APW + k:APW + k + 1], None, OP.mult),
                waits=[("s_dve", t_mod), ("s_act", t_apw)])
        wcomb_c = derv[:, 24:32]
        sb_c, cb1_c, gb_c = mod_sb[:, 24:32], mod_sb[:, 32:48 - 8], mod_sb[:, 40:48]
        n_dbg += dump("mod", mod_sb[:], [("s_dve", t_mod)])
        n_dbg += dump("derv", derv[:], [("s_dve", t_mod)])
        n_dbg += dump("dcol", dcol[:], [("s_act", t_kh), ("s_act", t_apw)])
        n_dbg += dump("dmat", dmat[:], [("s_dve", t_dmat)])

        def plan_wT(w_in_, wT_):
            tlast = None
            for ib in range(NB):
                ih, half, gx = rot_xstage.next()
                sx = "s_xa" if half == 0 else "s_xb"
                tld = dma("sync", xstage[:, 1024 * half:1024 * (half + 1)],
                          w_in_[128 * ib:128 * (ib + 1), :], waits=[gx],
                          sem=sx)
                tre = None
                for db in range(NB):
                    ipt, pt, gp = rot_pt.next()
                    t = tr(pt[:, 0:128],
                           xstage[:, 1024 * half + 128 * db:
                                  1024 * half + 128 * (db + 1)],
                           ident32[:], waits=[(sx, tld), gp])
                    te = act(lambda e, S, pt=pt, db=db, ib=ib, wT_=wT_:
                             e.activation(
                                 wT_[:, 1024 * db + 128 * ib:
                                     1024 * db + 128 * (ib + 1)],
                                 pt[:, 0:128], AF.Copy), waits=[("s_pe", t)])
                    rot_pt.done(ipt, "s_act", te)
                    tre = t
                    tlast = te
                rot_xstage.done(ih, "s_pe", tre)
            return tlast

        t_wq = plan_wT(wq_in, wqT)
        t_wk = plan_wT(wk_in, wkT)
        t_wv = plan_wT(wv_in, wvT)

        def plan_ln_stats(xf_t, tag, ready):
            isum, psum_s, gsum = rot_ps.next()
            isq, psum_q, gsq = rot_ps.next()
            t_summ = t_sqm = None
            for k in range(NB):
                i1, sl1, g1 = rot_sblk1.next()
                tc1 = dve(lambda e, S, k=k, sl1=sl1: e.tensor_copy(
                    sblk[:, 512 * sl1:512 * (sl1 + 1)],
                    xf_t[:, 512 * k:512 * (k + 1)]),
                    waits=[g1] + ready)
                t_summ = mm(psum_s[0:1, :], onescolbf[:],
                            sblk[:, 512 * sl1:512 * (sl1 + 1)],
                            k == 0, k == NB - 1,
                            waits=[("s_dve", tc1)] + ([gsum] if k == 0 else []),
                            inc="s_pe")
                rot_sblk1.done(i1, "s_pe", t_summ)
                i2, sl2, g2 = rot_sblk2.next()
                tc2 = act(lambda e, S, k=k, sl2=sl2: e.activation(
                    sblk[:, 512 * sl2:512 * (sl2 + 1)],
                    xf_t[:, 512 * k:512 * (k + 1)], AF.Square),
                    waits=[g2] + ready)
                t_sqm = mm(psum_q[0:1, :], onescolbf[:],
                           sblk[:, 512 * sl2:512 * (sl2 + 1)],
                           k == 0, k == NB - 1,
                           waits=[("s_act", tc2)] + ([gsq] if k == 0 else []),
                           inc="s_pe")
                rot_sblk2.done(i2, "s_pe", t_sqm)
                st[(tag, "c2", k)] = tc2

            R = lambda i: rows[0:1, 512 * i:512 * (i + 1)]
            t1 = dve(lambda e, S: e.tensor_scalar(
                R(0), psum_s[0:1, :], 1.0 / D, None, OP.mult),
                waits=[("s_pe", t_summ)])
            rot_ps.done(isum, "s_dve", t1)
            t2 = dve(lambda e, S: e.tensor_scalar(
                R(1), psum_q[0:1, :], 1.0 / D, None, OP.mult),
                waits=[("s_pe", t_sqm)])
            rot_ps.done(isq, "s_dve", t2)
            t3 = dve(lambda e, S: e.tensor_mul(R(2), R(0), R(0)),
                     waits=[("s_dve", t2)])
            t3 = dve(lambda e, S: e.tensor_sub(R(2), R(1), R(2)),
                     waits=[("s_dve", t3)])
            t3 = dve(lambda e, S: e.tensor_scalar(
                R(2), R(2), 1e-5, None, OP.add), waits=[("s_dve", t3)])
            t3 = dve(lambda e, S: e.reciprocal(R(3), R(2)),
                     waits=[("s_dve", t3)])
            t4 = act(lambda e, S: e.activation(R(3), R(3), AF.Sqrt),
                     waits=[("s_dve", t3)])
            t5 = dve(lambda e, S: e.tensor_mul(R(4), R(0), R(3)),
                     waits=[("s_act", t4)])
            return {"xf_t": xf_t, "tag": tag, "t4": t4, "t5": t5, "t1": t1,
                    "R": R, "xf_act": st[(tag, "c2", NB - 1)]}

        def plan_ln_apply(ctx, h_t, ca1col, sacol, h_guard):
            xf_t, tag, R = ctx["xf_t"], ctx["tag"], ctx["R"]
            ibr, ps_br, gbr = rot_ps.next()
            tba = mm(ps_br[:], onesrow128[:], R(3), True, True,
                     waits=[("s_act", ctx["t4"]), ("s_dve", ctx["t1"]), gbr],
                     inc="s_pe")
            ibm, ps_bm, gbm = rot_ps.next()
            tbb = mm(ps_bm[:], onesrow128[:], R(4), True, True,
                     waits=[("s_dve", ctx["t5"]), gbm], inc="s_pe")
            th = None
            tu_last = None
            for k in range(NB):
                tu = dve(lambda e, S, k=k, ps_br=ps_br: e.tensor_mul(
                    tstage[:], xf_t[:, 512 * k:512 * (k + 1)], ps_br[:]),
                    waits=[("s_pe", tba)])
                tu_last = tu
                tn = dve(lambda e, S, ps_bm=ps_bm: e.tensor_sub(
                    tstage2[:], tstage[:], ps_bm[:]),
                    waits=[("s_pe", tbb), ("s_dve", tu)])
                th = dve(lambda e, S, k=k: e.tensor_scalar(
                    h_t[:, 512 * k:512 * (k + 1)], tstage2[:],
                    ca1col[:, k:k + 1], sacol[:, k:k + 1], OP.mult, OP.add),
                    waits=[("s_dve", tn), ("s_dve", t_mod)] +
                          ([h_guard] if k == 0 else []))
                st[(tag, "h", k)] = th
            rot_ps.done(ibr, "s_dve", th)
            rot_ps.done(ibm, "s_dve", th)
            return {"h_last": th, "xf_dve": tu_last,
                    "xf_act": ctx["xf_act"]}

        def plan_ln(xf_t, h_t, ca1col, sacol, ready, tag, h_guard):
            ctx = plan_ln_stats(xf_t, tag, ready)
            return plan_ln_apply(ctx, h_t, ca1col, sacol, h_guard)

        def p1_x_ln(s):
            xf_s = xf if s % 2 == 0 else xf2
            prev2 = st.get(("p1ln", s + 2))
            t_xf = None
            for b in range(4):
                ih, half, gx = rot_xstage.next()
                xw = [gx]
                if s == NSEG - 1 and b == 0:
                    xw.append(("s_act", t_wv))
                if prev2 is not None and b == 0:
                    xw += [("s_dve", prev2["xf_dve"]),
                           ("s_act", prev2["xf_act"]),
                           ("s_xfa" if s % 2 == 0 else "s_xfb",
                            st.get(("xfsp", s + 2)))]
                sx = "s_xa" if half == 0 else "s_xb"
                tld = dma("sync", xstage[:, 1024 * half:1024 * (half + 1)],
                          x_in[512 * s + 128 * b:512 * s + 128 * (b + 1), :],
                          waits=xw, sem=sx)
                tre = None
                for db in range(NB):
                    ipt, pt, gp = rot_pt.next()
                    t = tr(pt[:, 0:128],
                           xstage[:, 1024 * half + 128 * db:
                                  1024 * half + 128 * (db + 1)],
                           ident32[:], waits=[(sx, tld), gp])
                    te = act(lambda e, S, pt=pt, db=db, b=b, xf_s=xf_s:
                             e.activation(
                        xf_s[:, 512 * db + 128 * b:512 * db + 128 * (b + 1)],
                        pt[:, 0:128], AF.Copy), waits=[("s_pe", t)])
                    rot_pt.done(ipt, "s_act", te)
                    tre = t
                    t_xf = te
                rot_xstage.done(ih, "s_pe", tre)
            ctx = plan_ln_stats(xf_s, ("l1", s), [("s_act", t_xf)])
            st[("p1ctx", s)] = ctx
            st[("xfsp", s)] = dma("gpsimd", xf_sp[s], xf_s[:],
                                  waits=[("s_act", t_xf)],
                                  sem="s_xfa" if s % 2 == 0 else "s_xfb")

        def p1_apply(s):
            lnres = plan_ln_apply(st[("p1ctx", s)], hseg, ca1_c, sa_c,
                                  h_guard=("s_pe",
                                           st.get(("vmm_last", s + 1))))
            st[("p1ln", s)] = lnres

        if stage >= 1:
            p1_x_ln(NSEG - 1)
            p1_apply(NSEG - 1)
        for s in (range(NSEG - 1, -1, -1) if stage >= 1 else []):
            lnres = st[("p1ln", s)]

            for tag, wT_, dest, spill in (("q", wqT, qseg, q_sp),
                                          ("k", wkT, kseg, k_sp)):
                tev = None
                spsem = "s_qsp" if tag == "q" else "s_ksp"
                for ib in range(NB):
                    ips, ps, gp = rot_ps.next()
                    w0 = [gp, (spsem, st.get((tag + "sp", s + 1)))]
                    t = None
                    for db in range(NB):
                        t = mm(ps[:],
                               wT_[:, 1024 * db + 128 * ib:
                                   1024 * db + 128 * (ib + 1)],
                               hseg[:, 512 * db:512 * (db + 1)],
                               db == 0, db == NB - 1,
                               waits=[("s_dve", st[(("l1", s), "h", db)])] +
                                     (w0 if db == 0 else []),
                               inc="s_pe" if db == NB - 1 else None)
                    tev = dve(lambda e, S, ps=ps, ib=ib, dest=dest:
                              e.tensor_copy(
                                  dest[:, 512 * ib:512 * (ib + 1)], ps[:]),
                              waits=[("s_pe", t)])
                    rot_ps.done(ips, "s_dve", tev)
                    st[(tag + "ev", s, ib)] = tev
                st[(tag + "sp", s)] = dma("gpsimd", spill[s], dest[:],
                                          waits=[("s_dve", tev)], sem=spsem)
            tvl = None
            for b in range(4):
                for eh in range(2):
                    ips, ps, gp = rot_ps.next()
                    t = None
                    for db in range(NB):
                        t = mm(ps[:],
                               hseg[:, 512 * db + 128 * b:
                                    512 * db + 128 * (b + 1)],
                               wvT[:, 1024 * db + 512 * eh:
                                   1024 * db + 512 * (eh + 1)],
                               db == 0, db == NB - 1,
                               waits=[("s_dve", st[(("l1", s), "h", NB - 1)])]
                                     + ([gp, ("s_vsp", st.get(("vsp", s + 1)))]
                                        if db == 0 else []),
                               inc="s_pe" if db == NB - 1 else None)
                    tev = dve(lambda e, S, ps=ps, b=b, eh=eh: e.tensor_copy(
                        vseg[:, 1024 * b + 512 * eh:1024 * b + 512 * (eh + 1)],
                        ps[:]), waits=[("s_pe", t)])
                    rot_ps.done(ips, "s_dve", tev)
                    tvl = tev
            st[("vmm_last", s)] = t
            st[("vev_last", s)] = tvl
            st[("vsp", s)] = dma("gpsimd", v_sp[s], vseg[:],
                                 waits=[("s_dve", tvl)], sem="s_vsp")

            if s - 1 >= 0:
                p1_x_ln(s - 1)

            tkh = None
            for b in range(4):
                for ib in range(NB):
                    ipk, pk, gp = rot_pk.next()
                    t = tr(pk[:],
                           kseg[:, 512 * ib + 128 * b:512 * ib + 128 * (b + 1)],
                           identbf[:],
                           waits=[("s_dve", st[("kev", s, ib)]), gp])
                    tt1 = dve(lambda e, S, pk=pk, b=b, ib=ib: e.tensor_scalar(
                        ktil[:, 1024 * b + 128 * ib:1024 * b + 128 * (ib + 1)],
                        pk[:], dcol[:, KH0 + b:KH0 + b + 1], None, OP.mult),
                        waits=[("s_pe", t), ("s_act", t_kh),
                               ("s_pe", st.get(("gmm_last", s + 1)))])
                    rot_pk.done(ipk, "s_dve", tt1)
                    tkh = tt1
            st[("kh_done", s)] = tkh

            tgm = None
            tnl = None
            for ib in range(NB):
                slots = [rot_ps.next(), rot_ps.next()]
                tstop = []
                for eh in range(2):
                    ipg, pg, gp = slots[eh]
                    t = None
                    for b in range(4):
                        t = mm(pg,
                               ktil[:, 1024 * b + 128 * ib:
                                    1024 * b + 128 * (ib + 1)],
                               vseg[:, 1024 * b + 512 * eh:
                                    1024 * b + 512 * (eh + 1)],
                               b == 0, b == 3,
                               waits=[("s_dve", tkh),
                                      ("s_dve", st[("vev_last", s)])] +
                                     ([gp] if b == 0 else []),
                               inc="s_pe" if b == 3 else None)
                    tstop.append(t)
                tgm = t
                for eh in range(2):
                    sl = nscan_sb[:, 1024 * ib + 512 * eh:
                                  1024 * ib + 512 * (eh + 1)]
                    spw = ("s_gsp0" if (s + 1) % 2 == 0 else "s_gsp1",
                           st.get(("nsp", s + 1)))
                    if s == NSEG - 1:
                        tnl = dve(lambda e, S, pg=slots[eh][1], sl=sl:
                                  e.tensor_copy(sl, pg),
                                  waits=[("s_pe", tstop[eh])])
                    else:
                        tnl = dve(lambda e, S, pg=slots[eh][1], sl=sl:
                                  e.scalar_tensor_tensor(
                                      sl, sl, alpha_bf[:, APW:APW + 1],
                                      pg, OP.mult, OP.add),
                                  waits=[("s_pe", tstop[eh]), W_APW, spw])
                    rot_ps.done(slots[eh][0], "s_dve", tnl)
            st[("gmm_last", s)] = tgm
            st[("nsc", s)] = tnl
            if s >= 1:
                st[("nsp", s)] = dma(
                    "gpsimd", g_sp[s], nscan_sb[:],
                    waits=[("s_dve", tnl)],
                    sem="s_gsp0" if s % 2 == 0 else "s_gsp1")
            if s - 1 >= 0:
                p1_apply(s - 1)

        t_p1_dve = (st[("nsc", 0)] if stage >= 1 else None)
        if stage >= 1:
            n_dbg += dump("ghat", nscan_sb[:], [("s_dve", t_p1_dve)])

        t_agd = None
        for ib in range(NB if stage >= 2 else 0):
            t_agd = dma("gpsimd", ag_in[ib],
                        nscan_sb[:, 1024 * ib:1024 * (ib + 1)],
                        waits=[("s_dve", t_p1_dve)],
                        sem="s_agin")
        t_cc = (P.emit("gpsimd",
                       lambda e, S: e.collective_compute(
                           "AllGather", OP.bypass,
                           replica_groups=[list(range(NCORE))],
                           ins=[ag_in], outs=[ag_out]),
                       waits=[full("s_agin")], inc="s_cc")
                if stage >= 2 else None)

        tlast = None
        for ib in range(NB if stage >= 2 else 0):
            ih, half, gx = rot_xstage.next()
            sx = "s_xa" if half == 0 else "s_xb"
            tld = dma("sync", xstage[:, 1024 * half:1024 * (half + 1)],
                      wo_in[128 * ib:128 * (ib + 1), :],
                      waits=[gx, ("s_dve", t_p1_dve)], sem=sx)
            tre = None
            for db in range(NB):
                ipt, pt, gp = rot_pt.next()
                t = tr(pt[:, 0:128],
                       xstage[:, 1024 * half + 128 * db:
                              1024 * half + 128 * (db + 1)],
                       ident32[:], waits=[(sx, tld), gp])
                te = act(lambda e, S, pt=pt, db=db, ib=ib: e.activation(
                    woT[:, 1024 * db + 128 * ib:1024 * db + 128 * (ib + 1)],
                    pt[:, 0:128], AF.Copy), waits=[("s_pe", t)])
                rot_pt.done(ipt, "s_act", te)
                tre = t
                tlast = te
            rot_xstage.done(ih, "s_pe", tre)
        t_wo = tlast

        for s in (range(NSEG) if stage >= 2 else []):
            prv = s - 1
            if s == 0:
                t_qld = t_kld = None
                W_Q = ("s_dve", st[("qev", 0, NB - 1)])
                W_K = ("s_dve", st[("kev", 0, NB - 1)])
            else:
                t_qld = dma("sync", qseg[:], q_sp[s],
                            waits=[full("s_qsp"),
                                   ("s_pe", st.get(("p2score", prv))),
                                   ("s_dve", st.get(("p2qtt", prv)))],
                            sem="s_ql")
                t_kld = dma("sync", kseg[:], k_sp[s],
                            waits=[full("s_ksp"),
                                   ("s_pe", st.get(("p2score", prv))),
                                   ("s_dve", t_p1_dve)], sem="s_kl")
                W_Q = ("s_ql", t_qld)
                W_K = ("s_kl", t_kld)
            if s < NSEG - 1:
                tnb = dma("sync", nbuf[:], g_sp[s + 1],
                          waits=[("s_gsp0" if (s + 1) % 2 == 0 else "s_gsp1",
                                  st[("nsp", s + 1)]),
                                 ("s_dve", t_p1_dve),
                                 ("s_pe", st.get(("p2ret", prv)))],
                          sem="s_nb")
                nw = ("s_nb", tnb)
            else:
                nw = None

            if s == 0:
                W_V = ("s_dve", st[("vev_last", 0)])
            else:
                t_vld = dma("sync", vseg[:], v_sp[s],
                            waits=[full("s_vsp"),
                                   ("s_pe", st.get(("p2ret", prv)))],
                            sem="s_vl")
                W_V = ("s_vl", t_vld)

            scT_s = scT if s % 2 == 0 else scT2
            t_scev = {}
            tsc_mm = None
            for b in range(4):
                ips, ps, gp = rot_ps.next()
                t = None
                for ib in range(NB):
                    t = mm(ps[:],
                           kseg[:, 512 * ib + 128 * b:512 * ib + 128 * (b + 1)],
                           qseg[:, 512 * ib:512 * (ib + 1)],
                           ib == 0, ib == NB - 1,
                           waits=[W_K, W_Q] +
                                 ([gp] if ib == 0 else []),
                           inc="s_pe" if ib == NB - 1 else None)
                tsc_mm = t
                tev = dve(lambda e, S, ps=ps, b=b, scT_s=scT_s: e.tensor_mul(
                    scT_s[:, 512 * b:512 * (b + 1)], ps[:],
                    dmat[:, 512 * b:512 * (b + 1)]),
                    waits=[("s_pe", t), ("s_dve", t_dmat),
                           ("s_pe", st.get(("p2ret", s - 2))),
                           ("s_dve", t_p1_dve)])
                rot_ps.done(ips, "s_dve", tev)
                t_scev[b] = tev
            st[("p2score", s)] = tsc_mm

            t_qt = None
            for ib in range(NB):
                t_qt = dve(lambda e, S, ib=ib: e.tensor_mul(
                    qtil[:, 512 * ib:512 * (ib + 1)],
                    qseg[:, 512 * ib:512 * (ib + 1)], bqtil[:]),
                    waits=[W_Q, ("s_dve", t_bqtil),
                           ("s_dve", t_p1_dve),
                           ("s_pe", st.get(("p2ret", prv))),
                           ("s_qtsp", st.get(("qtsp", prv)))])
            st[("p2qtt", s)] = t_qt
            st[("qtsp", s)] = dma("gpsimd", qt_sp[s], qtil[:],
                                  waits=[("s_dve", t_qt)], sem="s_qtsp")

            t_ret = None
            tevl = None
            for j in range(NB):
                ips, ps, gp = rot_ps.next()
                t = None
                for b in range(4):
                    t = mm(ps[:],
                           vseg[:, 1024 * b + 128 * j:1024 * b + 128 * (j + 1)],
                           scT_s[:, 512 * b:512 * (b + 1)],
                           b == 0, (b == 3 and s == NSEG - 1),
                           waits=[W_V, ("s_dve", t_scev[b])] +
                                 ([gp] if b == 0 else []),
                           inc="s_pe" if (b == 3 and s == NSEG - 1) else None)
                if s < NSEG - 1:
                    for ib in range(NB):
                        t = mm(ps[:],
                               nbuf[:, 1024 * ib + 128 * j:
                                    1024 * ib + 128 * (j + 1)],
                               qtil[:, 512 * ib:512 * (ib + 1)],
                               False, ib == NB - 1,
                               waits=[nw, ("s_dve", t_qt)],
                               inc="s_pe" if ib == NB - 1 else None)
                t_ret = t
                tev = dve(lambda e, S, ps=ps, j=j: e.tensor_copy(
                    retrev[:, 512 * j:512 * (j + 1)], ps[:]),
                    waits=[("s_pe", t),
                           ("s_rsp", st.get(("rsp", prv)))])
                rot_ps.done(ips, "s_dve", tev)
                tevl = tev
            st[("p2ret", s)] = t_ret
            st[("rsp", s)] = dma("gpsimd", retr_sp[s], retrev[:],
                                 waits=[("s_dve", tevl)], sem="s_rsp")

        t_sc = None
        pair_guard = {}
        for p in range(4 if stage >= 2 else 0):
            tacc0 = tacc1 = None
            for j in range(NCORE):
                sl = j % 4
                sag = ("s_aga", "s_agb", "s_age", "s_agf")[sl]
                tld = dma("gpsimd", agstage2[:, 2048 * sl:2048 * sl + 1024],
                          ag_out[NB * j + 2 * p],
                          waits=[("s_cc", t_cc), full("s_dbg"),
                                 ("s_dve", pair_guard.get(sl))],
                          sem=sag)
                tld = dma("gpsimd", agstage2[:, 2048 * sl + 1024:
                                             2048 * (sl + 1)],
                          ag_out[NB * j + 2 * p + 1],
                          waits=[], sem=sag)
                for h in range(2):
                    stg = agstage2[:, 2048 * sl + 1024 * h:
                                   2048 * sl + 1024 * (h + 1)]
                    ib = 2 * p + h
                    dst = sc_sb[:, 1024 * ib:1024 * (ib + 1)]
                    if j == 0:
                        tacc = dve(lambda e, S, stg=stg, dst=dst:
                                   e.tensor_scalar(dst, stg,
                                                   wcomb_c[:, 0:1], None,
                                                   OP.mult),
                                   waits=[(sag, tld), ("s_dve", t_wcomb),
                                          ("s_dve", t_p1_dve)])
                    else:
                        tacc = dve(lambda e, S, stg=stg, dst=dst, j=j:
                                   e.scalar_tensor_tensor(
                                       dst, stg, wcomb_c[:, j:j + 1], dst,
                                       OP.mult, OP.add),
                                   waits=[(sag, tld),
                                          ("s_dve", tacc0 if h == 0
                                           else tacc1)])
                    if h == 0:
                        tacc0 = tacc
                    else:
                        tacc1 = tacc
                pair_guard[sl] = tacc1
            st[("scdone", 2 * p)] = tacc0
            st[("scdone", 2 * p + 1)] = tacc1
            t_sc = tacc1
        if stage >= 2:
            n_dbg += dump("sc", sc_sb[:], [("s_dve", t_sc)])

        def p2b_x(s):
            xf_s = xf if s % 2 == 0 else xf2
            prev2 = st.get(("p2ln", s - 2))
            xw = [("s_xfa" if s % 2 == 0 else "s_xfb",
                   st[("xfsp", s)])]
            if prev2 is not None:
                xw += [("s_dve", prev2["xf_dve"]),
                       ("s_act", prev2["xf_act"]),
                       ("s_x2sp" if s % 2 == 0 else "s_x2b2",
                        st.get(("x2sp", s - 2)))]
            else:
                p1l = st.get(("p1ln", s))
                if p1l is not None:
                    xw += [("s_dve", p1l["xf_dve"]),
                           ("s_act", p1l["xf_act"])]
            t_xl = dma("sync", xf_s[:], xf_sp[s], waits=xw,
                       sem="s_xla" if s % 2 == 0 else "s_xlb")
            st[("p2bx", s)] = (xf_s, t_xl)

        def p2b_S(s):
            prv = s - 1
            qtil_s = qtil if s % 2 == 0 else qtil2
            rstr_s = rstr if s % 2 == 0 else rstr2
            retrev_s = retrev if s % 2 == 0 else retrevb
            sql = "s_ql" if s % 2 == 0 else "s_ql2"
            srl = "s_rl" if s % 2 == 0 else "s_rl2"
            alias_guards = ([("s_pe", st.get(("p2ret", NSEG - 1))),
                            ("s_pe", st.get(("p2score", NSEG - 1)))]
                           if s < 2 else [])
            t_qtl = dma("sync", qtil_s[:], qt_sp[s],
                        waits=[full("s_qtsp"),
                               ("s_pe", st.get(("p2bS", s - 2)))] +
                              alias_guards,
                        sem=sql)
            t_rld = dma("sync", rstr_s[:], retr_sp[s],
                        waits=[full("s_rsp"),
                               ("s_dve", st.get(("bev", s - 2)))] +
                              alias_guards,
                        sem=srl)
            apcol = (dcol[:, ONE:ONE + 1] if s == NSEG - 1 else
                     dcol[:, APW + (NSEG - 2 - s):APW + (NSEG - 1 - s)])
            tbs = None
            tbev = None
            for j in range(NB):
                ips, ps, gp = rot_ps.next()
                t = None
                for ib in range(NB):
                    t = mm(ps[:],
                           sc_sb[:, 1024 * ib + 128 * j:
                                 1024 * ib + 128 * (j + 1)],
                           qtil_s[:, 512 * ib:512 * (ib + 1)],
                           ib == 0, ib == NB - 1,
                           waits=[("s_dve", st[("scdone", ib)]),
                                  (sql, t_qtl)] +
                                 ([gp] if ib == 0 else []),
                           inc="s_pe" if ib == NB - 1 else None)
                tbs = t
                tev = dve(lambda e, S, ps=ps, j=j, apcol=apcol,
                          rstr_s=rstr_s, retrev_s=retrev_s:
                          e.scalar_tensor_tensor(
                              retrev_s[:, 512 * j:512 * (j + 1)], ps[:],
                              apcol, rstr_s[:, 512 * j:512 * (j + 1)],
                              OP.mult, OP.add),
                          waits=[("s_pe", t), (srl, t_rld),
                                 ("s_act", t_apw), ("s_dve", t_one),
                                 ("s_pe", st.get(("p2wo", s - 2))),
                                 ("s_pe", st.get(("p2ret", NSEG - 1))),
                                 ("s_rsp", st.get(("rsp", NSEG - 1)))])
                rot_ps.done(ips, "s_dve", tev)
                st[("rev", s, j)] = tev
                tbev = tev
            st[("p2bS", s)] = tbs
            st[("bev", s)] = tbev

        def p2b_tail(s):
            prv = s - 1
            retrev_s = retrev if s % 2 == 0 else retrevb
            xf_s, t_xl = st[("p2bx", s)]
            W_XF = ("s_xla" if s % 2 == 0 else "s_xlb", t_xl)
            t_wo_mm = None
            tx2 = None
            for o in range(NB):
                ips, ps, gp = rot_ps.next()
                t = None
                for eb in range(NB):
                    t = mm(ps[:],
                           woT[:, 1024 * eb + 128 * o:1024 * eb + 128 * (o + 1)],
                           retrev_s[:, 512 * eb:512 * (eb + 1)],
                           eb == 0, eb == NB - 1,
                           waits=[("s_dve", st[("rev", s, eb)]),
                                  ("s_act", t_wo)] + ([gp] if eb == 0 else []),
                           inc="s_pe" if eb == NB - 1 else None)
                t_wo_mm = t
                tx2 = dve(lambda e, S, ps=ps, o=o, xf_s=xf_s:
                          e.scalar_tensor_tensor(
                    xf_s[:, 512 * o:512 * (o + 1)], ps[:],
                    gaeff_c[:, o:o + 1], xf_s[:, 512 * o:512 * (o + 1)],
                    OP.mult, OP.add),
                    waits=[("s_pe", t), ("s_dve", t_mod), W_XF])
                rot_ps.done(ips, "s_dve", tx2)
            st[("p2wo", s)] = t_wo_mm
            st[("x2w", s)] = tx2
            st[("x2sp", s)] = dma("gpsimd", x2_sp[s], xf_s[:],
                                  waits=[("s_dve", tx2)],
                                  sem="s_x2sp" if s % 2 == 0 else "s_x2b2")
            lnres2 = plan_ln(xf_s, hseg, cb1_c, sb_c,
                             [("s_dve", tx2)], ("l2", s),
                             h_guard=("s_h2sp", st.get(("h2sp", prv)))
                             if s > 0 else
                             ("s_pe", st.get(("vmm_last", 0))))
            st[("p2ln", s)] = lnres2
            st[("h2sp", s)] = dma("gpsimd", h2_sp[s], hseg[:],
                                  waits=[("s_dve", lnres2["h_last"])],
                                  sem="s_h2sp")

        if stage >= 2:
            p2b_x(0)
            p2b_x(1)
            for p in range(NSEG // 2):
                p2b_S(2 * p)
                p2b_S(2 * p + 1)
                p2b_tail(2 * p)
                if 2 * p + 2 <= NSEG - 1:
                    p2b_x(2 * p + 2)
                p2b_tail(2 * p + 1)
                if 2 * p + 3 <= NSEG - 1:
                    p2b_x(2 * p + 3)

        W_P2 = ([full("s_x2sp"), full("s_x2b2"), full("s_h2sp"),
                 full("s_dbg")] if stage >= 3 else [])
        W_W1G = []
        if stage >= 3:
            for wq in range(4):
                sw = ("s_w1", "s_w1b", "s_w1c", "s_w1d")[wq]
                for db in range(NB):
                    dma("gpsimd",
                        w1sb[:, FW * db + 1024 * wq:FW * db + 1024 * (wq + 1)],
                        w1bf[128 * db:128 * (db + 1),
                             1024 * wq:1024 * (wq + 1)],
                        waits=W_P2 + [W_WPREP1], sem=sw)
                W_W1G.append(full(sw))
        for wb in range(NWB if stage >= 3 else 0):
            dma("gpsimd",
                w2sb[:, 1024 * wb:1024 * (wb + 1)],
                w2bf[128 * wb:128 * (wb + 1), :],
                waits=W_P2 + [W_WPREP2],
                sem="s_w2" if wb % 2 == 0 else "s_w2b")
        W_W2 = full("s_w2")
        W_W2B = full("s_w2b")

        n_out = 0
        for s in range(NSEG if stage >= 3 else 0):
            t_h2l = dma("sync", h2c[:], h2_sp[s],
                        waits=W_P2 + [("s_pe", st.get(("ffnmid", s - 1)))],
                        sem="s_hl")
            t_x2l = dma("sync", x2c[:], x2_sp[s],
                        waits=W_P2 + [("s_dve", st.get(("ffnx3", s - 1))),
                                      ("s_pe", st.get(("p3tr", s - 1)))],
                        sem="s_x2l")
            tmid = None
            for wb in range(NWB):
                ips, ps, gp = rot_ps.next()
                t = None
                for db in range(NB):
                    t = mm(ps[:],
                           w1sb[:, FW * db + 128 * wb:FW * db + 128 * (wb + 1)],
                           h2c[:, 512 * db:512 * (db + 1)],
                           db == 0, db == NB - 1,
                           waits=[("s_hl", t_h2l), W_W1G[wb // 8]] +
                                 ([gp] if db == 0 else []),
                           inc="s_pe" if db == NB - 1 else None)
                tmid = t
                te = act(lambda e, S, ps=ps, wb=wb: e.activation(
                    midc[:, 512 * wb:512 * (wb + 1)], ps[:], AF.Relu,
                    bias=b1col[:, wb:wb + 1]),
                    waits=[("s_pe", t), ("s_pe", st.get(("ffnout", s - 1)))])
                rot_ps.done(ips, "s_act", te)
                st[("ffnme", s, wb)] = te
            st[("ffnmid", s)] = tmid
            tout = None
            tx3 = None
            for o in range(NB):
                ipg, po, gp = rot_ps.next()
                t = mm(po, b2row[0:1, 128 * o:128 * (o + 1)],
                       onesrow512bf[:], True, False,
                       waits=[W_W2, W_W2B, W_WPREP2, gp, W_CONST])
                for wb in range(NWB):
                    t = mm(po,
                           w2sb[:, 1024 * wb + 128 * o:1024 * wb + 128 * (o + 1)],
                           midc[:, 512 * wb:512 * (wb + 1)],
                           False, wb == NWB - 1,
                           waits=[("s_act", st[("ffnme", s, wb)]), W_W2],
                           inc="s_pe" if wb == NWB - 1 else None)
                tout = t
                tx3 = dve(lambda e, S, po=po, o=o: e.scalar_tensor_tensor(
                    x2c[:, 512 * o:512 * (o + 1)], po,
                    gb_c[:, o:o + 1], x2c[:, 512 * o:512 * (o + 1)],
                    OP.mult, OP.add),
                    waits=[("s_pe", t), ("s_x2l", t_x2l), ("s_dve", t_mod)])
                rot_ps.done(ipg, "s_dve", tx3)
                st[("x3", s, o)] = tx3
            st[("ffnout", s)] = tout
            st[("ffnx3", s)] = tx3
            tp3 = None
            for b in range(4):
                io, osl, go = rot_ostage.next()
                tev = None
                for o in range(NB):
                    ipt, pt, gp = rot_pt.next()
                    t = tr(pt[:, 0:128],
                           x2c[:, 512 * o + 128 * b:512 * o + 128 * (b + 1)],
                           ident32[:],
                           waits=[("s_dve", st[("x3", s, o)]), gp] +
                                 ([go] if o == 0 else []))
                    tp3 = t
                    tev = act(lambda e, S, pt=pt, o=o, osl=osl: e.activation(
                        ostage[:, 1024 * osl + 128 * o:
                               1024 * osl + 128 * (o + 1)],
                        pt[:, 0:128], AF.Copy), waits=[("s_pe", t)])
                    rot_pt.done(ipt, "s_act", tev)
                so = "s_oa" if osl == 0 else "s_ob"
                tod = dma("sync",
                          out_dram[512 * s + 128 * b:512 * s + 128 * (b + 1), :],
                          ostage[:, 1024 * osl:1024 * (osl + 1)],
                          waits=[("s_act", tev)], sem=so)
                rot_ostage.done(io, so, tod)
                n_out += 1
            st[("p3tr", s)] = tp3

        P.emit("sync", lambda e, S: e.nop(),
               waits=[full("s_oa"), full("s_ob"), full("s_dbg")])

        with nc.Block() as block:
            def mk(ename):
                def body(eng):
                    for fn, waits, inc, amt in P.seq[ename]:
                        for semn, val in waits:
                            eng.wait_ge(sems[semn], val)
                        inst = fn(eng, sems)
                        if inc is not None:
                            inst.then_inc(sems[inc], amt)
                return body
            for ename in ENGINES:
                if P.seq[ename]:
                    getattr(block, ename)(mk(ename))

    return nc


_CACHE = {}


def _get_nc(debug=(), stage=3):
    key = (tuple((n, tuple(s), d) for n, s, d in debug), stage)
    if key not in _CACHE:
        _CACHE[key] = build_program(debug, stage)
    return _CACHE[key]


def _consts():
    bf = ml_dtypes.bfloat16
    return {
        "ident_f32": np.eye(128, dtype=np.float32),
        "ident_bf16": np.eye(128, dtype=np.float32).astype(bf),
        "ones_row128": np.ones((1, 128), np.float32),
        "ones_row512_bf": np.ones((1, 512), np.float32).astype(bf),
        "ones_col_bf": np.ones((128, 1), np.float32).astype(bf),
        "iota_col": np.arange(128, dtype=np.float32).reshape(128, 1),
        "iota_row128": np.arange(128, dtype=np.float32).reshape(1, 128),
        "iota_row512": np.arange(512, dtype=np.float32).reshape(1, 512),
        "expo_t": np.concatenate(
            [(128 * b + np.arange(128)[:, None]
              - np.arange(512)[None, :] - 1.0).astype(np.float32)
             for b in range(4)], axis=1),
    }


def kernel(L0_attn_x, L0_attn_Wq, L0_attn_Wk, L0_attn_Wv, L0_attn_Wo,
           L0_attn_decay_logit, L0_attn_out_scale,
           L0_ffn_w1, L0_ffn_b1, L0_ffn_w2, L0_ffn_b2, L0_c, L0_wmod,
           _debug=(), _stage=3):
    nc = _get_nc(_debug, _stage)
    x2d = np.ascontiguousarray(np.asarray(L0_attn_x, np.float32)
                               .reshape(T, D))
    common = dict(
        wq=np.ascontiguousarray(np.asarray(L0_attn_Wq, np.float32)),
        wk=np.ascontiguousarray(np.asarray(L0_attn_Wk, np.float32)),
        wv=np.ascontiguousarray(np.asarray(L0_attn_Wv, np.float32)),
        wo=np.ascontiguousarray(np.asarray(L0_attn_Wo, np.float32)),
        w1=np.ascontiguousarray(np.asarray(L0_ffn_w1, np.float32)),
        w2=np.ascontiguousarray(np.asarray(L0_ffn_w2, np.float32)),
        b1col=np.ascontiguousarray(
            np.asarray(L0_ffn_b1, np.float32).reshape(NWB, 128).T),
        b2row=np.ascontiguousarray(
            np.asarray(L0_ffn_b2, np.float32).reshape(1, D)),
        c_cols=np.ascontiguousarray(
            np.asarray(L0_c, np.float32).reshape(NB, 128).T),
        wmod=np.ascontiguousarray(np.asarray(L0_wmod, np.float32)),
        decay_logit=np.asarray(L0_attn_decay_logit,
                               np.float32).reshape(1, 1),
        out_scale=np.asarray(L0_attn_out_scale, np.float32).reshape(1, 1),
        **_consts(),
    )
    in_maps = []
    for c in range(NCORE):
        ecore = np.zeros((1, NCORE), np.float32)
        emask = np.zeros((1, NCORE), np.float32)
        for j in range(NCORE):
            if j > c:
                ecore[0, j] = float(TC * (j - c - 1))
                emask[0, j] = 1.0
        m = dict(common)
        m["x"] = np.ascontiguousarray(x2d[TC * c:TC * (c + 1)])
        m["ecore"] = ecore
        m["emask"] = emask
        in_maps.append(m)
    res = run_bass_kernel_spmd(nc, in_maps, core_ids=list(range(NCORE)))
    out = np.concatenate([res.results[c]["out"] for c in range(NCORE)],
                         axis=0)
    if _debug:
        return out.reshape(1, T, D), list(res.results)
    return out.reshape(1, T, D)
